# version 06
# baseline (speedup 1.0000x reference)
import math
import numpy as np
import ml_dtypes
import concourse.bass as bass
import concourse.mybir as mybir
from concourse.bass_utils import run_bass_kernel_spmd

F32 = mybir.dt.float32
BF16 = mybir.dt.bfloat16
AF = mybir.ActivationFunctionType
ALU = mybir.AluOpType

HID = 2048
NH = 8
DH = 128
NJ = 2
LQ = 4096
BATCH = 4
HALF = 2048
TILE = 512
NTILE = HALF // TILE
CT = 64
NCHUNK = TILE // CT
HPG = 4
INTER = 5632
NIT = INTER // 128
KT = HID // 128
LN128 = math.log(128.0)

ENG2BLK = {"sp": "sync", "pool": "gpsimd", "pe": "tensor", "dve": "vector", "act": "scalar"}



def cp(e, dst, src):
    f = getattr(e, "tensor_copy", None)
    if f is not None:
        return f(dst, src)
    return e.copy(dst, src)

class KB:

    NSEM_PER_Q = 8
    GROUP_QUEUES = ("dsp", "dact", "dpool", "pe")
    QINC = {"dsp": 16, "dact": 16, "dpool": 16, "pe": 1}

    def __init__(self, nc):
        self.nc = nc
        self.ops = {e: [] for e in ENG2BLK}
        self.cnt = {}
        self.known = {e: {} for e in ENG2BLK}
        self.sem_names = ["dve", "act", "pool"]
        for q in self.GROUP_QUEUES:
            self.sem_names += [f"{q}{i}" for i in range(self.NSEM_PER_Q)]
        for s in self.sem_names:
            self.cnt[s] = 0
        self.handles = {}
        self.prunable = {"dve", "act"}
        self.qcur = {q: 0 for q in self.GROUP_QUEUES}
        self.qclosed = {}
        self.last_tok = {}
        self.qrelease = {}
        self.pending_cert = {e: [] for e in ENG2BLK}
        self.dma_sem_set = {sn for q in self.GROUP_QUEUES for sn in
                            [f"{q}{i}" for i in range(self.NSEM_PER_Q)]}

    def dma_sem(self, q):
        for step in range(self.NSEM_PER_Q + 1):
            sn = f"{q}{self.qcur[q]}"
            if not self.qclosed.get(sn):
                return sn, ()
            cert = self.qrelease.get(sn)
            if cert is not None:
                self.qclosed[sn] = False
                del self.qrelease[sn]
                return sn, (cert,)
            self.qcur[q] = (self.qcur[q] + 1) % self.NSEM_PER_Q
        raise AssertionError(f"no reusable dma sem on queue {q}")

    def dma_barrier(self, q):
        toks = []
        for i in range(self.NSEM_PER_Q):
            sn = f"{q}{i}"
            if self.cnt[sn] > 0:
                toks.append((sn, self.cnt[sn]))
                self.qclosed[sn] = True
        return tuple(toks)

    @staticmethod
    def _flatten(waits):
        out = []
        def rec(t):
            if t is None:
                return
            if isinstance(t, tuple):
                if len(t) == 2 and isinstance(t[0], str):
                    out.append(t)
                else:
                    for x in t:
                        rec(x)
        for t in waits:
            rec(t)
        return out

    def op(self, eng, fn, waits=(), inc=None, inc_n=1, tag=""):
        w2 = []
        for t in self._flatten(waits):
            s, v = t
            if s in self.dma_sem_set:
                v = self.cnt[s]
                if v == 0:
                    continue
                if not self.qclosed.get(s):
                    self.qclosed[s] = True
                if s not in self.qrelease:
                    self.pending_cert[eng].append((s, v))
            if s in self.prunable or s == eng:
                if self.known[eng].get(s, 0) >= v:
                    continue
                self.known[eng][s] = max(self.known[eng].get(s, 0), v)
            else:
                if self.known[eng].get(s, 0) >= v:
                    continue
                self.known[eng][s] = max(self.known[eng].get(s, 0), v)
            w2.append((s, v))
        is_dma = inc in self.dma_sem_set if inc is not None else False
        extra = ()
        if eng in ("dve", "act", "pool") and not is_dma:
            lt = self.last_tok.get(eng)
            if lt is not None:
                s2, v2 = lt
                if self.known[eng].get(s2, 0) < v2:
                    self.known[eng][s2] = v2
                    w2.append((s2, v2))
            if inc is None:
                inc = {"dve": "dve", "act": "act", "pool": "pool"}[eng]
        elif eng == "pe":
            inc, extra = self.dma_sem("pe")
            inc_n = 1
            for t in extra:
                s2, v2 = t
                if self.known[eng].get(s2, 0) < v2:
                    self.known[eng][s2] = v2
                    w2.append((s2, v2))
        tok = None
        incs = []
        if inc is not None:
            self.cnt[inc] += inc_n
            incs = [(inc, inc_n)]
            tok = (inc, self.cnt[inc])
        if eng in ("dve", "act", "pool") and not is_dma:
            self.last_tok[eng] = tok
        if tok is not None and self.pending_cert[eng]:
            for (s, v) in self.pending_cert[eng]:
                if self.cnt[s] == v:
                    self.qrelease[s] = tok
            self.pending_cert[eng] = []
        self.ops[eng].append((w2, fn, incs, tag))
        return tok

    def barrier_token(self, eng):
        raise NotImplementedError

    def flush(self, name):
        nc = self.nc
        handles = self.handles
        with nc.Block(name) as blk:
            for ename, blkname in ENG2BLK.items():
                deco = getattr(blk, blkname)
                ops = self.ops[ename]

                def make_body(ops):
                    def body(eng):
                        for (waits, fn, incs, tag) in ops:
                            for (s, v) in waits:
                                eng.wait_ge(handles[s], v)
                            inst = fn(eng)
                            for (s, k) in incs:
                                assert inst is not None, f"op {tag} has inc but returned no instruction"
                                inst.then_inc(handles[s], k)
                    return body

                deco(make_body(ops))
            self.ops = {e: [] for e in ENG2BLK}
        comp = ["dve", "act", "pool"] + [f"pe{i}" for i in range(self.NSEM_PER_Q)]
        for e in ENG2BLK:
            for sname in comp:
                self.known[e][sname] = self.cnt[sname]



def blockify_lhsT(w, mtile=128):
    K, M = w.shape
    nK, nM = K // 128, M // mtile
    return np.ascontiguousarray(
        w.reshape(nK, 128, nM, mtile).transpose(2, 1, 0, 3).reshape(nM, 128, nK * mtile)
    )


def make_masks():
    tt = np.arange(CT)
    strict = (tt[None, :] < tt[:, None]).astype(np.float32)
    incl = (tt[None, :] <= tt[:, None]).astype(np.float32)
    maskM = np.block([[strict, strict], [incl, strict]])
    maskM_neg = np.tile(-maskM, (1, 4)).astype(ml_dtypes.bfloat16)
    maskAT = np.tile(-np.concatenate([incl.T, incl.T], axis=0), (1, 4)).astype(ml_dtypes.bfloat16)
    trilT = np.triu(np.ones((CT, CT), np.float32))
    return maskM_neg, maskAT, trilT


def stage_inputs_for_core(inputs, core):
    b, hh = core // 2, core % 2
    x = np.asarray(inputs["L0_attn_x"])
    d = {}
    d["x_own"] = np.ascontiguousarray(x[b, hh * HALF:(hh + 1) * HALF])
    d["x_replay"] = np.ascontiguousarray(x[b, 0:HALF])
    d["x_halo"] = (np.ascontiguousarray(x[b, HALF - 3:HALF]) if hh == 1
                   else np.zeros((3, HID), np.float32))
    d["flag"] = np.full((128, 1), float(hh), np.float32)

    d["Wq_b"] = blockify_lhsT(np.asarray(inputs["L0_attn_Wq"])).reshape(8 * 128, KT * 128)
    d["Wk_b"] = blockify_lhsT(np.asarray(inputs["L0_attn_Wk"])).reshape(16 * 128, KT * 128)
    d["Wv_b"] = blockify_lhsT(np.asarray(inputs["L0_attn_Wv"])).reshape(16 * 128, KT * 128)
    wg = np.asarray(inputs["L0_attn_Wg"])
    d["Wg_r"] = np.ascontiguousarray(np.concatenate([
        wg[:, hg * 512:(hg + 1) * 512].reshape(KT, 128, 512).transpose(1, 0, 2).reshape(128, KT * 512)
        for hg in range(2)], axis=0))
    wba = np.concatenate([np.asarray(inputs["L0_attn_Wb"]),
                          np.asarray(inputs["L0_attn_Wa"])], axis=1)
    d["Wba_b"] = blockify_lhsT(wba, mtile=24).reshape(128, KT * 24)
    d["Wo_b"] = np.ascontiguousarray(
        np.asarray(inputs["L0_attn_Wo"]).reshape(8, 128, 4, 512).transpose(0, 2, 1, 3)).reshape(8 * 4 * 128, 512)
    d["convq"] = np.ascontiguousarray(np.asarray(inputs["L0_attn_conv_q_w"])[:, 0, :])
    d["convk"] = np.ascontiguousarray(np.asarray(inputs["L0_attn_conv_k_w"])[:, 0, :])
    d["convv"] = np.ascontiguousarray(np.asarray(inputs["L0_attn_conv_v_w"])[:, 0, :])
    d["A_log"] = np.asarray(inputs["L0_attn_A_log"]).reshape(1, 8)
    d["dt_bias"] = np.asarray(inputs["L0_attn_dt_bias"]).reshape(1, 8)
    d["norm_w"] = np.asarray(inputs["L0_attn_norm_w"]).reshape(1, 128)

    gw = np.asarray(inputs["L0_ffn_gate_w"])
    uw = np.asarray(inputs["L0_ffn_up_w"])
    dw = np.asarray(inputs["L0_ffn_down_w"])
    d["gw_b"] = blockify_lhsT(np.ascontiguousarray(gw.T)).reshape(NIT * 128, KT * 128)
    d["uw_b"] = blockify_lhsT(np.ascontiguousarray(uw.T)).reshape(NIT * 128, KT * 128)
    dwT = np.ascontiguousarray(dw.T)
    d["dw_b"] = np.ascontiguousarray(dwT.reshape(NIT, 128, 4, 512).transpose(2, 0, 1, 3)).reshape(4 * NIT * 128, 512)
    d["gs"] = np.ascontiguousarray(np.asarray(inputs["L0_ffn_gate_s"]).reshape(NIT, 128).T)
    d["us"] = np.ascontiguousarray(np.asarray(inputs["L0_ffn_up_s"]).reshape(NIT, 128).T)
    d["ds_row"] = np.asarray(inputs["L0_ffn_down_s"]).reshape(1, HID)

    maskM_neg, maskAT, trilT = make_masks()
    d["maskM"] = maskM_neg
    d["maskAT"] = maskAT
    d["trilT"] = trilT
    d["id128b"] = np.eye(128, dtype=ml_dtypes.bfloat16)
    d["id128f"] = np.eye(128, dtype=np.float32)
    d["id64f"] = np.eye(64, dtype=np.float32)
    d["id4f"] = np.eye(4, dtype=np.float32)
    d["id3b"] = np.eye(3, dtype=ml_dtypes.bfloat16)
    d["ones1f"] = np.ones((1, 128), np.float32)
    d["epsc"] = np.concatenate([
        np.full((128, 1), 1e-5, np.float32),
        np.full((128, 1), 1e-18, np.float32),
        np.full((128, 1), 1.0, np.float32),
        np.full((128, 1), 0.0, np.float32),
    ], axis=1)
    return d



class Prog:
    def __init__(self, nc, debug_names=(), ntile=NTILE, do_ffn=True, do_phases=("Ra", "Fa", "Rb", "Fb")):
        self.nc = nc
        self.kb = KB(nc)
        self.debug_names = set(debug_names)
        self.debug_outs = {}
        self.ntile = ntile
        self.do_ffn = do_ffn
        self.do_phases = do_phases
        self.ctx = []
        self.guard = {}
        self.guardB = {}
        self.t = {}

    def dram_inputs(self):
        nc = self.nc
        T = self.t
        def di(name, shape, dt=F32):
            T[name] = nc.dram_tensor(name, list(shape), dt, kind="ExternalInput")
        di("x_own", (HALF, HID)); di("x_replay", (HALF, HID)); di("x_halo", (3, HID))
        di("flag", (128, 1))
        di("Wq_b", (8 * 128, KT * 128)); di("Wk_b", (16 * 128, KT * 128)); di("Wv_b", (16 * 128, KT * 128))
        di("Wg_r", (2 * 128, KT * 512)); di("Wba_b", (128, KT * 24))
        di("Wo_b", (8 * 4 * 128, 512))
        di("convq", (8 * 128, 4)); di("convk", (16 * 128, 4)); di("convv", (16 * 128, 4))
        di("A_log", (1, 8)); di("dt_bias", (1, 8)); di("norm_w", (1, 128))
        di("gw_b", (NIT * 128, KT * 128)); di("uw_b", (NIT * 128, KT * 128))
        di("dw_b", (4 * NIT * 128, 512))
        di("gs", (128, NIT)); di("us", (128, NIT)); di("ds_row", (1, HID))
        di("maskM", (128, 512), BF16); di("maskAT", (128, 256), BF16)
        di("trilT", (64, 64)); di("id128b", (128, 128), BF16); di("id128f", (128, 128))
        di("id64f", (64, 64)); di("id4f", (4, 4)); di("id3b", (3, 3), BF16)
        di("ones1f", (1, 128)); di("epsc", (128, 4))
        T["out"] = nc.dram_tensor("out", [HALF, HID], F32, kind="ExternalOutput")
        def dk(name):
            return "ExternalOutput" if name in self.debug_names else "Internal"
        T["xnTr_sp"] = nc.dram_tensor("xnTr_sp", [KT * 128, HALF], BF16, kind=dk("xnTr_sp"))
        T["xnTo_sp"] = nc.dram_tensor("xnTo_sp", [KT * 128, HALF], BF16, kind=dk("xnTo_sp"))
        T["o_sp"] = nc.dram_tensor("o_sp", [HALF, 1024], BF16, kind=dk("o_sp"))
        T["gate_sp"] = nc.dram_tensor("gate_sp", [HALF, 1024], BF16, kind=dk("gate_sp"))
        if "S" in self.debug_names:
            T["dbg_S"] = nc.dram_tensor("dbg_S", [128, NH * 2 * 128], BF16, kind="ExternalOutput")

    def dbg(self, name, shape, dt=F32):
        if name in self.debug_names:
            h = self.nc.dram_tensor("dbg_" + name, list(shape), dt, kind="ExternalOutput")
            self.debug_outs[name] = h
            return h
        return None

    def alloc_outer(self, ctx):
        nc = self.nc
        T = self.t
        def sb(name, shape, dt):
            T[name] = ctx.enter_context(nc.sbuf_tensor(name, list(shape), dt))
        sb("c_id128b", (128, 128), BF16); sb("c_id128f", (128, 128), F32)
        sb("c_id64f", (64, 64), F32); sb("c_id4f", (4, 4), F32); sb("c_id3b", (3, 3), BF16)
        sb("c_ones1f", (1, 128), F32); sb("c_maskM", (128, 512), BF16); sb("c_maskAT", (128, 256), BF16)
        sb("c_trilT", (64, 64), F32); sb("c_epsc", (128, 4), F32)
        sb("c_dtb64", (64, 64), F32)
        sb("c_negA64", (64, 64), F32)
        sb("c_normw_bc", (128, 128), F32)
        sb("c_negexpA", (128, 8), F32); sb("c_dtb", (128, 8), F32)
        sb("c_flag", (128, 1), F32)
        sb("c_rows", (1, 768), F32)
        sb("x_sb", (128, HID), F32)
        sb("xh_sb", (3, HID), F32)
        sb("xnh_sb", (3, HID), BF16)
        sb("xnTh_sb", (128, KT * 3), BF16)
        sb("xnT_sb", (128, KT * TILE), BF16)
        sb("stats", (128, 12), F32)
        sb("dump", (128, HID), BF16)
        sb("trow", (1, 256), F32)
        sb("trow2", (1, 128), F32)
        sb("panx", (128, 16), F32)
        for sname in self.kb.sem_names:
            self.kb.handles[sname] = ctx.enter_context(nc.semaphore("sem_" + sname))

    def alloc_attn(self, ctx):
        nc, T = self.nc, self.t
        def sb(name, shape, dt):
            T[name] = ctx.enter_context(nc.sbuf_tensor(name, list(shape), dt))
        def ps(name, shape, dt):
            T[name] = ctx.enter_context(nc.psum_tensor(name, list(shape), dt))
        ps("pT0", (128, 1024), BF16); ps("pT1", (128, 1024), BF16)
        ps("pA", (128, 512), F32); ps("pB", (128, 512), F32)
        ps("pC", (128, 512), F32); ps("pD", (128, 512), F32)
        ps("pE", (128, 512), F32); ps("pF", (128, 512), F32)
        sb("xn_sb", (128, HID), BF16)
        sb("bg_tm", (64, NCHUNK * 24), F32)
        sb("bgtmp", (24, TILE), BF16)
        sb("lnb_tm", (64, NCHUNK * 16), F32)
        sb("gsp_tm", (64, NCHUNK * 8), F32)
        sb("bgt2", (64, NCHUNK * 16), F32)
        sb("bgt3", (64, NCHUNK * 16), F32)
        sb("gcum_sb", (64, 8), F32)
        sb("gcumT_sb", (4, 64), F32)
        sb("gCl_big", (128, 4), F32)
        sb("Lam_big", (128, 8), F32)
        sb("gce", (128, 4), F32)
        sb("pan1", (128, 24), F32)
        sb("pan2", (64, 32), F32)
        sb("s_bf", (128, NH * 2 * 128), BF16)
        sb("srep_bf", (128, 4 * 128), BF16)
        sb("D_sb", (128, 512), F32)
        sb("Eq_sb", (128, 256), BF16)
        sb("nN_a", (128, 512), BF16)
        sb("nN_b", (128, 512), BF16)
        sb("nT_a", (128, 512), BF16)
        sb("nT_b", (128, 512), BF16)
        sb("nLN_a", (128, 512), BF16)
        sb("nLN_b", (128, 512), BF16)
        sb("nLT_a", (128, 512), BF16)
        sb("nLT_b", (128, 512), BF16)
        sb("ucp", (128, 1024), BF16)
        sb("ot1w", (64, 512), BF16)
        sb("ktm_sb", (128, 1024), BF16)
        sb("vtm_sb", (128, 1024), BF16)
        sb("am_sb", (128, 512), BF16)
        sb("kss", (128, 8), F32)
        sb("qss", (64, 8), F32)
        sb("otile", (128, 1024), BF16)
        sb("wq_sb", (128, 4 * KT * 128), BF16)
        sb("wk_sb", (128, 8 * KT * 128), BF16)
        sb("wv_sb", (128, 8 * KT * 128), BF16)
        sb("wg_sb", (128, KT * 512), BF16)
        sb("wba_sb", (128, KT * 24), BF16)
        sb("wstage", (128, 1024), F32)
        sb("convw", (128, 20 * 4), F32)
        sb("pre", (128, 20 * (TILE + 3)), BF16)
        sb("halosave", (128, 20 * 3), BF16)
        sb("cacc", (128, 2 * TILE), BF16)
        sb("gstage", (128, TILE), BF16)

    def alloc_ffn(self, ctx):
        nc, T = self.nc, self.t
        def sb(name, shape, dt):
            T[name] = ctx.enter_context(nc.sbuf_tensor(name, list(shape), dt))
        def ps(name, shape, dt):
            T[name] = ctx.enter_context(nc.psum_tensor(name, list(shape), dt))
        for i in range(8):
            ps(f"pf{i}", (128, 512), F32)
        sb("h_sb", (128, NIT * 1024), BF16)
        sb("big2", (128, 8192), BF16)
        sb("fstage", (128, 2048), F32)
        sb("fbf", (128, 2 * 4096), BF16)
        sb("wostage", (128, 512), F32)
        sb("wo_sb", (128, 8 * 512), BF16)
        sb("ds_bc", (128, HID), BF16)
        sb("gs_sb", (128, NIT), F32)
        sb("us_sb", (128, NIT), F32)
        sb("ogrp", (128, 1024), BF16)
        sb("ggrp", (128, 1024), BF16)
        sb("ong", (128, 1024), F32)
        sb("rmsp", (128, 24), F32)
        sb("hscr", (128, 2 * 512), F32)
        sb("ostage", (128, 2 * 512), F32)
    def pe(self, fn, waits=(), inc=False, tag=""):
        self.kb.op("pe", fn, waits, "pe" if inc else None, tag=tag)
        kb = self.kb
        return tuple((f"pe{i}", kb.cnt[f"pe{i}"]) for i in range(kb.NSEM_PER_Q)
                     if kb.cnt[f"pe{i}"] > 0)

    def dve(self, fn, waits=(), inc=False, tag=""):
        return self.kb.op("dve", fn, waits, "dve" if inc else None, tag=tag)

    def act(self, fn, waits=(), inc=False, tag=""):
        return self.kb.op("act", fn, waits, "act" if inc else None, tag=tag)

    def pool(self, fn, waits=(), inc=False, tag=""):
        return self.kb.op("pool", fn, waits, "pool" if inc else None, tag=tag)

    def ldma(self, dst, src, waits=(), tag=""):
        sn, extra = self.kb.dma_sem("dsp")
        return self.kb.op("sp", lambda e: e.dma_start(dst, src), tuple(waits) + extra, sn, 16, tag=tag)

    def sdma(self, dst, src, waits=(), tag=""):
        sn, extra = self.kb.dma_sem("dact")
        return self.kb.op("act", lambda e: e.dma_start(dst, src), tuple(waits) + extra, sn, 16, tag=tag)

    def pdma(self, dst, src, waits=(), tag=""):
        sn, extra = self.kb.dma_sem("dpool")
        return self.kb.op("pool", lambda e: e.dma_start(dst, src), tuple(waits) + extra, sn, 16, tag=tag)

    def prologue(self):
        T = self.t
        loads = [
            ("c_id128b", T["id128b"]), ("c_id128f", T["id128f"]), ("c_id64f", T["id64f"]),
            ("c_id4f", T["id4f"]), ("c_id3b", T["id3b"]), ("c_ones1f", T["ones1f"]),
            ("c_maskM", T["maskM"]), ("c_maskAT", T["maskAT"]), ("c_trilT", T["trilT"]),
            ("c_epsc", T["epsc"]), ("c_flag", T["flag"]),
        ]
        toks = []
        for name, src in loads:
            toks.append(self.ldma(T[name][:, :], src[:, :], tag="ld_" + name))
        t_alog = self.ldma(T["c_rows"][0:1, 0:8], T["A_log"][:, :], tag="ld_alog")
        t_dtb = self.ldma(T["c_rows"][0:1, 8:16], T["dt_bias"][:, :], tag="ld_dtb")
        t_nw = self.ldma(T["c_rows"][0:1, 16:144], T["norm_w"][:, :], tag="ld_nw")
        t_xh = self.ldma(T["xh_sb"][:, :], T["x_halo"][:, :], tag="ld_xh")
        self.tok_consts = toks[-1]
        def bc(dst_psum, row_ap, n):
            return lambda e: e.matmul(dst_psum, T["c_ones1f"][:, :], row_ap, start=True, stop=True)
        tm = self.pe(bc(T["pA"][:, 0:8], T["c_rows"][0:1, 0:8], 8),
                     waits=(t_alog, toks[5]), inc=True, tag="bc_alog")
        tm2 = self.pe(bc(T["pA"][:, 8:16], T["c_rows"][0:1, 8:16], 8),
                      waits=(t_dtb,), inc=True, tag="bc_dtb")
        tm3 = self.pe(bc(T["pA"][:, 16:144], T["c_rows"][0:1, 16:144], 128),
                      waits=(t_nw,), inc=True, tag="bc_nw")
        t1 = self.act(lambda e: e.activation(T["panx"][:, 0:8], T["pA"][:, 0:8], AF.Exp),
                      waits=(tm,), inc=True, tag="expA")
        t2 = self.dve(lambda e: e.tensor_scalar(T["c_negexpA"][:, :], T["panx"][:, 0:8], -1.0, None, op0=ALU.mult),
                      waits=(t1,), inc=True, tag="negA")
        t3 = self.dve(lambda e: cp(e, T["c_dtb"][:, :], T["pA"][:, 8:16]),
                      waits=(tm2,), inc=True, tag="cp_dtb")
        t4 = self.dve(lambda e: cp(e, T["c_normw_bc"][:, :], T["pA"][:, 16:144]),
                      waits=(tm3,), inc=True, tag="cp_nw")
        self.guard["pA"] = t4
        self.tok_prologue = (t2, t3, t4, self.tok_consts)
        tstat = self.act(lambda e: e.activation(T["xnh_sb"][0:3, :], T["xh_sb"][0:3, :], AF.Copy,
                                                accum_out=T["stats"][0:3, 1:2]),
                         waits=(t_xh,), inc=True, tag="halo_sum")
        tsq = self.act(lambda e: e.activation(T["xnh_sb"][0:3, :], T["xh_sb"][0:3, :], AF.Square,
                                              accum_out=T["stats"][0:3, 3:4]),
                       waits=(t_xh,), inc=True, tag="halo_sq")
        tmn = self.dve(lambda e: e.tensor_scalar(T["stats"][0:3, 4:5], T["stats"][0:3, 1:2], 1.0 / HID, None, op0=ALU.mult),
                       waits=(tstat,), inc=True, tag="halo_m")
        tms = self.dve(lambda e: e.tensor_scalar(T["stats"][0:3, 5:6], T["stats"][0:3, 3:4], 1.0 / HID, None, op0=ALU.mult),
                       waits=(tsq,), inc=True, tag="halo_ms")
        tm2v = self.dve(lambda e: e.tensor_tensor(T["stats"][0:3, 6:7], T["stats"][0:3, 4:5], T["stats"][0:3, 4:5], op=ALU.mult),
                        waits=(tmn,), inc=True, tag="halo_m2")
        tvar = self.dve(lambda e: e.tensor_tensor(T["stats"][0:3, 7:8], T["stats"][0:3, 5:6], T["stats"][0:3, 6:7], op=ALU.subtract),
                        waits=(tms, tm2v), inc=True, tag="halo_var")
        tln = self.act(lambda e: e.activation(T["stats"][0:3, 8:9], T["stats"][0:3, 7:8], AF.Ln, bias=T["c_epsc"][0:3, 0:1]),
                       waits=(tvar, self.tok_consts), inc=True, tag="halo_ln")
        trs = self.act(lambda e: e.activation(T["stats"][0:3, 9:10], T["stats"][0:3, 8:9], AF.Exp, scale=-0.5),
                       waits=(tln,), inc=True, tag="halo_rs")
        tmrs = self.dve(lambda e: e.tensor_tensor(T["stats"][0:3, 10:11], T["stats"][0:3, 4:5], T["stats"][0:3, 9:10], op=ALU.mult),
                        waits=(trs,), inc=True, tag="halo_mrs")
        txn = self.dve(lambda e: e.tensor_scalar(T["xnh_sb"][0:3, :], T["xh_sb"][0:3, :],
                                                 T["stats"][0:3, 9:10], T["stats"][0:3, 10:11],
                                                 op0=ALU.mult, op1=ALU.subtract),
                       waits=(tmrs,), inc=True, tag="halo_xn")
        gT = None
        for kt in range(KT):
            tt = self.pe(lambda e, kt=kt: e.transpose(T["pT0"][0:128, kt * 4:kt * 4 + 3],
                                                      T["xnh_sb"][0:3, kt * 128:(kt + 1) * 128],
                                                      T["c_id3b"][:, :]),
                         waits=(txn, self.guard.get("pT0"), self.tok_consts), inc=(kt == KT - 1), tag=f"halo_tp{kt}")
            if kt == KT - 1:
                gT = tt
        tcp = None
        for kt in range(KT):
            tcp = self.dve(lambda e, kt=kt: cp(e, T["xnTh_sb"][:, kt * 3:(kt + 1) * 3],
                                                          T["pT0"][:, kt * 4:kt * 4 + 3]),
                           waits=(gT,), inc=(kt == KT - 1), tag=f"halo_evac{kt}")
        self.guard["pT0"] = tcp
        self.tok_xnTh = tcp

    def prologue2(self):
        T = self.t
        t2, t3, _, _ = self.tok_prologue
        last = None
        for c in range(NCHUNK):
            last = self.dve(lambda e, c=c: cp(e, T["c_dtb64"][0:64, c * 8:(c + 1) * 8], T["c_dtb"][0:64, :]),
                            waits=(t3,), inc=(c == NCHUNK - 1), tag=f"dtb64_{c}")
        self.tok_dtb64 = last
        last = None
        for c in range(NCHUNK):
            last = self.dve(lambda e, c=c: cp(e, T["c_negA64"][0:64, c * 8:(c + 1) * 8], T["c_negexpA"][0:64, :]),
                            waits=(t2,), inc=(c == NCHUNK - 1), tag=f"negA64_{c}")
        self.tok_negA64 = last

    def load_attn_weights(self, hg, want_qg, only_qg=False):
        T = self.t
        jobs = []
        if not only_qg:
            for i in range(HPG):
                for j in range(NJ):
                    mt = j * NH + hg * HPG + i
                    loc = j * HPG + i
                    jobs.append((T["Wk_b"][mt * 128:(mt + 1) * 128, :], "wk_sb", loc * KT * 128, KT * 128))
                    jobs.append((T["Wv_b"][mt * 128:(mt + 1) * 128, :], "wv_sb", loc * KT * 128, KT * 128))
        if want_qg:
            for i in range(HPG):
                mt = hg * HPG + i
                jobs.append((T["Wq_b"][mt * 128:(mt + 1) * 128, :], "wq_sb", i * KT * 128, KT * 128))
            for c in range(4):
                jobs.append((T["Wg_r"][hg * 128:(hg + 1) * 128, c * 2048:(c + 1) * 2048], "wg_sb", c * 2048, 2048))
        if not only_qg:
            jobs.append((T["Wba_b"][:, :], "wba_sb", 0, KT * 24))
        last = None
        wsb = self.guard.get("wsball")
        for n, (src, dstn, c0, w) in enumerate(jobs):
            for half in range(0, w, 512):
                hw = min(512, w - half)
                flip = getattr(self, "_wst_flip", 0)
                self._wst_flip = 1 - flip
                stg = T["wstage"][:, flip * 512:flip * 512 + hw]
                tl = self.ldma(stg, src[:, half:half + hw],
                               waits=(self.guard.get(f"wstage{flip}"),),
                               tag=f"wld{hg}_{only_qg}_{n}_{half}")
                tc = self.pool(lambda e, stg=stg, dstn=dstn, c0=c0, half=half, hw=hw: cp(e, 
                    T[dstn][:, c0 + half:c0 + half + hw], stg), waits=(tl, wsb), inc=True,
                    tag=f"wcast{hg}_{only_qg}_{n}_{half}")
                self.guard[f"wstage{flip}"] = tc
                last = tc
        convjobs = []
        for i in range(HPG):
            if not only_qg:
                for j in range(NJ):
                    g = j * NH + hg * HPG + i
                    loc = j * HPG + i
                    convjobs.append((T["convk"][g * 128:(g + 1) * 128, :], 4 + loc))
                    convjobs.append((T["convv"][g * 128:(g + 1) * 128, :], 12 + loc))
            if want_qg:
                gq = hg * HPG + i
                convjobs.append((T["convq"][gq * 128:(gq + 1) * 128, :], 0 + i))
        tconv = None
        for src, slot in convjobs:
            tconv = self.ldma(T["convw"][:, slot * 4:(slot + 1) * 4], src, waits=(wsb,),
                              tag=f"cw{hg}_{slot}")
        return last, tconv

    def bank_wait(self, bank):
        return tuple(self.guardB.get(bank, ()))

    def bank_use(self, bank, tok):
        self.guardB.setdefault(bank, []).append(tok)

    def bank_clear(self, bank):
        self.guardB[bank] = []

    def ln_tile(self, phase, ti):
        T = self.t
        xsrc = T["x_replay"] if phase[0] == "R" else T["x_own"]
        spill = T["xnTr_sp"] if phase[0] == "R" else T["xnTo_sp"]
        if phase in ("Rb", "Fb"):
            tok = None
            for kt in range(KT):
                tok = self.ldma(T["xnT_sb"][:, kt * TILE:(kt + 1) * TILE],
                                spill[kt * 128:(kt + 1) * 128, ti * TILE:(ti + 1) * TILE],
                                waits=tuple(self.guard.get("xnT") or ()) + (self._spill_gates or ()),
                                tag=f"xnTld{ti}_{kt}")
            self.tok_xnT = (tok,)
            return (tok,)
        st = T["stats"]
        lastev = None
        prev = getattr(self, "_ln_prev", {})
        for g in range(4):
            par = 0
            rows = ti * TILE + g * 128
            xg = T["x_sb"][:, 0:HID]
            tx = self.ldma(xg, xsrc[rows:rows + 128, :],
                           waits=(self.guard.get(f"x{par}"),), tag=f"xld{ti}_{g}")
            xng0 = T["xn_sb"][:, 0:HID]
            t1 = self.act(lambda e, xg=xg, xng0=xng0: e.activation(xng0, xg, AF.Copy, accum_out=st[:, 1:2]),
                          waits=(tx, prev.get("t3"), self.guard.get(f"xn{par}")),
                          inc=True, tag=f"lnsum{ti}_{g}")
            t2 = self.act(lambda e, xg=xg, xng0=xng0: e.activation(xng0, xg, AF.Square, accum_out=st[:, 3:4]),
                          waits=(tx, prev.get("t4")), inc=True, tag=f"lnsq{ti}_{g}")
            t3 = self.dve(lambda e: e.tensor_scalar(st[:, 4:5], st[:, 1:2], 1.0 / HID, None, op0=ALU.mult),
                          waits=(t1,), inc=True, tag=f"lnm{ti}_{g}")
            t4 = self.dve(lambda e: e.tensor_scalar(st[:, 5:6], st[:, 3:4], 1.0 / HID, None, op0=ALU.mult),
                          waits=(t2,), inc=True, tag=f"lnms{ti}_{g}")
            t5 = self.dve(lambda e: e.tensor_tensor(st[:, 6:7], st[:, 4:5], st[:, 4:5], op=ALU.mult),
                          waits=(t3,), inc=True, tag=f"lnm2{ti}_{g}")
            t6 = self.dve(lambda e: e.tensor_tensor(st[:, 7:8], st[:, 5:6], st[:, 6:7], op=ALU.subtract),
                          waits=(t4, t5), inc=True, tag=f"lnvar{ti}_{g}")
            t7 = self.act(lambda e: e.activation(st[:, 8:9], st[:, 7:8], AF.Ln, bias=T["c_epsc"][:, 0:1]),
                          waits=(t6, prev.get("t10")), inc=True, tag=f"lnln{ti}_{g}")
            t8 = self.act(lambda e: e.activation(st[:, 9:10], st[:, 8:9], AF.Exp, scale=-0.5),
                          waits=(t7,), inc=True, tag=f"lnrs{ti}_{g}")
            t9 = self.dve(lambda e: e.tensor_tensor(st[:, 10:11], st[:, 4:5], st[:, 9:10], op=ALU.mult),
                          waits=(t8,), inc=True, tag=f"lnmrs{ti}_{g}")
            xng = T["xn_sb"][:, 0:HID]
            t10 = self.dve(lambda e, xg=xg, xng=xng: e.tensor_scalar(
                xng, xg, st[:, 9:10], st[:, 10:11], op0=ALU.mult, op1=ALU.subtract),
                waits=(t9, self.guard.get(f"xn{par}")), inc=True, tag=f"lnxn{ti}_{g}")
            prev = {"t3": t3, "t4": t4, "t10": t10}
            self.guard[f"x{par}"] = t10
            lastev = {}
            for kg in range(4):
                ptname = "pT0" if kg % 2 == 0 else "pT1"
                pt = T[ptname]
                ttp = None
                for i in range(4):
                    kt = kg * 4 + i
                    ttp = self.pe(lambda e, kt=kt, xng=xng, pt=pt, i=i: e.transpose(
                        pt[:, i * 128:(i + 1) * 128], xng[:, kt * 128:(kt + 1) * 128], T["c_id128b"][:, :]),
                        waits=(t10, self.tok_consts) + self.bank_wait(ptname), inc=(i == 3), tag=f"xtp{ti}_{g}_{kt}")
                self.bank_clear(ptname)
                eng = self.dve if kg % 2 == 0 else self.act
                ev = None
                for i in range(4):
                    kt = kg * 4 + i
                    ev = eng(lambda e, kt=kt, i=i, g=g, pt=pt: cp(e, 
                        T["xnT_sb"][:, kt * TILE + g * 128:kt * TILE + (g + 1) * 128],
                        pt[:, i * 128:(i + 1) * 128]),
                        waits=(ttp,) + tuple(self.guard.get("xnT") or ()), inc=(i == 3), tag=f"xev{ti}_{g}_{kt}")
                self.bank_use(ptname, ev)
                lastev[kg % 2] = ev
            self.guard[f"xn{par}"] = lastev[1]
        self._ln_prev = prev
        evtoks = (lastev[0], lastev[1])
        tok = None
        for kt in range(KT):
            tok = self.sdma(spill[kt * 128:(kt + 1) * 128, ti * TILE:(ti + 1) * TILE],
                            T["xnT_sb"][:, kt * TILE:(kt + 1) * TILE],
                            waits=evtoks, tag=f"xnTsp{ti}_{kt}")
        self.tok_xnT = evtoks
        self.tok_xnT_spill = tok
        return evtoks

    def halo_proj(self, hg, wtok):
        T = self.t
        jobs = [("wq_sb", i, i) for i in range(HPG)] + \
               [("wk_sb", l, 4 + l) for l in range(8)] + \
               [("wv_sb", l, 12 + l) for l in range(8)]
        self.tok_halo_pre = {}
        for n, (wname, strip, cht) in enumerate(jobs):
            bank = "pA" if n % 2 == 0 else "pB"
            pb = T[bank]
            tm = None
            for kt in range(KT):
                tm = self.pe(lambda e, wname=wname, strip=strip, kt=kt, pb=pb: e.matmul(
                    pb[:, 0:3],
                    T[wname][:, strip * KT * 128 + kt * 128:strip * KT * 128 + (kt + 1) * 128],
                    T["xnTh_sb"][:, kt * 3:(kt + 1) * 3],
                    start=(kt == 0), stop=(kt == KT - 1)),
                    waits=(wtok, self.tok_xnTh) + self.bank_wait(bank),
                    inc=(kt == KT - 1), tag=f"hproj{cht}_{kt}")
            self.bank_clear(bank)
            tv = self.dve(lambda e, cht=cht, pb=pb: cp(e, 
                T["pre"][:, cht * (TILE + 3):cht * (TILE + 3) + 3], pb[:, 0:3]),
                waits=(tm,), inc=True, tag=f"hpre{cht}")
            self.bank_use(bank, tv)
            self.tok_halo_pre[cht] = tv

    def proj_tile(self, phase, hg, ti, full):
        T = self.t
        txnT = self.tok_xnT
        PRE = TILE + 3
        jobs = [("wba_sb", 0, None)]
        for l in range(8):
            jobs.append(("wk_sb", l, 4 + l))
        for l in range(8):
            jobs.append(("wv_sb", l, 12 + l))
        if full:
            for i in range(HPG):
                jobs.append(("wq_sb", i, 0 + i))
        tok_evac = {}
        for n, (wname, strip, cht) in enumerate(jobs):
            bank = "pA" if n % 2 == 0 else "pB"
            pb = T[bank]
            ncols = 24 if cht is None else 128
            width = TILE
            tm = None
            nout = 24 if cht is None else 128
            for kt in range(KT):
                tm = self.pe(lambda e, wname=wname, strip=strip, kt=kt, pb=pb, ncols=ncols, nout=nout: e.matmul(
                    pb[0:nout, 0:width],
                    T[wname][:, strip * KT * ncols + kt * ncols:strip * KT * ncols + (kt + 1) * ncols],
                    T["xnT_sb"][:, kt * TILE:(kt + 1) * TILE],
                    start=(kt == 0), stop=(kt == KT - 1)),
                    waits=tuple(txnT) + (self.tok_wattn,) + self.bank_wait(bank),
                    inc=(kt == KT - 1), tag=f"proj{ti}_{n}_{kt}")
            self.bank_clear(bank)
            if cht is None:
                tcp = self.dve(lambda e, pb=pb: cp(e, T["bgtmp"][0:24, :], pb[0:24, 0:TILE]),
                               waits=(tm, self.guard.get("bgtmp")), inc=True, tag=f"bgev{ti}")
                self.bank_use(bank, tcp)
                self.emit_bg_panels(ti, tcp)
            else:
                if ti == 0:
                    if phase[0] == "R":
                        troll = self.dve(lambda e, cht=cht: e.memset(
                            T["pre"][:, cht * PRE:cht * PRE + 3], 0.0),
                            waits=(self.tok_conv_prev.get(cht),), inc=True, tag=f"roll0{cht}")
                    else:
                        troll = self.tok_halo_pre[cht]
                else:
                    troll = self.dve(lambda e, cht=cht: cp(e, 
                        T["pre"][:, cht * PRE:cht * PRE + 3],
                        T["halosave"][:, cht * 3:(cht + 1) * 3]),
                        waits=(self.tok_conv_prev.get(cht), self.guard.get(f"hs{cht}")),
                        inc=True, tag=f"roll{ti}_{cht}")
                eng = self.dve if n % 2 == 0 else self.act
                tev = eng(lambda e, cht=cht, pb=pb: cp(e, 
                    T["pre"][:, cht * PRE + 3:cht * PRE + PRE], pb[:, 0:TILE]),
                    waits=(tm, troll, self.guard.get("postall")), inc=True, tag=f"prev{ti}_{cht}")
                self.bank_use(bank, tev)
                tok_evac[cht] = (tev, troll)
        tok_post = {}
        conv_last = {}
        for cht in sorted(tok_evac):
            tev, troll = tok_evac[cht]
            eng = self.dve
            a_idx = (0, TILE)
            accA = T["cacc"][:, a_idx[0]:a_idx[0] + TILE]
            accB = T["cacc"][:, a_idx[1]:a_idx[1] + TILE]
            w = T["convw"]
            p0 = cht * PRE
            tc = eng(lambda e, p0=p0, cht=cht, accA=accA: e.tensor_scalar(
                accA, T["pre"][:, p0 + 0:p0 + 0 + TILE], w[:, cht * 4 + 0:cht * 4 + 1], None, op0=ALU.mult),
                waits=(tev, self.tok_wconv, self.guard.get(f"cacc{a_idx[0]}")), inc=True, tag=f"cv{ti}_{cht}_0")
            cur, oth = accB, accA
            for d in (1, 2, 3):
                tc = eng(lambda e, p0=p0, cht=cht, d=d, cur=cur, oth=oth: e.scalar_tensor_tensor(
                    cur, T["pre"][:, p0 + d:p0 + d + TILE], w[:, cht * 4 + d:cht * 4 + d + 1], oth,
                    op0=ALU.mult, op1=ALU.add),
                    waits=(tc, self.guard.get(f"cacc{a_idx[1] if d % 2 == 1 else a_idx[0]}"),),
                    inc=True, tag=f"cv{ti}_{cht}_{d}")
                cur, oth = oth, cur
            ths = self.dve(lambda e, cht=cht, p0=p0: cp(e, 
                T["halosave"][:, cht * 3:(cht + 1) * 3],
                T["pre"][:, p0 + TILE:p0 + TILE + 3]),
                waits=(tev, troll), inc=True, tag=f"hsave{ti}_{cht}")
            self.guard[f"hs{cht}"] = ths
            tpost = self.act(lambda e, cht=cht, p0=p0, oth=oth: e.activation(
                T["pre"][:, p0 + 3:p0 + PRE], oth, AF.Silu),
                waits=(tc, ths), inc=True, tag=f"silu{ti}_{cht}")
            self.guard[f"cacc{a_idx[0]}"] = tpost
            self.guard[f"cacc{a_idx[1]}"] = tpost
            conv_last[cht] = tc
            tok_post[cht] = tpost
        self.tok_conv_prev = conv_last
        self.tok_post = tok_post
        if full:
            for g in range(4):
                bank = "pA" if g % 2 == 0 else "pB"
                pb = T[bank]
                tm = None
                for kt in range(KT):
                    tm = self.pe(lambda e, g=g, kt=kt, pb=pb: e.matmul(
                        pb[:, 0:TILE],
                        T["xnT_sb"][:, kt * TILE + g * 128:kt * TILE + (g + 1) * 128],
                        T["wg_sb"][:, kt * 512:(kt + 1) * 512],
                        start=(kt == 0), stop=(kt == KT - 1)),
                        waits=tuple(txnT) + (self.tok_wattn,) + self.bank_wait(bank),
                        inc=(kt == KT - 1), tag=f"gate{ti}_{g}_{kt}")
                self.bank_clear(bank)
                tsl = self.act(lambda e, pb=pb: e.activation(
                    T["gstage"][:, 0:TILE], pb[:, 0:TILE], AF.Silu),
                    waits=(tm, self.guard.get("gst0")), inc=True, tag=f"gsilu{ti}_{g}")
                self.bank_use(bank, tsl)
                rows = ti * TILE + g * 128
                tdma = self.sdma(T["gate_sp"][rows:rows + 128, hg * 512:(hg + 1) * 512],
                                 T["gstage"][:, 0:TILE],
                                 waits=(tsl,), tag=f"gsp{ti}_{g}")
                self.guard["gst0"] = tdma
        self.guard["xnT"] = (tm,) if tm is not None else self.guard.get("xnT")
        return tok_post

    def emit_bg_panels(self, ti, tcp):
        T = self.t
        last = None
        for c in range(NCHUNK):
            tt = self.pe(lambda e, c=c: e.transpose(
                T["pT0"][0:64, 512:536], T["bgtmp"][0:24, c * 64:(c + 1) * 64], T["c_id128b"][0:24, 0:24]),
                waits=(tcp,) + self.bank_wait("pT0"), inc=True, tag=f"bgtp{ti}_{c}")
            self.bank_clear("pT0")
            tev = self.dve(lambda e, c=c: cp(e, 
                T["bg_tm"][0:64, c * 24:(c + 1) * 24], T["pT0"][0:64, 512:536]),
                waits=(tt, self.guard.get("bg_tm")), inc=True, tag=f"bgtm{ti}_{c}")
            self.bank_use("pT0", tev)
            last = tev
        self.guard["bgtmp"] = last
        bg3 = T["bg_tm"][0:64, :].rearrange("p (c w) -> p c w", c=NCHUNK)
        t1 = self.act(lambda e: e.activation(
            T["bgt2"][0:64, :].rearrange("p (c w) -> p c w", c=NCHUNK), bg3[:, :, 0:16], AF.Tanh, scale=0.5),
            waits=(last, self.guard.get("bgt2")), inc=True, tag=f"btanh{ti}")
        t2 = self.dve(lambda e: e.tensor_scalar(
            T["bgt3"][0:64, :], T["bgt2"][0:64, :], -1.0, None, op0=ALU.max),
            waits=(t1, self.guard.get("bgt3")), inc=True, tag=f"bmax{ti}")
        t3 = self.act(lambda e: e.activation(
            T["lnb_tm"][0:64, :], T["bgt3"][0:64, :], AF.Ln, bias=T["c_epsc"][0:64, 2:3]),
            waits=(t2, self.guard.get("lnb_tm")), inc=True, tag=f"blnb{ti}")
        self.tok_lnb = t3
        g1 = self.dve(lambda e: e.tensor_tensor(
            T["gsp_tm"][0:64, :].rearrange("p (c w) -> p c w", c=NCHUNK), bg3[:, :, 16:24],
            T["c_dtb64"][0:64, :].rearrange("p (c w) -> p c w", c=NCHUNK), op=ALU.add),
            waits=(last, self.tok_dtb64, self.guard.get("gsp_tm")), inc=True, tag=f"gy{ti}")
        g2a = self.dve(lambda e: e.tensor_scalar(
            T["bgt3"][0:64, 64:128], T["gsp_tm"][0:64, :], -1.0, None, op0=ALU.mult),
            waits=(g1, t3), inc=True, tag=f"gneg{ti}")
        g2 = self.dve(lambda e: e.tensor_tensor(
            T["bgt2"][0:64, 0:64], T["gsp_tm"][0:64, :], T["bgt3"][0:64, 64:128], op=ALU.max),
            waits=(g2a,), inc=True, tag=f"gabs{ti}")
        g3 = self.act(lambda e: e.activation(T["bgt2"][0:64, 64:128], T["bgt2"][0:64, 0:64], AF.Exp, scale=-1.0),
                      waits=(g2,), inc=True, tag=f"gexp{ti}")
        g4 = self.act(lambda e: e.activation(T["bgt3"][0:64, 0:64], T["bgt2"][0:64, 64:128], AF.Ln,
                                             bias=T["c_epsc"][0:64, 2:3]),
                      waits=(g3, t2), inc=True, tag=f"gln{ti}")
        g5 = self.dve(lambda e: e.tensor_scalar(T["bgt3"][0:64, 64:128], T["gsp_tm"][0:64, :], 0.0, None,
                                                op0=ALU.max),
                      waits=(g1, t2), inc=True, tag=f"gmax{ti}")
        g6 = self.dve(lambda e: e.tensor_tensor(T["bgt2"][0:64, 0:64], T["bgt3"][0:64, 0:64],
                                                T["bgt3"][0:64, 64:128], op=ALU.add),
                      waits=(g4, g5), inc=True, tag=f"gsp{ti}")
        g7 = self.dve(lambda e: e.tensor_tensor(T["gsp_tm"][0:64, :], T["bgt2"][0:64, 0:64],
                                                T["c_negA64"][0:64, :], op=ALU.mult),
                      waits=(g6, self.tok_negA64), inc=True, tag=f"gfin{ti}")
        self.tok_gsp = g7
        self.guard["bg_tm"] = g7
        self.guard["bgt2"] = g7
        self.guard["bgt3"] = g6

    def chunk_front(self, phase, hg, ti, c, full):
        T = self.t
        PRE = TILE + 3
        c64 = c * CT
        par = c % 2
        kj = [[T["pre"][:, (4 + j * 4 + hl) * PRE + 3 + c64:(4 + j * 4 + hl) * PRE + 3 + c64 + CT]
               for j in range(2)] for hl in range(HPG)]
        qh = [T["pre"][:, hl * PRE + 3 + c64:hl * PRE + 3 + c64 + CT] for hl in range(HPG)]
        tk = [[self.tok_post[4 + j * 4 + hl] for j in range(2)] for hl in range(HPG)]
        tv = [[self.tok_post[12 + j * 4 + hl] for j in range(2)] for hl in range(HPG)]
        tq = [self.tok_post[hl] if full else None for hl in range(HPG)]
        vj = [[T["pre"][:, (12 + j * 4 + hl) * PRE + 3 + c64:(12 + j * 4 + hl) * PRE + 3 + c64 + CT]
               for j in range(2)] for hl in range(HPG)]
        p1 = T["pan1"]
        p2 = T["pan2"]
        n0N = "nN_a" if par == 0 else "nN_b"
        n0T = "nT_a" if par == 0 else "nT_b"

        tg1 = self.pe(lambda e: e.matmul(T["pA"][0:64, 0:4], T["c_trilT"][:, :],
                                         T["gsp_tm"][0:64, c * 8 + hg * 4:c * 8 + hg * 4 + 4],
                                         start=True, stop=True),
                      waits=(self.tok_gsp, self.tok_consts) + self.bank_wait("pA"), inc=True, tag=f"gcum{ti}_{c}")
        self.bank_clear("pA")
        tg2 = self.dve(lambda e: cp(e, T["gcum_sb"][0:64, 0:4], T["pA"][0:64, 0:4]),
                       waits=(tg1, self.guard.get("gcum")), inc=True, tag=f"gcumev{ti}_{c}")
        self.bank_use("pA", tg2)
        tg3 = self.pe(lambda e: e.transpose(T["pA"][0:4, 4:68], T["gcum_sb"][0:64, 0:4], T["c_id64f"][:, :]),
                      waits=(tg2,) + self.bank_wait("pA"), inc=True, tag=f"gcumT{ti}_{c}")
        self.bank_clear("pA")
        tg4 = self.dve(lambda e: cp(e, T["gcumT_sb"][0:4, :], T["pA"][0:4, 4:68]),
                       waits=(tg3,), inc=True, tag=f"gcumTev{ti}_{c}")
        self.bank_use("pA", tg4)
        tg5 = self.pe(lambda e: e.transpose(T["pA"][0:1, 68:72], T["gcumT_sb"][0:4, 63:64], T["c_id4f"][:, :]),
                      waits=(tg4,) + self.bank_wait("pA"), inc=True, tag=f"gcl1{ti}_{c}")
        self.bank_clear("pA")
        tg6 = self.act(lambda e: e.copy(T["trow"][0:1, 0:4], T["pA"][0:1, 68:72]),
                       waits=(tg5, self.guard.get("trow4")), inc=True, tag=f"gcl2{ti}_{c}")
        self.bank_use("pA", tg6)
        tg7 = self.pe(lambda e: e.matmul(T["pA"][:, 72:76], T["c_ones1f"][:, :], T["trow"][0:1, 0:4],
                                         start=True, stop=True),
                      waits=(tg6,) + self.bank_wait("pA"), inc=True, tag=f"gcl3{ti}_{c}")
        self.bank_clear("pA")
        tg8 = self.dve(lambda e, par=par: cp(e, T["gCl_big"][:, 0:4], T["pA"][:, 72:76]),
                       waits=(tg7, self.guard.get("gClbig")), inc=True, tag=f"gcl4{ti}_{c}")
        self.bank_use("pA", tg8)
        tg9 = self.act(lambda e, par=par: e.activation(T["Lam_big"][:, par * 4:par * 4 + 4],
                                                       T["gCl_big"][:, 0:4], AF.Exp),
                       waits=(tg8, self.guard.get(f"Lambig{par}")), inc=True, tag=f"lam{ti}_{c}")
        self._t_lam = tg9
        tgA = self.dve(lambda e: cp(e, T["gce"][0:64, 0:4], T["gcum_sb"][0:64, 0:4]),
                       waits=(tg2, self.guard.get("gce")), inc=True, tag=f"gce0{ti}_{c}")
        tgB = self.dve(lambda e: cp(e, T["gce"][64:128, 0:4], T["gcum_sb"][0:64, 0:4]),
                       waits=(tg2,), inc=True, tag=f"gce1{ti}_{c}")
        tok_lnbe = None
        for hl in range(HPG):
            c0 = c * 16 + hg * 4 + hl
            c1 = c * 16 + 8 + hg * 4 + hl
            self.dve(lambda e, hl=hl, c0=c0: cp(e, p1[0:64, 16 + hl:17 + hl], T["lnb_tm"][0:64, c0:c0 + 1]),
                     waits=(self.tok_lnb, self.guard.get("pan1")), inc=False, tag=f"lnbe0{ti}_{c}_{hl}")
            tok_lnbe = self.dve(lambda e, hl=hl, c1=c1: cp(e, p1[64:128, 16 + hl:17 + hl],
                                                           T["lnb_tm"][0:64, c1:c1 + 1]),
                                waits=(self.tok_lnb,), inc=(hl == HPG - 1), tag=f"lnbe1{ti}_{c}_{hl}")

        w_t1 = self.bank_wait("pT1")
        t_tp = None
        for hl in range(HPG):
            for j in range(2):
                t_tp = self.pe(lambda e, hl=hl, j=j: e.transpose(
                    T["pT1"][j * 64:(j + 1) * 64, hl * 128:(hl + 1) * 128], kj[hl][j], T["c_id128b"][:, :]),
                    waits=(tk[hl][j],) + w_t1, inc=False, tag=f"tk{c}_{hl}_{j}")
        if full:
            for hl in range(HPG):
                t_tp = self.pe(lambda e, hl=hl: e.transpose(
                    T["pT1"][0:64, 512 + hl * 128:512 + (hl + 1) * 128], qh[hl], T["c_id128b"][:, :]),
                    waits=(tq[hl],), inc=False, tag=f"tq{c}_{hl}")
        self.bank_clear("pT1")
        w_t0 = self.bank_wait("pT0")
        t_tv = None
        for hl in range(HPG):
            for j in range(2):
                t_tv = self.pe(lambda e, hl=hl, j=j: e.transpose(
                    T["pT0"][j * 64:(j + 1) * 64, hl * 128:(hl + 1) * 128], vj[hl][j], T["c_id128b"][:, :]),
                    waits=(tv[hl][j],) + w_t0, inc=(hl == HPG - 1 and j == 1), tag=f"tv{c}_{hl}_{j}")
        self.bank_clear("pT0")
        t_tp = self.pe(lambda e: e.nop(), waits=(), inc=True, tag=f"tpfin{c}")

        ksq = T["dump"][:, 1024:1536]
        qsq = T["dump"][0:64, 1536:2048]
        t_ksq = self.act(lambda e, ksq=ksq: e.activation(ksq, T["pT1"][:, 0:512], AF.Square),
                         waits=(t_tp, self.guard.get("ksq")), inc=True, tag=f"ksq{c}")
        t_kss = self.dve(lambda e, ksq=ksq: e.tensor_reduce(T["kss"][:, 0:4],
                                                            ksq.rearrange("p (h f) -> p h f", h=4),
                                                            axis=mybir.AxisListType.X, op=ALU.add),
                         waits=(t_ksq, self.guard.get("kss")), inc=True, tag=f"kss{c}")
        self.guard["ksq"] = t_kss
        if full:
            t_qsq = self.act(lambda e, qsq=qsq: e.activation(qsq, T["pT1"][0:64, 512:1024], AF.Square),
                             waits=(t_tp, self.guard.get("qsq")), inc=True, tag=f"qsq{c}")
            t_qss = self.dve(lambda e, qsq=qsq: e.tensor_reduce(T["qss"][0:64, 0:4],
                                                                qsq.rearrange("p (h f) -> p h f", h=4),
                                                                axis=mybir.AxisListType.X, op=ALU.add),
                             waits=(t_qsq, self.guard.get("qss")), inc=True, tag=f"qss{c}")
            self.guard["qsq"] = t_qss
            self.bank_use("pT1", t_qsq)
        t_vev = self.dve(lambda e, par=par: cp(e, T["vtm_sb"][:, par * 512:par * 512 + 512],
                                               T["pT0"][:, 0:512]),
                         waits=(t_tv, self.guard.get(f"vtm{par}")), inc=True, tag=f"vev{c}")
        self.bank_use("pT0", t_vev)
        self.bank_use("pT1", t_ksq)
        t_lnk = self.act(lambda e: e.activation(p1[:, 8:12], T["kss"][:, 0:4], AF.Ln,
                                                bias=T["c_epsc"][:, 1:2]),
                         waits=(t_kss, self.guard.get("p1_8")), inc=True, tag=f"lnk{c}")
        self.guard["kss"] = t_lnk
        t_lnrn = self.dve(lambda e: e.tensor_scalar(p1[:, 20:24], p1[:, 8:12], -0.5, None, op0=ALU.mult),
                          waits=(t_lnk,), inc=True, tag=f"lnrn{c}")
        t_a = self.dve(lambda e: e.tensor_tensor(p1[:, 4:8], T["gce"][:, 0:4], p1[:, 20:24], op=ALU.add),
                       waits=(t_lnrn, tgA, tgB, self.guard.get("pan1")), inc=True, tag=f"a{c}")
        t_nb1 = self.dve(lambda e: e.tensor_tensor(p1[:, 8:12], p1[:, 20:24], p1[:, 16:20], op=ALU.add),
                         waits=(t_lnrn, tok_lnbe), inc=True, tag=f"nb1{c}")
        t_nb = self.dve(lambda e: e.tensor_tensor(p1[:, 12:16], p1[:, 8:12], T["gce"][:, 0:4], op=ALU.subtract),
                        waits=(t_nb1, self.guard.get("p1_12")), inc=True, tag=f"nb{c}")
        t_kpe = self.dve(lambda e: e.tensor_tensor(p1[:, 0:4], p1[:, 12:16], T["gCl_big"][:, 0:4], op=ALU.add),
                         waits=(t_nb, tg8, self.guard.get("pan1")), inc=True, tag=f"kpe{c}")
        t_exp = self.act(lambda e, par=par: e.activation(T["panx"][:, par * 8:par * 8 + 8], p1[:, 0:8], AF.Exp),
                         waits=(t_kpe, t_a, self.guard.get(f"panx{par}")), inc=True, tag=f"exp{c}")
        self._t_exp = t_exp
        if full:
            t_lnq = self.act(lambda e: e.activation(p2[0:64, 4:8], T["qss"][0:64, 0:4], AF.Ln,
                                                    bias=T["c_epsc"][0:64, 1:2]),
                             waits=(t_qss, self.guard.get("p2_8")), inc=True, tag=f"lnq{c}")
            t_aq1 = self.dve(lambda e: e.tensor_scalar(p2[0:64, 8:12], p2[0:64, 4:8], -0.5, -0.5 * LN128,
                                                       op0=ALU.mult, op1=ALU.add),
                             waits=(t_lnq,), inc=True, tag=f"aq1{c}")
            t_aq = self.dve(lambda e: e.tensor_tensor(p2[0:64, 0:4], p2[0:64, 8:12],
                                                      T["gcum_sb"][0:64, 0:4], op=ALU.add),
                            waits=(t_aq1, tg2, self.guard.get("p2_04")), inc=True, tag=f"aq{c}")
            t_lq = self.act(lambda e, par=par: e.activation(p2[0:64, par * 16 + 12:par * 16 + 16],
                                                            p2[0:64, 0:4], AF.Exp),
                            waits=(t_aq, self.guard.get(f"lamq{par}")), inc=True, tag=f"lamq{c}")
            self._t_lq = t_lq
            self.guard["p2_8"] = t_aq
            self.guard["p2_04"] = t_lq
        t_k2 = None
        for hl in range(HPG):
            t_k2 = self.dve(lambda e, hl=hl, par=par: e.tensor_scalar(
                T["ktm_sb"][:, par * 512 + hl * 128:par * 512 + (hl + 1) * 128],
                T["pT1"][:, hl * 128:(hl + 1) * 128],
                T["panx"][:, par * 8 + hl:par * 8 + hl + 1], -1.0, op0=ALU.mult, op1=ALU.mult),
                waits=(t_exp, t_tp, self.guard.get(f"ktm{par}")), inc=(hl == HPG - 1), tag=f"k2{c}_{hl}")
        self.bank_use("pT1", t_k2)
        self._t_k2 = t_k2

        w_e = self.bank_wait("pE")
        t_g = None
        for hl in range(HPG):
            for a in range(2):
                for b in range(2):
                    t_g = self.pe(lambda e, hl=hl, a=a, b=b: e.matmul(
                        T["pE"][a * 64:(a + 1) * 64, hl * 128 + b * 64:hl * 128 + (b + 1) * 64],
                        kj[hl][a], kj[hl][b], start=True, stop=True),
                        waits=(tk[hl][0], tk[hl][1]) + w_e,
                        inc=(hl == HPG - 1 and a == 1 and b == 1), tag=f"g{c}_{hl}_{a}{b}")
        self.bank_clear("pE")
        w_f = self.bank_wait("pF")
        t_bt = None
        for hl in range(HPG):
            t_bt = self.pe(lambda e, hl=hl: e.transpose(
                T["pA"][0:1, hl * 128:(hl + 1) * 128], p1[:, 12 + hl:13 + hl], T["c_id128f"][:, :]),
                waits=(t_nb,) + self.bank_wait("pA"), inc=(hl == HPG - 1), tag=f"bt{c}_{hl}")
        self.bank_clear("pA")
        t_br = self.act(lambda e: e.copy(T["c_rows"][0:1, 256:768], T["pA"][0:1, 0:512]),
                        waits=(t_bt, self.guard.get("negbT")), inc=True, tag=f"br{c}")
        self.bank_use("pA", t_br)
        t_bc = None
        for hl in range(HPG):
            t_bc = self.pe(lambda e, hl=hl: e.matmul(
                T["pF"][:, hl * 128:(hl + 1) * 128], T["c_ones1f"][:, :],
                T["c_rows"][0:1, 256 + hl * 128:256 + (hl + 1) * 128], start=True, stop=True),
                waits=(t_br,) + w_f, inc=(hl == HPG - 1), tag=f"bc{c}_{hl}")
        self.bank_clear("pF")
        self.guard["negbT"] = t_bc
        self.guard["p1_12"] = t_bt

        t_D = None
        for hl in range(HPG):
            t_D = self.dve(lambda e, hl=hl: e.tensor_scalar(
                T["D_sb"][:, hl * 128:(hl + 1) * 128], T["pF"][:, hl * 128:(hl + 1) * 128],
                p1[:, 4 + hl:5 + hl], 60.0, op0=ALU.add, op1=ALU.min),
                waits=(t_bc, t_a, self.guard.get("Dsb")), inc=(hl == HPG - 1), tag=f"D{c}_{hl}")
        self.bank_use("pF", t_D)
        t_E = self.act(lambda e: e.activation(T["dump"][:, 1024:1536], T["D_sb"][:, 0:512], AF.Exp),
                       waits=(t_D, t_kss), inc=True, tag=f"E{c}")
        self.guard["Dsb"] = t_E
        t_Em = self.dve(lambda e: e.tensor_tensor(T["dump"][:, 0:512], T["dump"][:, 1024:1536],
                                                  T["c_maskM"][:, :], op=ALU.mult),
                        waits=(t_E, self.guard.get("dumpA")), inc=True, tag=f"Em{c}")
        t_N0 = self.dve(lambda e, n0N=n0N: e.tensor_tensor(T[n0N][:, 0:512], T["dump"][:, 0:512],
                                                           T["pE"][:, 0:512], op=ALU.mult),
                        waits=(t_Em, t_g, self.guard.get(f"n0N{par}")), inc=True, tag=f"N0{c}")
        self.bank_use("pE", t_N0)
        self.guard["ksq"] = t_Em
        self.guard["dumpA"] = t_N0
        t_nt0 = None
        for hl in range(HPG):
            t_nt0 = self.pe(lambda e, hl=hl, n0N=n0N: e.transpose(
                T["pT0"][:, 512 + hl * 128:512 + (hl + 1) * 128],
                T[n0N][:, hl * 128:(hl + 1) * 128], T["c_id128b"][:, :]),
                waits=(t_N0,) + self.bank_wait("pT0"), inc=(hl == HPG - 1), tag=f"nt0{c}_{hl}")
        self.bank_clear("pT0")
        t_nt0e = self.act(lambda e, n0T=n0T: e.copy(T[n0T][:, 0:512], T["pT0"][:, 512:1024]),
                          waits=(t_nt0, self.guard.get(f"n0T{par}")), inc=True, tag=f"nt0e{c}")
        self.bank_use("pT0", t_nt0e)
        self._t_lvl0 = (t_nt0e, t_N0)

        if full:
            w_f2 = self.bank_wait("pF")
            t_ar = None
            for hl in range(HPG):
                for j in range(2):
                    t_ar = self.pe(lambda e, hl=hl, j=j: e.matmul(
                        T["pF"][j * 64:(j + 1) * 64, hl * 64:(hl + 1) * 64], kj[hl][j], qh[hl],
                        start=True, stop=True),
                        waits=(tk[hl][j], tq[hl]) + w_f2,
                        inc=(hl == HPG - 1 and j == 1), tag=f"ar{c}_{hl}_{j}")
            t_at = None
            for hl in range(HPG):
                t_at = self.pe(lambda e, hl=hl: e.transpose(
                    T["pA"][0:1, hl * 64:(hl + 1) * 64], p2[0:64, hl:hl + 1], T["c_id64f"][:, :]),
                    waits=(t_aq,) + self.bank_wait("pA"), inc=(hl == HPG - 1), tag=f"at{c}_{hl}")
            self.bank_clear("pA")
            t_arr = self.act(lambda e: e.copy(T["trow"][0:1, 0:256], T["pA"][0:1, 0:256]),
                             waits=(t_at, tg7, self.guard.get("trow4")), inc=True, tag=f"arr{c}")
            self.bank_use("pA", t_arr)
            t_ab = None
            for hl in range(HPG):
                t_ab = self.pe(lambda e, hl=hl: e.matmul(
                    T["pF"][:, 256 + hl * 64:256 + (hl + 1) * 64], T["c_ones1f"][:, :],
                    T["trow"][0:1, hl * 64:(hl + 1) * 64], start=True, stop=True),
                    waits=(t_arr,), inc=(hl == HPG - 1), tag=f"ab{c}_{hl}")
            self.bank_clear("pF")
            self.guard["trow4"] = t_ab
            self.guard["p2_04"] = (t_lq, t_at)
            t_D3 = None
            for hl in range(HPG):
                t_D3 = self.dve(lambda e, hl=hl: e.tensor_scalar(
                    T["D_sb"][:, hl * 64:(hl + 1) * 64], T["pF"][:, 256 + hl * 64:256 + (hl + 1) * 64],
                    p1[:, 12 + hl:13 + hl], 60.0, op0=ALU.add, op1=ALU.min),
                    waits=(t_ab, t_nb, t_E), inc=(hl == HPG - 1), tag=f"D3{c}_{hl}")
            self.bank_use("pF", t_D3)
            t_Eq = self.act(lambda e: e.activation(T["Eq_sb"][:, 0:256], T["D_sb"][:, 0:256], AF.Exp),
                            waits=(t_D3, self.guard.get("Eq")), inc=True, tag=f"Eq{c}")
            self.guard["Dsb"] = t_Eq
            t_Eqm = self.dve(lambda e: e.tensor_tensor(T["dump"][:, 512:768], T["Eq_sb"][:, 0:256],
                                                       T["c_maskAT"][:, :], op=ALU.mult),
                             waits=(t_Eq, self.guard.get("dumpB")), inc=True, tag=f"Eqm{c}")
            t_am = self.dve(lambda e, par=par: e.tensor_tensor(
                T["am_sb"][:, par * 256:par * 256 + 256], T["dump"][:, 512:768],
                T["pF"][:, 0:256], op=ALU.mult),
                waits=(t_Eqm, t_ar, self.guard.get(f"am{par}")), inc=True, tag=f"am{c}")
            self.bank_use("pF", t_am)
            self.guard["Eq"] = t_am
            self.guard["dumpB"] = t_am
            self._t_am = t_am
        self.guard["p1_8"] = t_nb
        self.guard["pan1"] = t_exp
        self.guard["gcum"] = (tg3, tgA, tgB) if not full else (tg3, tgA, tgB, t_aq)
        self.guard["gce"] = (t_a, t_kpe)
        self.guard["gClbig"] = (tg9, t_kpe)
        self._ck[c] = dict(t_exp=t_exp, t_lvl0=self._t_lvl0, t_vev=t_vev, t_k2=t_k2,
                           t_lam=self._t_lam,
                           t_am=(self._t_am if full else None),
                           t_lq=(self._t_lq if full else None),
                           t_lnbe=tok_lnbe, tg1=tg1)

    def chunk_back(self, phase, hg, ti, c, full):
        T = self.t
        PRE = TILE + 3
        c64 = c * CT
        par = c % 2
        kj = [[T["pre"][:, (4 + j * 4 + hl) * PRE + 3 + c64:(4 + j * 4 + hl) * PRE + 3 + c64 + CT]
               for j in range(2)] for hl in range(HPG)]
        qh = [T["pre"][:, hl * PRE + 3 + c64:hl * PRE + 3 + c64 + CT] for hl in range(HPG)]
        tk = [[self.tok_post[4 + j * 4 + hl] for j in range(2)] for hl in range(HPG)]
        tq = [self.tok_post[hl] if full else None for hl in range(HPG)]
        heads = [hg * HPG + hl for hl in range(HPG)]
        ping = self.s_ping[heads[0]]
        S0 = [T["s_bf"][:, (h * 2 + ping) * 128:(h * 2 + ping) * 128 + 128] for h in heads]
        Snx = [T["s_bf"][:, (h * 2 + 1 - ping) * 128:(h * 2 + 1 - ping) * 128 + 128] for h in heads]
        tok_s_in = tuple(self.tok_s.get(h) for h in heads)
        p2 = T["pan2"]
        n0N = "nN_a" if par == 0 else "nN_b"
        n0T = "nT_a" if par == 0 else "nT_b"
        t_exp = self._ck[c]["t_exp"]
        t_lvl0 = self._ck[c]["t_lvl0"]
        t_vev = self._ck[c]["t_vev"]
        t_k2 = self._ck[c]["t_k2"]
        t_lam = self._ck[c]["t_lam"]

        w_x = self.bank_wait("pD")
        t_x = None
        for hl in range(HPG):
            for j in range(2):
                t_x = self.pe(lambda e, hl=hl, j=j: e.matmul(
                    T["pD"][j * 64:(j + 1) * 64, hl * 128:(hl + 1) * 128], kj[hl][j], S0[hl],
                    start=True, stop=True),
                    waits=(tk[hl][j], tok_s_in[hl]) + w_x,
                    inc=(hl == HPG - 1 and j == 1), tag=f"x{c}_{hl}_{j}")
        self.bank_clear("pD")
        t_u0 = None
        for hl in range(HPG):
            t_u0 = self.dve(lambda e, hl=hl, par=par: e.scalar_tensor_tensor(
                T["ucp"][:, hl * 128:(hl + 1) * 128], T["pD"][:, hl * 128:(hl + 1) * 128],
                T["panx"][:, par * 8 + 4 + hl:par * 8 + 5 + hl],
                T["vtm_sb"][:, par * 512 + hl * 128:par * 512 + (hl + 1) * 128],
                op0=ALU.mult, op1=ALU.subtract),
                waits=(t_x, t_exp, t_vev, self.guard.get("ucp0")), inc=(hl == HPG - 1), tag=f"u0{c}_{hl}")
        self.bank_use("pD", t_u0)

        nLN = {0: "nLN_a", 1: "nLN_b"}
        nLT = {0: "nLT_a", 1: "nLT_b"}
        def lvlbuf(k):
            if k == 0:
                return n0N, n0T
            return nLN[(k + 1) % 2], nLT[(k + 1) % 2]
        t_lvl = {0: t_lvl0}
        t_ucur = t_u0
        t_ap_hist = {}
        for k in range(7):
            kpar = k % 2
            knpar = (k + 1) % 2
            ucur0 = kpar * 512
            unext0 = knpar * 512
            bN, bT = lvlbuf(k)
            w_u = self.bank_wait("pC")
            t_mm = None
            for hl in range(HPG):
                t_mm = self.pe(lambda e, hl=hl, bT=bT, ucur0=ucur0: e.matmul(
                    T["pC"][:, hl * 128:(hl + 1) * 128],
                    T[bT][:, hl * 128:(hl + 1) * 128],
                    T["ucp"][:, ucur0 + hl * 128:ucur0 + (hl + 1) * 128],
                    start=True, stop=True),
                    waits=(t_ucur,) + t_lvl[k] + w_u, inc=(hl == HPG - 1), tag=f"ap{c}_{hl}_{k}")
            self.bank_clear("pC")
            t_ap_hist[k] = t_mm
            t_ucur = self.dve(lambda e, ucur0=ucur0, unext0=unext0: e.tensor_tensor(
                T["ucp"][:, unext0:unext0 + 512], T["ucp"][:, ucur0:ucur0 + 512],
                T["pC"][:, 0:512], op=ALU.add),
                waits=(t_mm, self.guard.get(f"ucp{knpar}")), inc=True, tag=f"uacc{c}_{k}")
            self.bank_use("pC", t_ucur)
            self.guard[f"ucp{kpar}"] = t_ucur
            if k < 6:
                dN, dT = lvlbuf(k + 1)
                w_d = self.bank_wait("pD")
                w_b = self.bank_wait("pB")
                dst_guard = (t_mm, self.guard.get("nL") if k > 0 else None)
                t_s1 = t_s2 = None
                if k < 5:
                    for hl in range(HPG):
                        t_s1 = self.pe(lambda e, hl=hl, bN=bN, bT=bT: e.matmul(
                            T["pD"][:, hl * 128:(hl + 1) * 128],
                            T[bT][:, hl * 128:(hl + 1) * 128],
                            T[bN][:, hl * 128:(hl + 1) * 128], start=True, stop=True),
                            waits=t_lvl[k] + w_d, inc=(hl == HPG - 1), tag=f"sqN{c}_{hl}_{k}")
                    for hl in range(HPG):
                        t_s2 = self.pe(lambda e, hl=hl, bN=bN, bT=bT: e.matmul(
                            T["pB"][:, hl * 128:(hl + 1) * 128],
                            T[bN][:, hl * 128:(hl + 1) * 128],
                            T[bT][:, hl * 128:(hl + 1) * 128], start=True, stop=True),
                            waits=t_lvl[k] + w_b, inc=(hl == HPG - 1), tag=f"sqT{c}_{hl}_{k}")
                    self.bank_clear("pD")
                    self.bank_clear("pB")
                    t_evN = self.dve(lambda e, dN=dN: cp(e, T[dN][:, 0:512], T["pD"][:, 0:512]),
                                     waits=(t_s1,) + dst_guard, inc=True, tag=f"sqeN{c}_{k}")
                    self.bank_use("pD", t_evN)
                    t_evT = self.act(lambda e, dT=dT: e.copy(T[dT][:, 0:512], T["pB"][:, 0:512]),
                                     waits=(t_s2,) + dst_guard, inc=True, tag=f"sqeT{c}_{k}")
                    self.bank_use("pB", t_evT)
                    t_lvl[k + 1] = (t_evN, t_evT)
                else:
                    for hl in range(HPG):
                        t_s1 = self.pe(lambda e, hl=hl, bN=bN, bT=bT: e.matmul(
                            T["pD"][:, hl * 128:(hl + 1) * 128],
                            T[bN][:, hl * 128:(hl + 1) * 128],
                            T[bT][:, hl * 128:(hl + 1) * 128], start=True, stop=True),
                            waits=t_lvl[k] + w_d, inc=(hl == HPG - 1), tag=f"sqT6{c}_{hl}")
                    self.bank_clear("pD")
                    t_ev6 = self.dve(lambda e, dT=dT: cp(e, T[dT][:, 0:512], T["pD"][:, 0:512]),
                                     waits=(t_s1,) + dst_guard, inc=True, tag=f"sqe6{c}")
                    self.bank_use("pD", t_ev6)
                    t_lvl[6] = (t_ev6,)
                self.guard["nL"] = t_mm
        t_uf = t_ucur

        if full:
            t_am = self._ck[c]["t_am"]
            t_lq = self._ck[c]["t_lq"]
            w_d2 = self.bank_wait("pD")
            t_o1 = None
            for hl in range(HPG):
                t_o1 = self.pe(lambda e, hl=hl: e.matmul(
                    T["pD"][0:64, hl * 128:(hl + 1) * 128], qh[hl], S0[hl], start=True, stop=True),
                    waits=(tq[hl], tok_s_in[hl]) + w_d2, inc=(hl == HPG - 1), tag=f"o1{c}_{hl}")
            self.bank_clear("pD")
            w_c2 = self.bank_wait("pC")
            t_o2 = None
            for hl in range(HPG):
                t_o2 = self.pe(lambda e, hl=hl, par=par: e.matmul(
                    T["pC"][0:64, hl * 128:(hl + 1) * 128],
                    T["am_sb"][:, par * 256 + hl * 64:par * 256 + (hl + 1) * 64],
                    T["ucp"][:, 512 + hl * 128:512 + (hl + 1) * 128], start=True, stop=True),
                    waits=(t_am, t_uf) + w_c2, inc=(hl == HPG - 1), tag=f"o2{c}_{hl}")
            self.bank_clear("pC")
            t_ot1 = None
            for hl in range(HPG):
                t_ot1 = self.dve(lambda e, hl=hl, par=par: e.tensor_scalar(
                    T["ot1w"][0:64, hl * 128:(hl + 1) * 128], T["pD"][0:64, hl * 128:(hl + 1) * 128],
                    p2[0:64, par * 16 + 12 + hl:par * 16 + 13 + hl], None, op0=ALU.mult),
                    waits=(t_o1, t_lq, self.guard.get("ot1w")), inc=(hl == HPG - 1), tag=f"ot1{c}_{hl}")
            self.bank_use("pD", t_ot1)
            orow = (c % 2) * 64
            ocol = ((c // 2) % 2) * 512
            t_o = self.dve(lambda e, orow=orow, ocol=ocol: e.tensor_tensor(
                T["otile"][orow:orow + 64, ocol:ocol + 512],
                T["ot1w"][0:64, 0:512], T["pC"][0:64, 0:512], op=ALU.add),
                waits=(t_ot1, t_o2, self.guard.get(f"otile{(c // 2) % 2}")), inc=True, tag=f"o{c}")
            self.bank_use("pC", t_o)
            self.guard["ot1w"] = t_o
            self.tok_otile[c] = t_o
            self.guard[f"lamq{par}"] = t_o

        w_s = self.bank_wait("pB")
        t_sn = None
        for hl in range(HPG):
            t_sn = self.pe(lambda e, hl=hl, par=par: e.matmul(
                T["pB"][:, hl * 128:(hl + 1) * 128],
                T["ktm_sb"][:, par * 512 + hl * 128:par * 512 + (hl + 1) * 128],
                T["ucp"][:, 512 + hl * 128:512 + (hl + 1) * 128], start=True, stop=True),
                waits=(t_k2, t_uf) + w_s, inc=(hl == HPG - 1), tag=f"sn{c}_{hl}")
        self.bank_clear("pB")
        t_s = None
        for hl in range(HPG):
            t_s = self.dve(lambda e, hl=hl, par=par: e.scalar_tensor_tensor(
                Snx[hl], S0[hl], T["Lam_big"][:, par * 4 + hl:par * 4 + hl + 1],
                T["pB"][:, hl * 128:(hl + 1) * 128], op0=ALU.mult, op1=ALU.add),
                waits=(t_sn, t_lam), inc=True, tag=f"s{c}_{hl}")
            self.tok_s[heads[hl]] = t_s
            self.s_ping[heads[hl]] = 1 - self.s_ping[heads[hl]]
        self.bank_use("pB", t_s)
        self.guard[f"n0N{par}"] = t_lvl[1]
        self.guard[f"n0T{par}"] = (t_ap_hist[0],) + t_lvl[1]
        self.guard["ucp0"] = t_uf
        self.guard["ucp1"] = (t_sn, t_o2) if full else t_sn
        self.guard[f"vtm{par}"] = t_u0
        self.guard[f"ktm{par}"] = t_sn
        self.guard[f"panx{par}"] = (t_u0, t_k2)
        self.guard[f"Lambig{par}"] = t_s
        if full:
            self.guard[f"am{par}"] = t_o2

    def chunks_tile(self, phase, hg, ti, full):
        T = self.t
        self.tok_otile = {}
        self._ck = {}
        NC = getattr(self, "nchunk_dbg", None) or NCHUNK
        self.chunk_front(phase, hg, ti, 0, full)
        for c in range(NC):
            if c + 1 < NC:
                self.chunk_front(phase, hg, ti, c + 1, full)
            self.chunk_back(phase, hg, ti, c, full)
            if full and c % 2 == 1:
                m = c // 2
                rows = ti * TILE + m * 128
                w = (self.tok_otile.get(c - 1), self.tok_otile.get(c))
                tsp = self.sdma(T["o_sp"][rows:rows + 128, hg * 512:(hg + 1) * 512],
                                T["otile"][:, (m % 2) * 512:(m % 2) * 512 + 512],
                                waits=w, tag=f"osp{ti}_{m}")
                self.guard[f"otile{m % 2}"] = tsp
        hmax = hg * HPG + HPG - 1
        self.guard["postall"] = (self.tok_s[hmax], self.tok_otile.get(NC - 1))

    def attn_phase(self, phase):
        T = self.t
        hg = 0 if phase.endswith("a") else 1
        full = phase[0] == "F"
        self._spill_gates = self.kb.dma_barrier("dact")
        if full and getattr(self, "_pref_hg", None) == hg:
            wtok, ctok = self._pref_wtok, self._pref_ctok
        elif (not full) and getattr(self, "_kv_hg", None) == hg:
            wtok, ctok = self._kv_wtok, self._kv_ctok
        else:
            wtok, ctok = self.load_attn_weights(hg, want_qg=full)
        self.tok_wattn = wtok
        self.tok_wconv = ctok
        if not full and getattr(self, "_pref_hg", None) != hg:
            pw, pc = self.load_attn_weights(hg, want_qg=True, only_qg=True)
            self._pref_wtok = (wtok, pw)
            self._pref_ctok = (ctok, pc)
            self._pref_hg = hg
        if phase[0] == "R":
            for hl in range(HPG):
                h = hg * HPG + hl
                self.s_ping[h] = 0
                self.tok_s[h] = self.dve(lambda e, h=h: e.memset(
                    T["s_bf"][:, h * 256:h * 256 + 128], 0.0),
                    waits=(self.tok_s.get(h),), inc=True, tag=f"sinit{phase}_{h}")
        else:
            for hl in range(HPG):
                h = hg * HPG + hl
                self.s_ping[h] = 0
                if self.tok_srep.get(h) is None:
                    self.tok_srep[h] = self.dve(lambda e, h=h: e.memset(
                        T["srep_bf"][:, (h % 4) * 128:(h % 4) * 128 + 128], 0.0), inc=True,
                        tag=f"srep0{phase}_{h}")
                self.tok_s[h] = self.dve(lambda e, h=h: e.tensor_scalar(
                    T["s_bf"][:, h * 256:h * 256 + 128],
                    T["srep_bf"][:, (h % 4) * 128:(h % 4) * 128 + 128],
                    T["c_flag"][:, 0:1], None, op0=ALU.mult),
                    waits=(self.tok_s.get(h), self.tok_srep.get(h), self.tok_consts), inc=True,
                    tag=f"sinitF{phase}_{h}")
        if full:
            self.halo_proj(hg, wtok)
        for ti in range(self.ntile):
            self.ln_tile(phase, ti)
            self.proj_tile(phase, hg, ti, full)
            if phase == "Fa" and ti == self.ntile - 1 and "Rb" in self.do_phases:
                kb = self.kb
                snap = tuple((f"pe{i}", kb.cnt[f"pe{i}"]) for i in range(kb.NSEM_PER_Q)
                             if kb.cnt[f"pe{i}"] > 0)
                fence = self.dve(lambda e: e.memset(T["stats"][0:1, 11:12], 0.0),
                                 waits=(snap,), inc=True, tag="whfence")
                self.guard["wsball"] = fence
                nw, nc2 = self.load_attn_weights(1, want_qg=False)
                pw, pc2 = self.load_attn_weights(1, want_qg=True, only_qg=True)
                self._kv_wtok, self._kv_ctok, self._kv_hg = nw, nc2, 1
                self._pref_wtok = (nw, pw)
                self._pref_ctok = (nc2, pc2)
                self._pref_hg = 1
            self.chunks_tile(phase, hg, ti, full)
        if phase[0] == "R":
            for hl in range(HPG):
                h = hg * HPG + hl
                ping = self.s_ping[h]
                self.tok_srep[h] = self.dve(lambda e, h=h, ping=ping: cp(e, 
                    T["srep_bf"][:, (h % 4) * 128:(h % 4) * 128 + 128],
                    T["s_bf"][:, (h * 2 + ping) * 128:(h * 2 + ping) * 128 + 128]),
                    waits=(self.tok_s[h], self.tok_s.get(h - 4) if h >= 4 else None),
                    inc=True, tag=f"srep{phase}_{h}")

    def ffn(self):
        T = self.t
        self._spill_gates = self.kb.dma_barrier("dact")
        tprev = None
        for piece in range(16):
            tl = self.ldma(T["c_rows"][0:1, 128:256], T["ds_row"][:, piece * 128:(piece + 1) * 128],
                           waits=(tprev,), tag=f"dsl{piece}")
            tb = self.pe(lambda e: e.matmul(T["pf0"][:, 0:128], T["c_ones1f"][:, :],
                                            T["c_rows"][0:1, 128:256], start=True, stop=True),
                         waits=(tl,) + self.bank_wait("pf0"), inc=True, tag=f"dsb{piece}")
            self.bank_clear("pf0")
            tprev = self.dve(lambda e, piece=piece: cp(e, 
                T["ds_bc"][:, piece * 128:(piece + 1) * 128], T["pf0"][:, 0:128]),
                waits=(tb,), inc=True, tag=f"dsc{piece}")
            self.bank_use("pf0", tprev)
        tok_dsbc = tprev
        t_gs = self.ldma(T["gs_sb"][:, :], T["gs"][:, :], tag="gs")
        t_us = self.ldma(T["us_sb"][:, :], T["us"][:, :], tag="us")

        for sp in range(2):
            self.ffn_subpass(sp, tok_dsbc, t_gs, t_us)
        fin = self.kb.dma_barrier("dact") + self.kb.dma_barrier("dsp") + self.kb.dma_barrier("dpool")
        self.kb.op("act", lambda e: e.nop(), waits=fin, tag="final_wait")

    def ffn_subpass(self, sp, tok_dsbc, t_gs, t_us):
        T = self.t
        base = sp * 1024
        tx0 = None
        tx1 = None
        for kt in range(KT):
            tx0 = self.ldma(T["xnT_sb"][:, kt * TILE:(kt + 1) * TILE],
                            T["xnTo_sp"][kt * 128:(kt + 1) * 128, base:base + 512],
                            waits=tuple(self.guard.get("xnT") or ()) + (self._spill_gates or ()),
                            tag=f"fxnT0_{sp}_{kt}")
            tx1 = self.ldma(T["big2"][:, kt * TILE:(kt + 1) * TILE],
                            T["xnTo_sp"][kt * 128:(kt + 1) * 128, base + 512:base + 1024],
                            waits=(self.guard.get("big2"), *(self._spill_gates or (None,))), tag=f"fxnT1_{sp}_{kt}")
        txnT = (tx0, tx1)
        t_h = {}
        for it in range(NIT):
            par = it % 2
            tg = self.ldma(T["fstage"][:, 0:2048], T["gw_b"][it * 128:(it + 1) * 128, :],
                           waits=(self.guard.get("fstage"), self.guard.get("dstage0"),
                                  self.guard.get("dstage1")), tag=f"fg{sp}_{it}")
            tcg = self.pool(lambda e, par=par: cp(e, 
                T["fbf"][:, par * 4096:par * 4096 + 2048], T["fstage"][:, 0:2048]),
                waits=(tg, self.guard.get(f"fbf{par}"), self.guard.get("dbf0"),
                       self.guard.get("dbf1")), inc=True, tag=f"fcg{sp}_{it}")
            tu = self.ldma(T["fstage"][:, 0:2048], T["uw_b"][it * 128:(it + 1) * 128, :],
                           waits=(tcg,), tag=f"fu{sp}_{it}")
            tcu = self.pool(lambda e, par=par: cp(e, 
                T["fbf"][:, par * 4096 + 2048:par * 4096 + 4096], T["fstage"][:, 0:2048]),
                waits=(tu, self.guard.get(f"fbf{par}")), inc=True, tag=f"fcu{sp}_{it}")
            self.guard["fstage"] = tcu
            for half in range(2):
                xsrc = T["xnT_sb"] if half == 0 else T["big2"]
                bg = f"pf{half * 2}"
                bu = f"pf{half * 2 + 1}"
                tmg = None
                tmu = None
                for kt in range(KT):
                    tmg = self.pe(lambda e, kt=kt, par=par, xsrc=xsrc, bg=bg: e.matmul(
                        T[bg][:, 0:512],
                        T["fbf"][:, par * 4096 + kt * 128:par * 4096 + (kt + 1) * 128],
                        xsrc[:, kt * TILE:(kt + 1) * TILE],
                        start=(kt == 0), stop=(kt == KT - 1)),
                        waits=(tcg, txnT[half]) + self.bank_wait(bg),
                        inc=(kt == KT - 1), tag=f"fgm{sp}_{it}_{half}_{kt}")
                self.bank_clear(bg)
                for kt in range(KT):
                    tmu = self.pe(lambda e, kt=kt, par=par, xsrc=xsrc, bu=bu: e.matmul(
                        T[bu][:, 0:512],
                        T["fbf"][:, par * 4096 + 2048 + kt * 128:par * 4096 + 2048 + (kt + 1) * 128],
                        xsrc[:, kt * TILE:(kt + 1) * TILE],
                        start=(kt == 0), stop=(kt == KT - 1)),
                        waits=(tcu, txnT[half]) + self.bank_wait(bu),
                        inc=(kt == KT - 1), tag=f"fum{sp}_{it}_{half}_{kt}")
                self.bank_clear(bu)
                hpar = half
                tsg = self.act(lambda e, bg=bg, it=it, hpar=hpar: e.activation(
                    T["hscr"][:, hpar * 512:(hpar + 1) * 512], T[bg][:, 0:512], AF.Silu,
                    scale=T["gs_sb"][:, it:it + 1]),
                    waits=(tmg, t_gs, self.guard.get(f"hscr{hpar}")), inc=True, tag=f"fsilu{sp}_{it}_{half}")
                self.bank_use(bg, tsg)
                th = self.dve(lambda e, bu=bu, it=it, half=half, hpar=hpar: e.scalar_tensor_tensor(
                    T["h_sb"][:, it * 1024 + half * 512:it * 1024 + (half + 1) * 512],
                    T[bu][:, 0:512], T["us_sb"][:, it:it + 1],
                    T["hscr"][:, hpar * 512:(hpar + 1) * 512],
                    op0=ALU.mult, op1=ALU.mult),
                    waits=(tmu, tsg, t_us, self.guard.get("h_sb")), inc=True, tag=f"fh{sp}_{it}_{half}")
                self.guard[f"hscr{hpar}"] = th
                t_h[(it, half)] = th
            self.guard[f"fbf{par}"] = t_h[(it, 1)]
        t_ong = {}
        for g in range(8):
            rows = base + g * 128
            t_og = self.ldma(T["ogrp"][:, :], T["o_sp"][rows:rows + 128, :],
                             waits=(self.guard.get("ogrp"), *(self._spill_gates or (None,))), tag=f"fog{sp}_{g}")
            t_gg = self.ldma(T["ggrp"][:, :], T["gate_sp"][rows:rows + 128, :],
                             waits=(self.guard.get("ggrp"), *(self._spill_gates or (None,))), tag=f"fgg{sp}_{g}")
            t_sq = self.act(lambda e: e.activation(T["dump"][:, 0:1024], T["ogrp"][:, :], AF.Square),
                            waits=(t_og, self.guard.get("fdump")), inc=True, tag=f"fsq{sp}_{g}")
            t_red = self.dve(lambda e: e.tensor_reduce(
                T["rmsp"][:, 0:8], T["dump"][:, 0:1024].rearrange("p (h f) -> p h f", h=8),
                axis=mybir.AxisListType.X, op=ALU.add),
                waits=(t_sq, self.guard.get("rmsp")), inc=True, tag=f"fred{sp}_{g}")
            self.guard["fdump"] = t_red
            t_ms = self.dve(lambda e: e.tensor_scalar(T["rmsp"][:, 8:16], T["rmsp"][:, 0:8],
                                                      1.0 / 128.0, None, op0=ALU.mult),
                            waits=(t_red,), inc=True, tag=f"fms{sp}_{g}")
            t_ln = self.act(lambda e: e.activation(T["rmsp"][:, 16:24], T["rmsp"][:, 8:16], AF.Ln,
                                                   bias=T["c_epsc"][:, 0:1]),
                            waits=(t_ms,), inc=True, tag=f"fln{sp}_{g}")
            t_rs = self.act(lambda e: e.activation(T["rmsp"][:, 8:16], T["rmsp"][:, 16:24], AF.Exp,
                                                   scale=-0.5),
                            waits=(t_ln,), inc=True, tag=f"frs{sp}_{g}")
            self.guard["rmsp"] = t_rs
            tong_last = None
            for hl8 in range(8):
                sl = slice(hl8 * 128, (hl8 + 1) * 128)
                t_a1 = self.dve(lambda e, sl=sl, hl8=hl8: e.scalar_tensor_tensor(
                    T["ong"][:, sl], T["ogrp"][:, sl], T["rmsp"][:, 8 + hl8:9 + hl8],
                    T["c_normw_bc"][:, :], op0=ALU.mult, op1=ALU.mult),
                    waits=(t_rs, self.guard.get("ong"), self.tok_prologue[2]), inc=False, tag=f"fo1{sp}_{g}_{hl8}")
                tong_last = self.dve(lambda e, sl=sl: e.tensor_tensor(
                    T["ong"][:, sl], T["ong"][:, sl], T["ggrp"][:, sl], op=ALU.mult),
                    waits=(t_gg,), inc=(hl8 == 7), tag=f"fo2{sp}_{g}_{hl8}")
            self.guard["ogrp"] = tong_last
            self.guard["ggrp"] = tong_last
            tev = None
            for hl8 in range(8):
                tt = self.pe(lambda e, hl8=hl8: e.transpose(
                    T["pf4"][:, 0:128], T["ong"][:, hl8 * 128:(hl8 + 1) * 128], T["c_id128f"][:, :]),
                    waits=(tong_last,) + self.bank_wait("pf4"), inc=True, tag=f"fot{sp}_{g}_{hl8}")
                self.bank_clear("pf4")
                tev = self.dve(lambda e, g=g, hl8=hl8: cp(e, 
                    T["big2"][:, g * 1024 + hl8 * 128:g * 1024 + (hl8 + 1) * 128], T["pf4"][:, 0:128]),
                    waits=(tt, self.guard.get("big2f3"), t_h[(NIT - 1, 1)]), inc=True, tag=f"foe{sp}_{g}_{hl8}")
                self.bank_use("pf4", tev)
            self.guard["ong"] = tev
            t_ong[g] = tev
        for hidN in range(4):
            t_wo = {}
            for odt in range(8):
                row = (odt * 4 + hidN) * 128
                tl = self.ldma(T["wostage"][:, :], T["Wo_b"][row:row + 128, :],
                               waits=(self.guard.get("wostage"),), tag=f"fwol{sp}_{hidN}_{odt}")
                tc = self.pool(lambda e, odt=odt: cp(e, 
                    T["wo_sb"][:, odt * 512:(odt + 1) * 512], T["wostage"][:, :]),
                    waits=(tl, self.guard.get(f"wo{odt}")), inc=True, tag=f"fwoc{sp}_{hidN}_{odt}")
                self.guard["wostage"] = tc
                t_wo[odt] = tc
            t_mm = {ts: None for ts in range(8)}
            started = {ts: False for ts in range(8)}
            for it in range(NIT):
                dpar = it % 2
                row = (hidN * NIT + it) * 128
                dstg = T["fstage"][:, dpar * 1024:dpar * 1024 + 512]
                dbfs = T["fbf"][:, dpar * 512:(dpar + 1) * 512]
                tdl = self.ldma(dstg, T["dw_b"][row:row + 128, :],
                                waits=(self.guard.get(f"dstage{dpar}"),), tag=f"fdl{sp}_{hidN}_{it}")
                tdc = self.dve(lambda e, hidN=hidN, dstg=dstg, dbfs=dbfs: e.tensor_tensor(
                    dbfs, dstg, T["ds_bc"][:, hidN * 512:(hidN + 1) * 512], op=ALU.mult),
                    waits=(tdl, tok_dsbc, self.guard.get(f"dbf{dpar}")), inc=True, tag=f"fdc{sp}_{hidN}_{it}")
                self.guard[f"dstage{dpar}"] = tdc
                last_ts = None
                for ts in range(8):
                    bk = f"pf{ts}"
                    wts = (tdc, t_h[(it, ts // 4)])
                    if not started[ts]:
                        wts = wts + self.bank_wait(bk)
                    last_ts = self.pe(lambda e, it=it, ts=ts, bk=bk, dbfs=dbfs, st=(not started[ts]): e.matmul(
                        T[bk][:, 0:512],
                        T["h_sb"][:, it * 1024 + ts * 128:it * 1024 + (ts + 1) * 128],
                        dbfs, start=st, stop=False),
                        waits=wts, inc=(ts == 7), tag=f"fdm{sp}_{hidN}_{it}_{ts}")
                    if not started[ts]:
                        self.bank_clear(bk)
                        started[ts] = True
                self.guard[f"dbf{dpar}"] = last_ts
                self._last_h_reader = last_ts
            t_wom = {}
            for odt in range(8):
                for ts in range(8):
                    bk = f"pf{ts}"
                    t_wom[ts] = self.pe(lambda e, odt=odt, ts=ts, bk=bk: e.matmul(
                        T[bk][:, 0:512],
                        T["big2"][:, ts * 1024 + odt * 128:ts * 1024 + (odt + 1) * 128],
                        T["wo_sb"][:, odt * 512:(odt + 1) * 512],
                        start=False, stop=(odt == 7)),
                        waits=(t_wo[odt], t_ong[ts]), inc=(odt == 7), tag=f"fwm{sp}_{hidN}_{odt}_{ts}")
            for ts in range(8):
                rows = base + ts * 128
                opar = ts % 2
                txl = self.ldma(T["x_sb"][:, opar * 512:(opar + 1) * 512],
                                T["x_own"][rows:rows + 128, hidN * 512:(hidN + 1) * 512],
                                waits=(self.guard.get(f"fx{opar}"),), tag=f"fxl{sp}_{hidN}_{ts}")
                tev = self.dve(lambda e, ts=ts, opar=opar: e.tensor_tensor(
                    T["ostage"][:, opar * 512:(opar + 1) * 512], T[f"pf{ts}"][:, 0:512],
                    T["x_sb"][:, opar * 512:(opar + 1) * 512], op=ALU.add),
                    waits=(t_wom[ts], txl, self.guard.get(f"ost{opar}")), inc=True, tag=f"fev{sp}_{hidN}_{ts}")
                self.bank_use(f"pf{ts}", tev)
                started[ts] = False
                tout = self.sdma(T["out"][rows:rows + 128, hidN * 512:(hidN + 1) * 512],
                                 T["ostage"][:, opar * 512:(opar + 1) * 512],
                                 waits=(tev,), tag=f"fout{sp}_{hidN}_{ts}")
                self.guard[f"ost{opar}"] = tout
                self.guard[f"fx{opar}"] = tev
        self.guard["big2f3"] = t_wom[7] if t_wom else None
        self.guard["big2"] = self.guard.get("big2f3")
        self.guard["xnT"] = (t_h[(NIT - 1, 0)],)
        self.guard["h_sb"] = self._last_h_reader

    def build_all(self):
        from contextlib import ExitStack
        nc = self.nc
        self.dram_inputs()
        self.s_ping = {h: 0 for h in range(NH)}
        self.tok_s = {}
        self.tok_srep = {}
        self.tok_conv_prev = {}
        self.tok_otile = {}
        self._spill_gates = ()
        with ExitStack() as outer:
            self.alloc_outer(outer)
            with ExitStack() as actx:
                self.alloc_attn(actx)
                self.prologue()
                self.prologue2()
                if "S" in self.debug_names:
                    self.dve(lambda e: e.memset(self.t["s_bf"][:, :], 0.0), inc=True, tag="dbgS0")
                for phase in self.do_phases:
                    self.attn_phase(phase)
                if "S" in self.debug_names:
                    ts = self.sdma(self.t["dbg_S"][:, :], self.t["s_bf"][:, :],
                                   waits=tuple(self.tok_s.get(h) for h in range(NH)) +
                                         tuple(self.tok_srep.get(h) for h in range(NH)),
                                   tag="dbgS")
                if not self.do_ffn:
                    fin = self.kb.dma_barrier("dact") + self.kb.dma_barrier("dsp")
                    self.kb.op("act", lambda e: e.nop(), waits=fin, tag="final_wait_dbg")
                self.kb.flush("attn")
            if self.do_ffn:
                with ExitStack() as fctx:
                    self.alloc_ffn(fctx)
                    self.ffn()
                    self.kb.flush("ffn")
        return nc


def build_kernel(debug_names=(), ntile=NTILE, do_ffn=True, do_phases=("Ra", "Fa", "Rb", "Fb")):
    nc = bass.Bass("TRN2", target_bir_lowering=False)
    p = Prog(nc, debug_names=debug_names, ntile=ntile, do_ffn=do_ffn, do_phases=do_phases)
    p.build_all()
    return nc, p


def kernel(**inputs):
    nc, _ = build_kernel()
    in_maps = [stage_inputs_for_core(inputs, c) for c in range(8)]
    res = run_bass_kernel_spmd(nc, in_maps, core_ids=list(range(8)))
    out = np.empty((BATCH, LQ, HID), np.float32)
    for c in range(8):
        b, hh = c // 2, c % 2
        out[b, hh * HALF:(hh + 1) * HALF] = np.asarray(res.results[c]["out"])
    return out


# version 07
# speedup vs baseline: 1.0319x; 1.0319x over previous
import math
import numpy as np
import ml_dtypes
import concourse.bass as bass
import concourse.mybir as mybir
from concourse.bass_utils import run_bass_kernel_spmd

F32 = mybir.dt.float32
BF16 = mybir.dt.bfloat16
AF = mybir.ActivationFunctionType
ALU = mybir.AluOpType

HID = 2048
NH = 8
DH = 128
NJ = 2
LQ = 4096
BATCH = 4
HALF = 2048
TILE = 512
NTILE = HALF // TILE
CT = 64
NCHUNK = TILE // CT
HPG = 4
INTER = 5632
NIT = INTER // 128
KT = HID // 128
LN128 = math.log(128.0)

ENG2BLK = {"sp": "sync", "pool": "gpsimd", "pe": "tensor", "dve": "vector", "act": "scalar"}



def cp(e, dst, src):
    f = getattr(e, "tensor_copy", None)
    if f is not None:
        return f(dst, src)
    return e.copy(dst, src)

class KB:

    NSEM_PER_Q = 8
    GROUP_QUEUES = ("dsp", "dact", "dpool", "pe")
    QINC = {"dsp": 16, "dact": 16, "dpool": 16, "pe": 1}

    def __init__(self, nc):
        self.nc = nc
        self.ops = {e: [] for e in ENG2BLK}
        self.cnt = {}
        self.known = {e: {} for e in ENG2BLK}
        self.sem_names = ["dve", "act", "pool"]
        for q in self.GROUP_QUEUES:
            self.sem_names += [f"{q}{i}" for i in range(self.NSEM_PER_Q)]
        for s in self.sem_names:
            self.cnt[s] = 0
        self.handles = {}
        self.prunable = {"dve", "act"}
        self.qcur = {q: 0 for q in self.GROUP_QUEUES}
        self.qclosed = {}
        self.last_tok = {}
        self.qrelease = {}
        self.pending_cert = {e: [] for e in ENG2BLK}
        self.dma_sem_set = {sn for q in self.GROUP_QUEUES for sn in
                            [f"{q}{i}" for i in range(self.NSEM_PER_Q)]}

    def dma_sem(self, q):
        for step in range(self.NSEM_PER_Q + 1):
            sn = f"{q}{self.qcur[q]}"
            if not self.qclosed.get(sn):
                return sn, ()
            cert = self.qrelease.get(sn)
            if cert is not None:
                self.qclosed[sn] = False
                del self.qrelease[sn]
                return sn, (cert,)
            self.qcur[q] = (self.qcur[q] + 1) % self.NSEM_PER_Q
        raise AssertionError(f"no reusable dma sem on queue {q}")

    def dma_barrier(self, q):
        toks = []
        for i in range(self.NSEM_PER_Q):
            sn = f"{q}{i}"
            if self.cnt[sn] > 0:
                toks.append((sn, self.cnt[sn]))
                self.qclosed[sn] = True
        return tuple(toks)

    @staticmethod
    def _flatten(waits):
        out = []
        def rec(t):
            if t is None:
                return
            if isinstance(t, tuple):
                if len(t) == 2 and isinstance(t[0], str):
                    out.append(t)
                else:
                    for x in t:
                        rec(x)
        for t in waits:
            rec(t)
        return out

    def op(self, eng, fn, waits=(), inc=None, inc_n=1, tag=""):
        w2 = []
        for t in self._flatten(waits):
            s, v = t
            if s in self.dma_sem_set:
                v = self.cnt[s]
                if v == 0:
                    continue
                if not self.qclosed.get(s):
                    self.qclosed[s] = True
                if s not in self.qrelease:
                    self.pending_cert[eng].append((s, v))
            if s in self.prunable or s == eng:
                if self.known[eng].get(s, 0) >= v:
                    continue
                self.known[eng][s] = max(self.known[eng].get(s, 0), v)
            else:
                if self.known[eng].get(s, 0) >= v:
                    continue
                self.known[eng][s] = max(self.known[eng].get(s, 0), v)
            w2.append((s, v))
        is_dma = inc in self.dma_sem_set if inc is not None else False
        extra = ()
        if eng in ("dve", "act", "pool") and not is_dma:
            lt = self.last_tok.get(eng)
            if lt is not None:
                s2, v2 = lt
                if self.known[eng].get(s2, 0) < v2:
                    self.known[eng][s2] = v2
                    w2.append((s2, v2))
            if inc is None:
                inc = {"dve": "dve", "act": "act", "pool": "pool"}[eng]
        elif eng == "pe":
            inc, extra = self.dma_sem("pe")
            inc_n = 1
            for t in extra:
                s2, v2 = t
                if self.known[eng].get(s2, 0) < v2:
                    self.known[eng][s2] = v2
                    w2.append((s2, v2))
        tok = None
        incs = []
        if inc is not None:
            self.cnt[inc] += inc_n
            incs = [(inc, inc_n)]
            tok = (inc, self.cnt[inc])
        if eng in ("dve", "act", "pool") and not is_dma:
            self.last_tok[eng] = tok
        if tok is not None and self.pending_cert[eng]:
            for (s, v) in self.pending_cert[eng]:
                if self.cnt[s] == v:
                    self.qrelease[s] = tok
            self.pending_cert[eng] = []
        self.ops[eng].append((w2, fn, incs, tag))
        return tok

    def barrier_token(self, eng):
        raise NotImplementedError

    def flush(self, name):
        nc = self.nc
        handles = self.handles
        with nc.Block(name) as blk:
            for ename, blkname in ENG2BLK.items():
                deco = getattr(blk, blkname)
                ops = self.ops[ename]

                def make_body(ops):
                    def body(eng):
                        for (waits, fn, incs, tag) in ops:
                            for (s, v) in waits:
                                eng.wait_ge(handles[s], v)
                            inst = fn(eng)
                            for (s, k) in incs:
                                assert inst is not None, f"op {tag} has inc but returned no instruction"
                                inst.then_inc(handles[s], k)
                    return body

                deco(make_body(ops))
            self.ops = {e: [] for e in ENG2BLK}
        comp = ["dve", "act", "pool"] + [f"pe{i}" for i in range(self.NSEM_PER_Q)]
        for e in ENG2BLK:
            for sname in comp:
                self.known[e][sname] = self.cnt[sname]



def blockify_lhsT(w, mtile=128):
    K, M = w.shape
    nK, nM = K // 128, M // mtile
    return np.ascontiguousarray(
        w.reshape(nK, 128, nM, mtile).transpose(2, 1, 0, 3).reshape(nM, 128, nK * mtile)
    )


def make_masks():
    tt = np.arange(CT)
    strict = (tt[None, :] < tt[:, None]).astype(np.float32)
    incl = (tt[None, :] <= tt[:, None]).astype(np.float32)
    maskM = np.block([[strict, strict], [incl, strict]])
    maskM_neg = np.tile(-maskM, (1, 4)).astype(ml_dtypes.bfloat16)
    maskAT = np.tile(-np.concatenate([incl.T, incl.T], axis=0), (1, 4)).astype(ml_dtypes.bfloat16)
    trilT = np.triu(np.ones((CT, CT), np.float32))
    return maskM_neg, maskAT, trilT


def stage_inputs_for_core(inputs, core):
    b, hh = core // 2, core % 2
    x = np.asarray(inputs["L0_attn_x"])
    d = {}
    d["x_own"] = np.ascontiguousarray(x[b, hh * HALF:(hh + 1) * HALF])
    d["x_replay"] = np.ascontiguousarray(x[b, 0:HALF])
    d["x_halo"] = (np.ascontiguousarray(x[b, HALF - 3:HALF]) if hh == 1
                   else np.zeros((3, HID), np.float32))
    d["flag"] = np.full((128, 1), float(hh), np.float32)

    d["Wq_b"] = blockify_lhsT(np.asarray(inputs["L0_attn_Wq"])).reshape(8 * 128, KT * 128)
    d["Wk_b"] = blockify_lhsT(np.asarray(inputs["L0_attn_Wk"])).reshape(16 * 128, KT * 128)
    d["Wv_b"] = blockify_lhsT(np.asarray(inputs["L0_attn_Wv"])).reshape(16 * 128, KT * 128)
    wg = np.asarray(inputs["L0_attn_Wg"])
    d["Wg_r"] = np.ascontiguousarray(np.concatenate([
        wg[:, hg * 512:(hg + 1) * 512].reshape(KT, 128, 512).transpose(1, 0, 2).reshape(128, KT * 512)
        for hg in range(2)], axis=0))
    wba = np.concatenate([np.asarray(inputs["L0_attn_Wb"]),
                          np.asarray(inputs["L0_attn_Wa"])], axis=1)
    d["Wba_b"] = blockify_lhsT(wba, mtile=24).reshape(128, KT * 24)
    d["Wo_b"] = np.ascontiguousarray(
        np.asarray(inputs["L0_attn_Wo"]).reshape(8, 128, 4, 512).transpose(0, 2, 1, 3)).reshape(8 * 4 * 128, 512)
    d["convq"] = np.ascontiguousarray(np.asarray(inputs["L0_attn_conv_q_w"])[:, 0, :])
    d["convk"] = np.ascontiguousarray(np.asarray(inputs["L0_attn_conv_k_w"])[:, 0, :])
    d["convv"] = np.ascontiguousarray(np.asarray(inputs["L0_attn_conv_v_w"])[:, 0, :])
    d["A_log"] = np.asarray(inputs["L0_attn_A_log"]).reshape(1, 8)
    d["dt_bias"] = np.asarray(inputs["L0_attn_dt_bias"]).reshape(1, 8)
    d["norm_w"] = np.asarray(inputs["L0_attn_norm_w"]).reshape(1, 128)

    gw = np.asarray(inputs["L0_ffn_gate_w"])
    uw = np.asarray(inputs["L0_ffn_up_w"])
    dw = np.asarray(inputs["L0_ffn_down_w"])
    d["gw_b"] = blockify_lhsT(np.ascontiguousarray(gw.T)).reshape(NIT * 128, KT * 128)
    d["uw_b"] = blockify_lhsT(np.ascontiguousarray(uw.T)).reshape(NIT * 128, KT * 128)
    dwT = np.ascontiguousarray(dw.T)
    d["dw_b"] = np.ascontiguousarray(dwT.reshape(NIT, 128, 4, 512).transpose(2, 0, 1, 3)).reshape(4 * NIT * 128, 512)
    d["gs"] = np.ascontiguousarray(np.asarray(inputs["L0_ffn_gate_s"]).reshape(NIT, 128).T)
    d["us"] = np.ascontiguousarray(np.asarray(inputs["L0_ffn_up_s"]).reshape(NIT, 128).T)
    d["ds_row"] = np.asarray(inputs["L0_ffn_down_s"]).reshape(1, HID)

    maskM_neg, maskAT, trilT = make_masks()
    d["maskM"] = maskM_neg
    d["maskAT"] = maskAT
    d["trilT"] = trilT
    d["id128b"] = np.eye(128, dtype=ml_dtypes.bfloat16)
    d["id128f"] = np.eye(128, dtype=np.float32)
    d["id64f"] = np.eye(64, dtype=np.float32)
    d["id4f"] = np.eye(4, dtype=np.float32)
    d["id3b"] = np.eye(3, dtype=ml_dtypes.bfloat16)
    d["ones1f"] = np.ones((1, 128), np.float32)
    d["epsc"] = np.concatenate([
        np.full((128, 1), 1e-5, np.float32),
        np.full((128, 1), 1e-18, np.float32),
        np.full((128, 1), 1.0, np.float32),
        np.full((128, 1), 0.0, np.float32),
    ], axis=1)
    return d



class Prog:
    def __init__(self, nc, debug_names=(), ntile=NTILE, do_ffn=True, do_phases=("Ra", "Fa", "Rb", "Fb")):
        self.nc = nc
        self.kb = KB(nc)
        self.debug_names = set(debug_names)
        self.debug_outs = {}
        self.ntile = ntile
        self.do_ffn = do_ffn
        self.do_phases = do_phases
        self.ctx = []
        self.guard = {}
        self.guardB = {}
        self.t = {}

    def dram_inputs(self):
        nc = self.nc
        T = self.t
        def di(name, shape, dt=F32):
            T[name] = nc.dram_tensor(name, list(shape), dt, kind="ExternalInput")
        di("x_own", (HALF, HID)); di("x_replay", (HALF, HID)); di("x_halo", (3, HID))
        di("flag", (128, 1))
        di("Wq_b", (8 * 128, KT * 128)); di("Wk_b", (16 * 128, KT * 128)); di("Wv_b", (16 * 128, KT * 128))
        di("Wg_r", (2 * 128, KT * 512)); di("Wba_b", (128, KT * 24))
        di("Wo_b", (8 * 4 * 128, 512))
        di("convq", (8 * 128, 4)); di("convk", (16 * 128, 4)); di("convv", (16 * 128, 4))
        di("A_log", (1, 8)); di("dt_bias", (1, 8)); di("norm_w", (1, 128))
        di("gw_b", (NIT * 128, KT * 128)); di("uw_b", (NIT * 128, KT * 128))
        di("dw_b", (4 * NIT * 128, 512))
        di("gs", (128, NIT)); di("us", (128, NIT)); di("ds_row", (1, HID))
        di("maskM", (128, 512), BF16); di("maskAT", (128, 256), BF16)
        di("trilT", (64, 64)); di("id128b", (128, 128), BF16); di("id128f", (128, 128))
        di("id64f", (64, 64)); di("id4f", (4, 4)); di("id3b", (3, 3), BF16)
        di("ones1f", (1, 128)); di("epsc", (128, 4))
        T["out"] = nc.dram_tensor("out", [HALF, HID], F32, kind="ExternalOutput")
        def dk(name):
            return "ExternalOutput" if name in self.debug_names else "Internal"
        T["xnTr_sp"] = nc.dram_tensor("xnTr_sp", [KT * 128, HALF], BF16, kind=dk("xnTr_sp"))
        T["xnTo_sp"] = nc.dram_tensor("xnTo_sp", [KT * 128, HALF], BF16, kind=dk("xnTo_sp"))
        T["o_sp"] = nc.dram_tensor("o_sp", [HALF, 1024], BF16, kind=dk("o_sp"))
        T["gate_sp"] = nc.dram_tensor("gate_sp", [HALF, 1024], BF16, kind=dk("gate_sp"))
        if "S" in self.debug_names:
            T["dbg_S"] = nc.dram_tensor("dbg_S", [128, NH * 2 * 128], BF16, kind="ExternalOutput")

    def dbg(self, name, shape, dt=F32):
        if name in self.debug_names:
            h = self.nc.dram_tensor("dbg_" + name, list(shape), dt, kind="ExternalOutput")
            self.debug_outs[name] = h
            return h
        return None

    def alloc_outer(self, ctx):
        nc = self.nc
        T = self.t
        def sb(name, shape, dt):
            T[name] = ctx.enter_context(nc.sbuf_tensor(name, list(shape), dt))
        sb("c_id128b", (128, 128), BF16); sb("c_id128f", (128, 128), F32)
        sb("c_id64f", (64, 64), F32); sb("c_id4f", (4, 4), F32); sb("c_id3b", (3, 3), BF16)
        sb("c_ones1f", (1, 128), F32); sb("c_maskM", (128, 512), BF16); sb("c_maskAT", (128, 256), BF16)
        sb("c_trilT", (64, 64), F32); sb("c_epsc", (128, 4), F32)
        sb("c_dtb64", (64, 64), F32)
        sb("c_negA64", (64, 64), F32)
        sb("c_normw_bc", (128, 128), F32)
        sb("c_negexpA", (128, 8), F32); sb("c_dtb", (128, 8), F32)
        sb("c_flag", (128, 1), F32)
        sb("c_rows", (1, 768), F32)
        sb("x_sb", (128, HID), F32)
        sb("xh_sb", (3, HID), F32)
        sb("xnh_sb", (3, HID), BF16)
        sb("xnTh_sb", (128, KT * 3), BF16)
        sb("xnT_sb", (128, KT * TILE), BF16)
        sb("stats", (128, 12), F32)
        sb("dump", (128, HID), BF16)
        sb("trow", (1, 256), F32)
        sb("trow2", (1, 128), F32)
        sb("panx", (128, 16), F32)
        for sname in self.kb.sem_names:
            self.kb.handles[sname] = ctx.enter_context(nc.semaphore("sem_" + sname))

    def alloc_attn(self, ctx):
        nc, T = self.nc, self.t
        def sb(name, shape, dt):
            T[name] = ctx.enter_context(nc.sbuf_tensor(name, list(shape), dt))
        def ps(name, shape, dt):
            T[name] = ctx.enter_context(nc.psum_tensor(name, list(shape), dt))
        ps("pT0", (128, 1024), BF16); ps("pT1", (128, 1024), BF16)
        ps("pA", (128, 512), F32); ps("pB", (128, 512), F32)
        ps("pC", (128, 512), F32); ps("pD", (128, 512), F32)
        ps("pE", (128, 512), F32); ps("pF", (128, 512), F32)
        sb("xn_sb", (128, HID), BF16)
        sb("bg_tm", (64, NCHUNK * 24), F32)
        sb("bgtmp", (24, TILE), BF16)
        sb("lnb_tm", (64, NCHUNK * 16), F32)
        sb("gsp_tm", (64, NCHUNK * 8), F32)
        sb("bgt2", (64, NCHUNK * 16), F32)
        sb("bgt3", (64, NCHUNK * 16), F32)
        sb("gcum_sb", (64, 8), F32)
        sb("gcumT_sb", (4, 64), F32)
        sb("gCl_big", (128, 4), F32)
        sb("Lam_big", (128, 8), F32)
        sb("gce", (128, 4), F32)
        sb("pan1", (128, 24), F32)
        sb("pan2", (64, 32), F32)
        sb("s_bf", (128, NH * 2 * 128), BF16)
        sb("srep_bf", (128, 4 * 128), BF16)
        sb("D_sb", (128, 512), F32)
        sb("Eq_sb", (128, 256), BF16)
        sb("nN_a", (128, 512), BF16)
        sb("nN_b", (128, 512), BF16)
        sb("nT_a", (128, 512), BF16)
        sb("nT_b", (128, 512), BF16)
        sb("nLN_a", (128, 512), BF16)
        sb("nLN_b", (128, 512), BF16)
        sb("nLT_a", (128, 512), BF16)
        sb("nLT_b", (128, 512), BF16)
        sb("ucp", (128, 1024), BF16)
        sb("ot1w", (64, 512), BF16)
        sb("ktm_sb", (128, 1024), BF16)
        sb("vtm_sb", (128, 1024), BF16)
        sb("am_sb", (128, 512), BF16)
        sb("kss", (128, 8), F32)
        sb("qss", (64, 8), F32)
        sb("otile", (128, 1024), BF16)
        sb("wq_sb", (128, 4 * KT * 128), BF16)
        sb("wk_sb", (128, 8 * KT * 128), BF16)
        sb("wv_sb", (128, 8 * KT * 128), BF16)
        sb("wg_sb", (128, KT * 512), BF16)
        sb("wba_sb", (128, KT * 24), BF16)
        sb("wstage", (128, 1024), F32)
        sb("convw", (128, 20 * 4), F32)
        sb("pre", (128, 20 * (TILE + 3)), BF16)
        sb("halosave", (128, 20 * 3), BF16)
        sb("cacc", (128, 2 * TILE), BF16)
        sb("gstage", (128, TILE), BF16)

    def alloc_ffn(self, ctx):
        nc, T = self.nc, self.t
        def sb(name, shape, dt):
            T[name] = ctx.enter_context(nc.sbuf_tensor(name, list(shape), dt))
        def ps(name, shape, dt):
            T[name] = ctx.enter_context(nc.psum_tensor(name, list(shape), dt))
        for i in range(8):
            ps(f"pf{i}", (128, 512), F32)
        sb("h_sb", (128, NIT * 1024), BF16)
        sb("big2", (128, 8192), BF16)
        sb("fstage", (128, 2048), F32)
        sb("fbf", (128, 2 * 4096), BF16)
        sb("wostage", (128, 512), F32)
        sb("wo_sb", (128, 8 * 512), BF16)
        sb("ds_bc", (128, HID), BF16)
        sb("gs_sb", (128, NIT), F32)
        sb("us_sb", (128, NIT), F32)
        sb("ogrp", (128, 1024), BF16)
        sb("ggrp", (128, 1024), BF16)
        sb("ong", (128, 1024), F32)
        sb("rmsp", (128, 24), F32)
        sb("hscr", (128, 2 * 512), F32)
        sb("ostage", (128, 2 * 512), F32)
    def pe(self, fn, waits=(), inc=False, tag=""):
        self.kb.op("pe", fn, waits, "pe" if inc else None, tag=tag)
        kb = self.kb
        return tuple((f"pe{i}", kb.cnt[f"pe{i}"]) for i in range(kb.NSEM_PER_Q)
                     if kb.cnt[f"pe{i}"] > 0)

    def dve(self, fn, waits=(), inc=False, tag=""):
        return self.kb.op("dve", fn, waits, "dve" if inc else None, tag=tag)

    def act(self, fn, waits=(), inc=False, tag=""):
        return self.kb.op("act", fn, waits, "act" if inc else None, tag=tag)

    def pool(self, fn, waits=(), inc=False, tag=""):
        return self.kb.op("pool", fn, waits, "pool" if inc else None, tag=tag)

    def ldma(self, dst, src, waits=(), tag=""):
        sn, extra = self.kb.dma_sem("dsp")
        return self.kb.op("sp", lambda e: e.dma_start(dst, src), tuple(waits) + extra, sn, 16, tag=tag)

    def sdma(self, dst, src, waits=(), tag=""):
        sn, extra = self.kb.dma_sem("dact")
        return self.kb.op("act", lambda e: e.dma_start(dst, src), tuple(waits) + extra, sn, 16, tag=tag)

    def pdma(self, dst, src, waits=(), tag=""):
        sn, extra = self.kb.dma_sem("dpool")
        return self.kb.op("pool", lambda e: e.dma_start(dst, src), tuple(waits) + extra, sn, 16, tag=tag)

    def prologue(self):
        T = self.t
        loads = [
            ("c_id128b", T["id128b"]), ("c_id128f", T["id128f"]), ("c_id64f", T["id64f"]),
            ("c_id4f", T["id4f"]), ("c_id3b", T["id3b"]), ("c_ones1f", T["ones1f"]),
            ("c_maskM", T["maskM"]), ("c_maskAT", T["maskAT"]), ("c_trilT", T["trilT"]),
            ("c_epsc", T["epsc"]), ("c_flag", T["flag"]),
        ]
        toks = []
        for name, src in loads:
            toks.append(self.ldma(T[name][:, :], src[:, :], tag="ld_" + name))
        t_alog = self.ldma(T["c_rows"][0:1, 0:8], T["A_log"][:, :], tag="ld_alog")
        t_dtb = self.ldma(T["c_rows"][0:1, 8:16], T["dt_bias"][:, :], tag="ld_dtb")
        t_nw = self.ldma(T["c_rows"][0:1, 16:144], T["norm_w"][:, :], tag="ld_nw")
        t_xh = self.ldma(T["xh_sb"][:, :], T["x_halo"][:, :], tag="ld_xh")
        self.tok_consts = self.dve(lambda e: e.memset(T["stats"][0:1, 11:12], 0.0),
                                   waits=(toks[-1],), inc=True, tag="constfence")
        def bc(dst_psum, row_ap, n):
            return lambda e: e.matmul(dst_psum, T["c_ones1f"][:, :], row_ap, start=True, stop=True)
        tm = self.pe(bc(T["pA"][:, 0:8], T["c_rows"][0:1, 0:8], 8),
                     waits=(t_alog, toks[5]), inc=True, tag="bc_alog")
        tm2 = self.pe(bc(T["pA"][:, 8:16], T["c_rows"][0:1, 8:16], 8),
                      waits=(t_dtb,), inc=True, tag="bc_dtb")
        tm3 = self.pe(bc(T["pA"][:, 16:144], T["c_rows"][0:1, 16:144], 128),
                      waits=(t_nw,), inc=True, tag="bc_nw")
        t1 = self.act(lambda e: e.activation(T["panx"][:, 0:8], T["pA"][:, 0:8], AF.Exp),
                      waits=(tm,), inc=True, tag="expA")
        t2 = self.dve(lambda e: e.tensor_scalar(T["c_negexpA"][:, :], T["panx"][:, 0:8], -1.0, None, op0=ALU.mult),
                      waits=(t1,), inc=True, tag="negA")
        t3 = self.dve(lambda e: cp(e, T["c_dtb"][:, :], T["pA"][:, 8:16]),
                      waits=(tm2,), inc=True, tag="cp_dtb")
        t4 = self.dve(lambda e: cp(e, T["c_normw_bc"][:, :], T["pA"][:, 16:144]),
                      waits=(tm3,), inc=True, tag="cp_nw")
        self.guard["pA"] = t4
        self.tok_prologue = (t2, t3, t4, self.tok_consts)
        tstat = self.act(lambda e: e.activation(T["xnh_sb"][0:3, :], T["xh_sb"][0:3, :], AF.Copy,
                                                accum_out=T["stats"][0:3, 1:2]),
                         waits=(t_xh,), inc=True, tag="halo_sum")
        tsq = self.act(lambda e: e.activation(T["xnh_sb"][0:3, :], T["xh_sb"][0:3, :], AF.Square,
                                              accum_out=T["stats"][0:3, 3:4]),
                       waits=(t_xh,), inc=True, tag="halo_sq")
        tmn = self.dve(lambda e: e.tensor_scalar(T["stats"][0:3, 4:5], T["stats"][0:3, 1:2], 1.0 / HID, None, op0=ALU.mult),
                       waits=(tstat,), inc=True, tag="halo_m")
        tms = self.dve(lambda e: e.tensor_scalar(T["stats"][0:3, 5:6], T["stats"][0:3, 3:4], 1.0 / HID, None, op0=ALU.mult),
                       waits=(tsq,), inc=True, tag="halo_ms")
        tm2v = self.dve(lambda e: e.tensor_tensor(T["stats"][0:3, 6:7], T["stats"][0:3, 4:5], T["stats"][0:3, 4:5], op=ALU.mult),
                        waits=(tmn,), inc=True, tag="halo_m2")
        tvar = self.dve(lambda e: e.tensor_tensor(T["stats"][0:3, 7:8], T["stats"][0:3, 5:6], T["stats"][0:3, 6:7], op=ALU.subtract),
                        waits=(tms, tm2v), inc=True, tag="halo_var")
        tln = self.act(lambda e: e.activation(T["stats"][0:3, 8:9], T["stats"][0:3, 7:8], AF.Ln, bias=T["c_epsc"][0:3, 0:1]),
                       waits=(tvar, self.tok_consts), inc=True, tag="halo_ln")
        trs = self.act(lambda e: e.activation(T["stats"][0:3, 9:10], T["stats"][0:3, 8:9], AF.Exp, scale=-0.5),
                       waits=(tln,), inc=True, tag="halo_rs")
        tmrs = self.dve(lambda e: e.tensor_tensor(T["stats"][0:3, 10:11], T["stats"][0:3, 4:5], T["stats"][0:3, 9:10], op=ALU.mult),
                        waits=(trs,), inc=True, tag="halo_mrs")
        txn = self.dve(lambda e: e.tensor_scalar(T["xnh_sb"][0:3, :], T["xh_sb"][0:3, :],
                                                 T["stats"][0:3, 9:10], T["stats"][0:3, 10:11],
                                                 op0=ALU.mult, op1=ALU.subtract),
                       waits=(tmrs,), inc=True, tag="halo_xn")
        gT = None
        for kt in range(KT):
            tt = self.pe(lambda e, kt=kt: e.transpose(T["pT0"][0:128, kt * 4:kt * 4 + 3],
                                                      T["xnh_sb"][0:3, kt * 128:(kt + 1) * 128],
                                                      T["c_id3b"][:, :]),
                         waits=(txn, self.guard.get("pT0"), self.tok_consts), inc=(kt == KT - 1), tag=f"halo_tp{kt}")
            if kt == KT - 1:
                gT = tt
        tcp = None
        for kt in range(KT):
            tcp = self.dve(lambda e, kt=kt: cp(e, T["xnTh_sb"][:, kt * 3:(kt + 1) * 3],
                                                          T["pT0"][:, kt * 4:kt * 4 + 3]),
                           waits=(gT,), inc=(kt == KT - 1), tag=f"halo_evac{kt}")
        self.guard["pT0"] = tcp
        self.tok_xnTh = tcp

    def prologue2(self):
        T = self.t
        t2, t3, _, _ = self.tok_prologue
        last = None
        for c in range(NCHUNK):
            last = self.dve(lambda e, c=c: cp(e, T["c_dtb64"][0:64, c * 8:(c + 1) * 8], T["c_dtb"][0:64, :]),
                            waits=(t3,), inc=(c == NCHUNK - 1), tag=f"dtb64_{c}")
        self.tok_dtb64 = last
        last = None
        for c in range(NCHUNK):
            last = self.dve(lambda e, c=c: cp(e, T["c_negA64"][0:64, c * 8:(c + 1) * 8], T["c_negexpA"][0:64, :]),
                            waits=(t2,), inc=(c == NCHUNK - 1), tag=f"negA64_{c}")
        self.tok_negA64 = last

    def load_attn_weights(self, hg, want_qg, only_qg=False):
        T = self.t
        jobs = []
        if not only_qg:
            for i in range(HPG):
                for j in range(NJ):
                    mt = j * NH + hg * HPG + i
                    loc = j * HPG + i
                    jobs.append((T["Wk_b"][mt * 128:(mt + 1) * 128, :], "wk_sb", loc * KT * 128, KT * 128))
                    jobs.append((T["Wv_b"][mt * 128:(mt + 1) * 128, :], "wv_sb", loc * KT * 128, KT * 128))
        if want_qg:
            for i in range(HPG):
                mt = hg * HPG + i
                jobs.append((T["Wq_b"][mt * 128:(mt + 1) * 128, :], "wq_sb", i * KT * 128, KT * 128))
            for c in range(4):
                jobs.append((T["Wg_r"][hg * 128:(hg + 1) * 128, c * 2048:(c + 1) * 2048], "wg_sb", c * 2048, 2048))
        if not only_qg:
            jobs.append((T["Wba_b"][:, :], "wba_sb", 0, KT * 24))
        last = None
        wsb = self.guard.get("wsball")
        for n, (src, dstn, c0, w) in enumerate(jobs):
            for half in range(0, w, 512):
                hw = min(512, w - half)
                flip = getattr(self, "_wst_flip", 0)
                self._wst_flip = 1 - flip
                stg = T["wstage"][:, flip * 512:flip * 512 + hw]
                tl = self.ldma(stg, src[:, half:half + hw],
                               waits=(self.guard.get(f"wstage{flip}"),),
                               tag=f"wld{hg}_{only_qg}_{n}_{half}")
                tc = self.pool(lambda e, stg=stg, dstn=dstn, c0=c0, half=half, hw=hw: cp(e, 
                    T[dstn][:, c0 + half:c0 + half + hw], stg), waits=(tl, wsb), inc=True,
                    tag=f"wcast{hg}_{only_qg}_{n}_{half}")
                self.guard[f"wstage{flip}"] = tc
                last = tc
        convjobs = []
        for i in range(HPG):
            if not only_qg:
                for j in range(NJ):
                    g = j * NH + hg * HPG + i
                    loc = j * HPG + i
                    convjobs.append((T["convk"][g * 128:(g + 1) * 128, :], 4 + loc))
                    convjobs.append((T["convv"][g * 128:(g + 1) * 128, :], 12 + loc))
            if want_qg:
                gq = hg * HPG + i
                convjobs.append((T["convq"][gq * 128:(gq + 1) * 128, :], 0 + i))
        tconv = None
        for src, slot in convjobs:
            tconv = self.ldma(T["convw"][:, slot * 4:(slot + 1) * 4], src, waits=(wsb,),
                              tag=f"cw{hg}_{slot}")
        return last, tconv

    def bank_wait(self, bank):
        return tuple(self.guardB.get(bank, ()))

    def bank_use(self, bank, tok):
        self.guardB.setdefault(bank, []).append(tok)

    def bank_clear(self, bank):
        self.guardB[bank] = []

    def ln_tile(self, phase, ti):
        T = self.t
        xsrc = T["x_replay"] if phase[0] == "R" else T["x_own"]
        spill = T["xnTr_sp"] if phase[0] == "R" else T["xnTo_sp"]
        if phase in ("Rb", "Fb"):
            tok = None
            for kt in range(KT):
                tok = self.ldma(T["xnT_sb"][:, kt * TILE:(kt + 1) * TILE],
                                spill[kt * 128:(kt + 1) * 128, ti * TILE:(ti + 1) * TILE],
                                waits=tuple(self.guard.get("xnT") or ()) + (self._spill_gates or ()),
                                tag=f"xnTld{ti}_{kt}")
            self.tok_xnT = (tok,)
            return (tok,)
        st = T["stats"]
        lastev = None
        prev = getattr(self, "_ln_prev", {})
        for g in range(4):
            par = 0
            rows = ti * TILE + g * 128
            xg = T["x_sb"][:, 0:HID]
            tx = self.ldma(xg, xsrc[rows:rows + 128, :],
                           waits=(self.guard.get(f"x{par}"),), tag=f"xld{ti}_{g}")
            xng0 = T["xn_sb"][:, 0:HID]
            t1 = self.act(lambda e, xg=xg, xng0=xng0: e.activation(xng0, xg, AF.Copy, accum_out=st[:, 1:2]),
                          waits=(tx, prev.get("t3"), self.guard.get(f"xn{par}")),
                          inc=True, tag=f"lnsum{ti}_{g}")
            t2 = self.act(lambda e, xg=xg, xng0=xng0: e.activation(xng0, xg, AF.Square, accum_out=st[:, 3:4]),
                          waits=(tx, prev.get("t4")), inc=True, tag=f"lnsq{ti}_{g}")
            t3 = self.dve(lambda e: e.tensor_scalar(st[:, 4:5], st[:, 1:2], 1.0 / HID, None, op0=ALU.mult),
                          waits=(t1,), inc=True, tag=f"lnm{ti}_{g}")
            t4 = self.dve(lambda e: e.tensor_scalar(st[:, 5:6], st[:, 3:4], 1.0 / HID, None, op0=ALU.mult),
                          waits=(t2,), inc=True, tag=f"lnms{ti}_{g}")
            t5 = self.dve(lambda e: e.tensor_tensor(st[:, 6:7], st[:, 4:5], st[:, 4:5], op=ALU.mult),
                          waits=(t3,), inc=True, tag=f"lnm2{ti}_{g}")
            t6 = self.dve(lambda e: e.tensor_tensor(st[:, 7:8], st[:, 5:6], st[:, 6:7], op=ALU.subtract),
                          waits=(t4, t5), inc=True, tag=f"lnvar{ti}_{g}")
            t7 = self.act(lambda e: e.activation(st[:, 8:9], st[:, 7:8], AF.Ln, bias=T["c_epsc"][:, 0:1]),
                          waits=(t6, prev.get("t10")), inc=True, tag=f"lnln{ti}_{g}")
            t8 = self.act(lambda e: e.activation(st[:, 9:10], st[:, 8:9], AF.Exp, scale=-0.5),
                          waits=(t7,), inc=True, tag=f"lnrs{ti}_{g}")
            t9 = self.dve(lambda e: e.tensor_tensor(st[:, 10:11], st[:, 4:5], st[:, 9:10], op=ALU.mult),
                          waits=(t8,), inc=True, tag=f"lnmrs{ti}_{g}")
            xng = T["xn_sb"][:, 0:HID]
            t10 = self.dve(lambda e, xg=xg, xng=xng: e.tensor_scalar(
                xng, xg, st[:, 9:10], st[:, 10:11], op0=ALU.mult, op1=ALU.subtract),
                waits=(t9, self.guard.get(f"xn{par}")), inc=True, tag=f"lnxn{ti}_{g}")
            prev = {"t3": t3, "t4": t4, "t10": t10}
            self.guard[f"x{par}"] = t10
            lastev = {}
            for kg in range(4):
                ptname = "pT0" if kg % 2 == 0 else "pT1"
                pt = T[ptname]
                ttp = None
                for i in range(4):
                    kt = kg * 4 + i
                    ttp = self.pe(lambda e, kt=kt, xng=xng, pt=pt, i=i: e.transpose(
                        pt[:, i * 128:(i + 1) * 128], xng[:, kt * 128:(kt + 1) * 128], T["c_id128b"][:, :]),
                        waits=(t10, self.tok_consts) + self.bank_wait(ptname), inc=(i == 3), tag=f"xtp{ti}_{g}_{kt}")
                self.bank_clear(ptname)
                eng = self.dve if kg % 2 == 0 else self.act
                ev = None
                for i in range(4):
                    kt = kg * 4 + i
                    ev = eng(lambda e, kt=kt, i=i, g=g, pt=pt: cp(e, 
                        T["xnT_sb"][:, kt * TILE + g * 128:kt * TILE + (g + 1) * 128],
                        pt[:, i * 128:(i + 1) * 128]),
                        waits=(ttp,) + tuple(self.guard.get("xnT") or ()), inc=(i == 3), tag=f"xev{ti}_{g}_{kt}")
                self.bank_use(ptname, ev)
                lastev[kg % 2] = ev
            self.guard[f"xn{par}"] = lastev[1]
        self._ln_prev = prev
        evtoks = (lastev[0], lastev[1])
        tok = None
        for kt in range(KT):
            tok = self.sdma(spill[kt * 128:(kt + 1) * 128, ti * TILE:(ti + 1) * TILE],
                            T["xnT_sb"][:, kt * TILE:(kt + 1) * TILE],
                            waits=evtoks, tag=f"xnTsp{ti}_{kt}")
        self.tok_xnT = evtoks
        self.tok_xnT_spill = tok
        return evtoks

    def halo_proj(self, hg, wtok):
        T = self.t
        jobs = [("wq_sb", i, i) for i in range(HPG)] + \
               [("wk_sb", l, 4 + l) for l in range(8)] + \
               [("wv_sb", l, 12 + l) for l in range(8)]
        self.tok_halo_pre = {}
        for n, (wname, strip, cht) in enumerate(jobs):
            bank = "pA" if n % 2 == 0 else "pB"
            pb = T[bank]
            tm = None
            for kt in range(KT):
                tm = self.pe(lambda e, wname=wname, strip=strip, kt=kt, pb=pb: e.matmul(
                    pb[:, 0:3],
                    T[wname][:, strip * KT * 128 + kt * 128:strip * KT * 128 + (kt + 1) * 128],
                    T["xnTh_sb"][:, kt * 3:(kt + 1) * 3],
                    start=(kt == 0), stop=(kt == KT - 1)),
                    waits=(wtok, self.tok_xnTh) + self.bank_wait(bank),
                    inc=(kt == KT - 1), tag=f"hproj{cht}_{kt}")
            self.bank_clear(bank)
            tv = self.dve(lambda e, cht=cht, pb=pb: cp(e, 
                T["pre"][:, cht * (TILE + 3):cht * (TILE + 3) + 3], pb[:, 0:3]),
                waits=(tm,), inc=True, tag=f"hpre{cht}")
            self.bank_use(bank, tv)
            self.tok_halo_pre[cht] = tv

    def proj_tile(self, phase, hg, ti, full):
        T = self.t
        txnT = self.tok_xnT
        PRE = TILE + 3
        jobs = [("wba_sb", 0, None)]
        for l in range(8):
            jobs.append(("wk_sb", l, 4 + l))
        for l in range(8):
            jobs.append(("wv_sb", l, 12 + l))
        if full:
            for i in range(HPG):
                jobs.append(("wq_sb", i, 0 + i))
        tok_evac = {}
        for n, (wname, strip, cht) in enumerate(jobs):
            bank = "pA" if n % 2 == 0 else "pB"
            pb = T[bank]
            ncols = 24 if cht is None else 128
            width = TILE
            tm = None
            nout = 24 if cht is None else 128
            for kt in range(KT):
                tm = self.pe(lambda e, wname=wname, strip=strip, kt=kt, pb=pb, ncols=ncols, nout=nout: e.matmul(
                    pb[0:nout, 0:width],
                    T[wname][:, strip * KT * ncols + kt * ncols:strip * KT * ncols + (kt + 1) * ncols],
                    T["xnT_sb"][:, kt * TILE:(kt + 1) * TILE],
                    start=(kt == 0), stop=(kt == KT - 1)),
                    waits=tuple(txnT) + (self.tok_wattn,) + self.bank_wait(bank),
                    inc=(kt == KT - 1), tag=f"proj{ti}_{n}_{kt}")
            self.bank_clear(bank)
            if cht is None:
                tcp = self.dve(lambda e, pb=pb: cp(e, T["bgtmp"][0:24, :], pb[0:24, 0:TILE]),
                               waits=(tm, self.guard.get("bgtmp")), inc=True, tag=f"bgev{ti}")
                self.bank_use(bank, tcp)
                self.emit_bg_panels(ti, tcp)
            else:
                if ti == 0:
                    if phase[0] == "R":
                        troll = self.dve(lambda e, cht=cht: e.memset(
                            T["pre"][:, cht * PRE:cht * PRE + 3], 0.0),
                            waits=(self.tok_conv_prev.get(cht),), inc=True, tag=f"roll0{cht}")
                    else:
                        troll = self.tok_halo_pre[cht]
                else:
                    troll = self.dve(lambda e, cht=cht: cp(e, 
                        T["pre"][:, cht * PRE:cht * PRE + 3],
                        T["halosave"][:, cht * 3:(cht + 1) * 3]),
                        waits=(self.tok_conv_prev.get(cht), self.guard.get(f"hs{cht}")),
                        inc=True, tag=f"roll{ti}_{cht}")
                eng = self.dve if n % 2 == 0 else self.act
                tev = eng(lambda e, cht=cht, pb=pb: cp(e, 
                    T["pre"][:, cht * PRE + 3:cht * PRE + PRE], pb[:, 0:TILE]),
                    waits=(tm, troll, self.guard.get("postall")), inc=True, tag=f"prev{ti}_{cht}")
                self.bank_use(bank, tev)
                tok_evac[cht] = (tev, troll)
        tok_post = {}
        conv_last = {}
        for cht in sorted(tok_evac):
            tev, troll = tok_evac[cht]
            eng = self.dve
            a_idx = (0, TILE)
            accA = T["cacc"][:, a_idx[0]:a_idx[0] + TILE]
            accB = T["cacc"][:, a_idx[1]:a_idx[1] + TILE]
            w = T["convw"]
            p0 = cht * PRE
            tc = eng(lambda e, p0=p0, cht=cht, accA=accA: e.tensor_scalar(
                accA, T["pre"][:, p0 + 0:p0 + 0 + TILE], w[:, cht * 4 + 0:cht * 4 + 1], None, op0=ALU.mult),
                waits=(tev, self.tok_wconv, self.guard.get(f"cacc{a_idx[0]}")), inc=True, tag=f"cv{ti}_{cht}_0")
            cur, oth = accB, accA
            for d in (1, 2, 3):
                tc = eng(lambda e, p0=p0, cht=cht, d=d, cur=cur, oth=oth: e.scalar_tensor_tensor(
                    cur, T["pre"][:, p0 + d:p0 + d + TILE], w[:, cht * 4 + d:cht * 4 + d + 1], oth,
                    op0=ALU.mult, op1=ALU.add),
                    waits=(tc, self.guard.get(f"cacc{a_idx[1] if d % 2 == 1 else a_idx[0]}"),),
                    inc=True, tag=f"cv{ti}_{cht}_{d}")
                cur, oth = oth, cur
            ths = self.dve(lambda e, cht=cht, p0=p0: cp(e, 
                T["halosave"][:, cht * 3:(cht + 1) * 3],
                T["pre"][:, p0 + TILE:p0 + TILE + 3]),
                waits=(tev, troll), inc=True, tag=f"hsave{ti}_{cht}")
            self.guard[f"hs{cht}"] = ths
            tpost = self.act(lambda e, cht=cht, p0=p0, oth=oth: e.activation(
                T["pre"][:, p0 + 3:p0 + PRE], oth, AF.Silu),
                waits=(tc, ths), inc=True, tag=f"silu{ti}_{cht}")
            self.guard[f"cacc{a_idx[0]}"] = tpost
            self.guard[f"cacc{a_idx[1]}"] = tpost
            conv_last[cht] = tc
            tok_post[cht] = tpost
        self.tok_conv_prev = conv_last
        self.tok_post = tok_post
        if full:
            for g in range(4):
                bank = "pA" if g % 2 == 0 else "pB"
                pb = T[bank]
                tm = None
                for kt in range(KT):
                    tm = self.pe(lambda e, g=g, kt=kt, pb=pb: e.matmul(
                        pb[:, 0:TILE],
                        T["xnT_sb"][:, kt * TILE + g * 128:kt * TILE + (g + 1) * 128],
                        T["wg_sb"][:, kt * 512:(kt + 1) * 512],
                        start=(kt == 0), stop=(kt == KT - 1)),
                        waits=tuple(txnT) + (self.tok_wattn,) + self.bank_wait(bank),
                        inc=(kt == KT - 1), tag=f"gate{ti}_{g}_{kt}")
                self.bank_clear(bank)
                tsl = self.act(lambda e, pb=pb: e.activation(
                    T["gstage"][:, 0:TILE], pb[:, 0:TILE], AF.Silu),
                    waits=(tm, self.guard.get("gst0")), inc=True, tag=f"gsilu{ti}_{g}")
                self.bank_use(bank, tsl)
                rows = ti * TILE + g * 128
                tdma = self.sdma(T["gate_sp"][rows:rows + 128, hg * 512:(hg + 1) * 512],
                                 T["gstage"][:, 0:TILE],
                                 waits=(tsl,), tag=f"gsp{ti}_{g}")
                self.guard["gst0"] = tdma
        self.guard["xnT"] = (tm,) if tm is not None else self.guard.get("xnT")
        return tok_post

    def emit_bg_panels(self, ti, tcp):
        T = self.t
        last = None
        for c in range(NCHUNK):
            tt = self.pe(lambda e, c=c: e.transpose(
                T["pT0"][0:64, 512:536], T["bgtmp"][0:24, c * 64:(c + 1) * 64], T["c_id128b"][0:24, 0:24]),
                waits=(tcp,) + self.bank_wait("pT0"), inc=True, tag=f"bgtp{ti}_{c}")
            self.bank_clear("pT0")
            tev = self.dve(lambda e, c=c: cp(e, 
                T["bg_tm"][0:64, c * 24:(c + 1) * 24], T["pT0"][0:64, 512:536]),
                waits=(tt, self.guard.get("bg_tm")), inc=True, tag=f"bgtm{ti}_{c}")
            self.bank_use("pT0", tev)
            last = tev
        self.guard["bgtmp"] = last
        bg3 = T["bg_tm"][0:64, :].rearrange("p (c w) -> p c w", c=NCHUNK)
        t1 = self.act(lambda e: e.activation(
            T["bgt2"][0:64, :].rearrange("p (c w) -> p c w", c=NCHUNK), bg3[:, :, 0:16], AF.Tanh, scale=0.5),
            waits=(last, self.guard.get("bgt2")), inc=True, tag=f"btanh{ti}")
        t2 = self.dve(lambda e: e.tensor_scalar(
            T["bgt3"][0:64, :], T["bgt2"][0:64, :], -1.0, None, op0=ALU.max),
            waits=(t1, self.guard.get("bgt3")), inc=True, tag=f"bmax{ti}")
        t3 = self.act(lambda e: e.activation(
            T["lnb_tm"][0:64, :], T["bgt3"][0:64, :], AF.Ln, bias=T["c_epsc"][0:64, 2:3]),
            waits=(t2, self.guard.get("lnb_tm")), inc=True, tag=f"blnb{ti}")
        self.tok_lnb = t3
        g1 = self.dve(lambda e: e.tensor_tensor(
            T["gsp_tm"][0:64, :].rearrange("p (c w) -> p c w", c=NCHUNK), bg3[:, :, 16:24],
            T["c_dtb64"][0:64, :].rearrange("p (c w) -> p c w", c=NCHUNK), op=ALU.add),
            waits=(last, self.tok_dtb64, self.guard.get("gsp_tm")), inc=True, tag=f"gy{ti}")
        g2a = self.dve(lambda e: e.tensor_scalar(
            T["bgt3"][0:64, 64:128], T["gsp_tm"][0:64, :], -1.0, None, op0=ALU.mult),
            waits=(g1, t3), inc=True, tag=f"gneg{ti}")
        g2 = self.dve(lambda e: e.tensor_tensor(
            T["bgt2"][0:64, 0:64], T["gsp_tm"][0:64, :], T["bgt3"][0:64, 64:128], op=ALU.max),
            waits=(g2a,), inc=True, tag=f"gabs{ti}")
        g3 = self.act(lambda e: e.activation(T["bgt2"][0:64, 64:128], T["bgt2"][0:64, 0:64], AF.Exp, scale=-1.0),
                      waits=(g2,), inc=True, tag=f"gexp{ti}")
        g4 = self.act(lambda e: e.activation(T["bgt3"][0:64, 0:64], T["bgt2"][0:64, 64:128], AF.Ln,
                                             bias=T["c_epsc"][0:64, 2:3]),
                      waits=(g3, t2), inc=True, tag=f"gln{ti}")
        g5 = self.dve(lambda e: e.tensor_scalar(T["bgt3"][0:64, 64:128], T["gsp_tm"][0:64, :], 0.0, None,
                                                op0=ALU.max),
                      waits=(g1, t2), inc=True, tag=f"gmax{ti}")
        g6 = self.dve(lambda e: e.tensor_tensor(T["bgt2"][0:64, 0:64], T["bgt3"][0:64, 0:64],
                                                T["bgt3"][0:64, 64:128], op=ALU.add),
                      waits=(g4, g5), inc=True, tag=f"gsp{ti}")
        g7 = self.dve(lambda e: e.tensor_tensor(T["gsp_tm"][0:64, :], T["bgt2"][0:64, 0:64],
                                                T["c_negA64"][0:64, :], op=ALU.mult),
                      waits=(g6, self.tok_negA64), inc=True, tag=f"gfin{ti}")
        self.tok_gsp = g7
        self.guard["bg_tm"] = g7
        self.guard["bgt2"] = g7
        self.guard["bgt3"] = g6

    def chunk_front(self, phase, hg, ti, c, full):
        T = self.t
        PRE = TILE + 3
        c64 = c * CT
        par = c % 2
        kj = [[T["pre"][:, (4 + j * 4 + hl) * PRE + 3 + c64:(4 + j * 4 + hl) * PRE + 3 + c64 + CT]
               for j in range(2)] for hl in range(HPG)]
        qh = [T["pre"][:, hl * PRE + 3 + c64:hl * PRE + 3 + c64 + CT] for hl in range(HPG)]
        tk = [[self.tok_post[4 + j * 4 + hl] for j in range(2)] for hl in range(HPG)]
        tv = [[self.tok_post[12 + j * 4 + hl] for j in range(2)] for hl in range(HPG)]
        tq = [self.tok_post[hl] if full else None for hl in range(HPG)]
        vj = [[T["pre"][:, (12 + j * 4 + hl) * PRE + 3 + c64:(12 + j * 4 + hl) * PRE + 3 + c64 + CT]
               for j in range(2)] for hl in range(HPG)]
        p1 = T["pan1"]
        p2 = T["pan2"]
        n0N = "nN_a" if par == 0 else "nN_b"
        n0T = "nT_a" if par == 0 else "nT_b"

        tg1 = self.pe(lambda e: e.matmul(T["pA"][0:64, 0:4], T["c_trilT"][:, :],
                                         T["gsp_tm"][0:64, c * 8 + hg * 4:c * 8 + hg * 4 + 4],
                                         start=True, stop=True),
                      waits=(self.tok_gsp, self.tok_consts) + self.bank_wait("pA"), inc=True, tag=f"gcum{ti}_{c}")
        self.bank_clear("pA")
        tg2 = self.dve(lambda e: cp(e, T["gcum_sb"][0:64, 0:4], T["pA"][0:64, 0:4]),
                       waits=(tg1, self.guard.get("gcum")), inc=True, tag=f"gcumev{ti}_{c}")
        self.bank_use("pA", tg2)
        tg3 = self.pe(lambda e: e.transpose(T["pA"][0:4, 4:68], T["gcum_sb"][0:64, 0:4], T["c_id64f"][:, :]),
                      waits=(tg2,) + self.bank_wait("pA"), inc=True, tag=f"gcumT{ti}_{c}")
        self.bank_clear("pA")
        tg4 = self.dve(lambda e: cp(e, T["gcumT_sb"][0:4, :], T["pA"][0:4, 4:68]),
                       waits=(tg3,), inc=True, tag=f"gcumTev{ti}_{c}")
        self.bank_use("pA", tg4)
        tg5 = self.pe(lambda e: e.transpose(T["pA"][0:1, 68:72], T["gcumT_sb"][0:4, 63:64], T["c_id4f"][:, :]),
                      waits=(tg4,) + self.bank_wait("pA"), inc=True, tag=f"gcl1{ti}_{c}")
        self.bank_clear("pA")
        tg6 = self.act(lambda e: e.copy(T["trow"][0:1, 0:4], T["pA"][0:1, 68:72]),
                       waits=(tg5, self.guard.get("trow4")), inc=True, tag=f"gcl2{ti}_{c}")
        self.bank_use("pA", tg6)
        tg7 = self.pe(lambda e: e.matmul(T["pA"][:, 72:76], T["c_ones1f"][:, :], T["trow"][0:1, 0:4],
                                         start=True, stop=True),
                      waits=(tg6,) + self.bank_wait("pA"), inc=True, tag=f"gcl3{ti}_{c}")
        self.bank_clear("pA")
        tg8 = self.dve(lambda e, par=par: cp(e, T["gCl_big"][:, 0:4], T["pA"][:, 72:76]),
                       waits=(tg7, self.guard.get("gClbig")), inc=True, tag=f"gcl4{ti}_{c}")
        self.bank_use("pA", tg8)
        tg9 = self.act(lambda e, par=par: e.activation(T["Lam_big"][:, par * 4:par * 4 + 4],
                                                       T["gCl_big"][:, 0:4], AF.Exp),
                       waits=(tg8, self.guard.get(f"Lambig{par}")), inc=True, tag=f"lam{ti}_{c}")
        self._t_lam = tg9
        tgA = self.dve(lambda e: cp(e, T["gce"][0:64, 0:4], T["gcum_sb"][0:64, 0:4]),
                       waits=(tg2, self.guard.get("gce")), inc=True, tag=f"gce0{ti}_{c}")
        tgB = self.dve(lambda e: cp(e, T["gce"][64:128, 0:4], T["gcum_sb"][0:64, 0:4]),
                       waits=(tg2,), inc=True, tag=f"gce1{ti}_{c}")
        tok_lnbe = None
        for hl in range(HPG):
            c0 = c * 16 + hg * 4 + hl
            c1 = c * 16 + 8 + hg * 4 + hl
            self.dve(lambda e, hl=hl, c0=c0: cp(e, p1[0:64, 16 + hl:17 + hl], T["lnb_tm"][0:64, c0:c0 + 1]),
                     waits=(self.tok_lnb, self.guard.get("pan1")), inc=False, tag=f"lnbe0{ti}_{c}_{hl}")
            tok_lnbe = self.dve(lambda e, hl=hl, c1=c1: cp(e, p1[64:128, 16 + hl:17 + hl],
                                                           T["lnb_tm"][0:64, c1:c1 + 1]),
                                waits=(self.tok_lnb,), inc=(hl == HPG - 1), tag=f"lnbe1{ti}_{c}_{hl}")

        w_t1 = self.bank_wait("pT1")
        t_tp = None
        for hl in range(HPG):
            for j in range(2):
                t_tp = self.pe(lambda e, hl=hl, j=j: e.transpose(
                    T["pT1"][j * 64:(j + 1) * 64, hl * 128:(hl + 1) * 128], kj[hl][j], T["c_id128b"][:, :]),
                    waits=(tk[hl][j],) + w_t1, inc=False, tag=f"tk{c}_{hl}_{j}")
        if full:
            for hl in range(HPG):
                t_tp = self.pe(lambda e, hl=hl: e.transpose(
                    T["pT1"][0:64, 512 + hl * 128:512 + (hl + 1) * 128], qh[hl], T["c_id128b"][:, :]),
                    waits=(tq[hl],), inc=False, tag=f"tq{c}_{hl}")
        self.bank_clear("pT1")
        w_t0 = self.bank_wait("pT0")
        t_tv = None
        for hl in range(HPG):
            for j in range(2):
                t_tv = self.pe(lambda e, hl=hl, j=j: e.transpose(
                    T["pT0"][j * 64:(j + 1) * 64, hl * 128:(hl + 1) * 128], vj[hl][j], T["c_id128b"][:, :]),
                    waits=(tv[hl][j],) + w_t0, inc=(hl == HPG - 1 and j == 1), tag=f"tv{c}_{hl}_{j}")
        self.bank_clear("pT0")
        t_tp = self.pe(lambda e: e.nop(), waits=(), inc=True, tag=f"tpfin{c}")

        ksq = T["dump"][:, 1024:1536]
        qsq = T["dump"][0:64, 1536:2048]
        t_ksq = self.act(lambda e, ksq=ksq: e.activation(ksq, T["pT1"][:, 0:512], AF.Square),
                         waits=(t_tp, self.guard.get("ksq")), inc=True, tag=f"ksq{c}")
        t_kss = self.dve(lambda e, ksq=ksq: e.tensor_reduce(T["kss"][:, 0:4],
                                                            ksq.rearrange("p (h f) -> p h f", h=4),
                                                            axis=mybir.AxisListType.X, op=ALU.add),
                         waits=(t_ksq, self.guard.get("kss")), inc=True, tag=f"kss{c}")
        self.guard["ksq"] = t_kss
        if full:
            t_qsq = self.act(lambda e, qsq=qsq: e.activation(qsq, T["pT1"][0:64, 512:1024], AF.Square),
                             waits=(t_tp, self.guard.get("qsq")), inc=True, tag=f"qsq{c}")
            t_qss = self.dve(lambda e, qsq=qsq: e.tensor_reduce(T["qss"][0:64, 0:4],
                                                                qsq.rearrange("p (h f) -> p h f", h=4),
                                                                axis=mybir.AxisListType.X, op=ALU.add),
                             waits=(t_qsq, self.guard.get("qss")), inc=True, tag=f"qss{c}")
            self.guard["qsq"] = t_qss
            self.bank_use("pT1", t_qsq)
        t_vev = self.dve(lambda e, par=par: cp(e, T["vtm_sb"][:, par * 512:par * 512 + 512],
                                               T["pT0"][:, 0:512]),
                         waits=(t_tv, self.guard.get(f"vtm{par}")), inc=True, tag=f"vev{c}")
        self.bank_use("pT0", t_vev)
        self.bank_use("pT1", t_ksq)
        t_lnk = self.act(lambda e: e.activation(p1[:, 8:12], T["kss"][:, 0:4], AF.Ln,
                                                bias=T["c_epsc"][:, 1:2]),
                         waits=(t_kss, self.guard.get("p1_8")), inc=True, tag=f"lnk{c}")
        self.guard["kss"] = t_lnk
        t_lnrn = self.dve(lambda e: e.tensor_scalar(p1[:, 20:24], p1[:, 8:12], -0.5, None, op0=ALU.mult),
                          waits=(t_lnk,), inc=True, tag=f"lnrn{c}")
        t_a = self.dve(lambda e: e.tensor_tensor(p1[:, 4:8], T["gce"][:, 0:4], p1[:, 20:24], op=ALU.add),
                       waits=(t_lnrn, tgA, tgB, self.guard.get("pan1")), inc=True, tag=f"a{c}")
        t_nb1 = self.dve(lambda e: e.tensor_tensor(p1[:, 8:12], p1[:, 20:24], p1[:, 16:20], op=ALU.add),
                         waits=(t_lnrn, tok_lnbe), inc=True, tag=f"nb1{c}")
        t_nb = self.dve(lambda e: e.tensor_tensor(p1[:, 12:16], p1[:, 8:12], T["gce"][:, 0:4], op=ALU.subtract),
                        waits=(t_nb1, self.guard.get("p1_12")), inc=True, tag=f"nb{c}")
        t_kpe = self.dve(lambda e: e.tensor_tensor(p1[:, 0:4], p1[:, 12:16], T["gCl_big"][:, 0:4], op=ALU.add),
                         waits=(t_nb, tg8, self.guard.get("pan1")), inc=True, tag=f"kpe{c}")
        t_exp = self.act(lambda e, par=par: e.activation(T["panx"][:, par * 8:par * 8 + 8], p1[:, 0:8], AF.Exp),
                         waits=(t_kpe, t_a, self.guard.get(f"panx{par}")), inc=True, tag=f"exp{c}")
        self._t_exp = t_exp
        if full:
            t_lnq = self.act(lambda e: e.activation(p2[0:64, 4:8], T["qss"][0:64, 0:4], AF.Ln,
                                                    bias=T["c_epsc"][0:64, 1:2]),
                             waits=(t_qss, self.guard.get("p2_8")), inc=True, tag=f"lnq{c}")
            t_aq1 = self.dve(lambda e: e.tensor_scalar(p2[0:64, 8:12], p2[0:64, 4:8], -0.5, -0.5 * LN128,
                                                       op0=ALU.mult, op1=ALU.add),
                             waits=(t_lnq,), inc=True, tag=f"aq1{c}")
            t_aq = self.dve(lambda e: e.tensor_tensor(p2[0:64, 0:4], p2[0:64, 8:12],
                                                      T["gcum_sb"][0:64, 0:4], op=ALU.add),
                            waits=(t_aq1, tg2, self.guard.get("p2_04")), inc=True, tag=f"aq{c}")
            t_lq = self.act(lambda e, par=par: e.activation(p2[0:64, par * 16 + 12:par * 16 + 16],
                                                            p2[0:64, 0:4], AF.Exp),
                            waits=(t_aq, self.guard.get(f"lamq{par}")), inc=True, tag=f"lamq{c}")
            self._t_lq = t_lq
            self.guard["p2_8"] = t_aq
            self.guard["p2_04"] = t_lq
        t_k2 = None
        for hl in range(HPG):
            t_k2 = self.dve(lambda e, hl=hl, par=par: e.tensor_scalar(
                T["ktm_sb"][:, par * 512 + hl * 128:par * 512 + (hl + 1) * 128],
                T["pT1"][:, hl * 128:(hl + 1) * 128],
                T["panx"][:, par * 8 + hl:par * 8 + hl + 1], -1.0, op0=ALU.mult, op1=ALU.mult),
                waits=(t_exp, t_tp, self.guard.get(f"ktm{par}")), inc=(hl == HPG - 1), tag=f"k2{c}_{hl}")
        self.bank_use("pT1", t_k2)
        self._t_k2 = t_k2

        w_e = self.bank_wait("pE")
        t_g = None
        for hl in range(HPG):
            for a in range(2):
                for b in range(2):
                    t_g = self.pe(lambda e, hl=hl, a=a, b=b: e.matmul(
                        T["pE"][a * 64:(a + 1) * 64, hl * 128 + b * 64:hl * 128 + (b + 1) * 64],
                        kj[hl][a], kj[hl][b], start=True, stop=True),
                        waits=(tk[hl][0], tk[hl][1]) + w_e,
                        inc=(hl == HPG - 1 and a == 1 and b == 1), tag=f"g{c}_{hl}_{a}{b}")
        self.bank_clear("pE")
        w_f = self.bank_wait("pF")
        t_bt = None
        for hl in range(HPG):
            t_bt = self.pe(lambda e, hl=hl: e.transpose(
                T["pA"][0:1, hl * 128:(hl + 1) * 128], p1[:, 12 + hl:13 + hl], T["c_id128f"][:, :]),
                waits=(t_nb,) + self.bank_wait("pA"), inc=(hl == HPG - 1), tag=f"bt{c}_{hl}")
        self.bank_clear("pA")
        t_br = self.act(lambda e: e.copy(T["c_rows"][0:1, 256:768], T["pA"][0:1, 0:512]),
                        waits=(t_bt, self.guard.get("negbT")), inc=True, tag=f"br{c}")
        self.bank_use("pA", t_br)
        t_bc = None
        for hl in range(HPG):
            t_bc = self.pe(lambda e, hl=hl: e.matmul(
                T["pF"][:, hl * 128:(hl + 1) * 128], T["c_ones1f"][:, :],
                T["c_rows"][0:1, 256 + hl * 128:256 + (hl + 1) * 128], start=True, stop=True),
                waits=(t_br,) + w_f, inc=(hl == HPG - 1), tag=f"bc{c}_{hl}")
        self.bank_clear("pF")
        self.guard["negbT"] = t_bc
        self.guard["p1_12"] = t_bt

        t_D = None
        for hl in range(HPG):
            t_D = self.dve(lambda e, hl=hl: e.tensor_scalar(
                T["D_sb"][:, hl * 128:(hl + 1) * 128], T["pF"][:, hl * 128:(hl + 1) * 128],
                p1[:, 4 + hl:5 + hl], 60.0, op0=ALU.add, op1=ALU.min),
                waits=(t_bc, t_a, self.guard.get("Dsb")), inc=(hl == HPG - 1), tag=f"D{c}_{hl}")
        self.bank_use("pF", t_D)
        t_E = self.act(lambda e: e.activation(T["dump"][:, 1024:1536], T["D_sb"][:, 0:512], AF.Exp),
                       waits=(t_D, t_kss), inc=True, tag=f"E{c}")
        self.guard["Dsb"] = t_E
        t_Em = self.dve(lambda e: e.tensor_tensor(T["dump"][:, 0:512], T["dump"][:, 1024:1536],
                                                  T["c_maskM"][:, :], op=ALU.mult),
                        waits=(t_E, self.guard.get("dumpA")), inc=True, tag=f"Em{c}")
        t_N0 = self.dve(lambda e, n0N=n0N: e.tensor_tensor(T[n0N][:, 0:512], T["dump"][:, 0:512],
                                                           T["pE"][:, 0:512], op=ALU.mult),
                        waits=(t_Em, t_g, self.guard.get(f"n0N{par}")), inc=True, tag=f"N0{c}")
        self.bank_use("pE", t_N0)
        self.guard["ksq"] = t_Em
        self.guard["dumpA"] = t_N0
        t_nt0 = None
        for hl in range(HPG):
            t_nt0 = self.pe(lambda e, hl=hl, n0N=n0N: e.transpose(
                T["pT0"][:, 512 + hl * 128:512 + (hl + 1) * 128],
                T[n0N][:, hl * 128:(hl + 1) * 128], T["c_id128b"][:, :]),
                waits=(t_N0,) + self.bank_wait("pT0"), inc=(hl == HPG - 1), tag=f"nt0{c}_{hl}")
        self.bank_clear("pT0")
        t_nt0e = self.act(lambda e, n0T=n0T: e.copy(T[n0T][:, 0:512], T["pT0"][:, 512:1024]),
                          waits=(t_nt0, self.guard.get(f"n0T{par}")), inc=True, tag=f"nt0e{c}")
        self.bank_use("pT0", t_nt0e)
        self._t_lvl0 = (t_nt0e, t_N0)

        if full:
            w_f2 = self.bank_wait("pF")
            t_ar = None
            for hl in range(HPG):
                for j in range(2):
                    t_ar = self.pe(lambda e, hl=hl, j=j: e.matmul(
                        T["pF"][j * 64:(j + 1) * 64, hl * 64:(hl + 1) * 64], kj[hl][j], qh[hl],
                        start=True, stop=True),
                        waits=(tk[hl][j], tq[hl]) + w_f2,
                        inc=(hl == HPG - 1 and j == 1), tag=f"ar{c}_{hl}_{j}")
            t_at = None
            for hl in range(HPG):
                t_at = self.pe(lambda e, hl=hl: e.transpose(
                    T["pA"][0:1, hl * 64:(hl + 1) * 64], p2[0:64, hl:hl + 1], T["c_id64f"][:, :]),
                    waits=(t_aq,) + self.bank_wait("pA"), inc=(hl == HPG - 1), tag=f"at{c}_{hl}")
            self.bank_clear("pA")
            t_arr = self.act(lambda e: e.copy(T["trow"][0:1, 0:256], T["pA"][0:1, 0:256]),
                             waits=(t_at, tg7, self.guard.get("trow4")), inc=True, tag=f"arr{c}")
            self.bank_use("pA", t_arr)
            t_ab = None
            for hl in range(HPG):
                t_ab = self.pe(lambda e, hl=hl: e.matmul(
                    T["pF"][:, 256 + hl * 64:256 + (hl + 1) * 64], T["c_ones1f"][:, :],
                    T["trow"][0:1, hl * 64:(hl + 1) * 64], start=True, stop=True),
                    waits=(t_arr,), inc=(hl == HPG - 1), tag=f"ab{c}_{hl}")
            self.bank_clear("pF")
            self.guard["trow4"] = t_ab
            self.guard["p2_04"] = (t_lq, t_at)
            t_D3 = None
            for hl in range(HPG):
                t_D3 = self.dve(lambda e, hl=hl: e.tensor_scalar(
                    T["D_sb"][:, hl * 64:(hl + 1) * 64], T["pF"][:, 256 + hl * 64:256 + (hl + 1) * 64],
                    p1[:, 12 + hl:13 + hl], 60.0, op0=ALU.add, op1=ALU.min),
                    waits=(t_ab, t_nb, t_E), inc=(hl == HPG - 1), tag=f"D3{c}_{hl}")
            self.bank_use("pF", t_D3)
            t_Eq = self.act(lambda e: e.activation(T["Eq_sb"][:, 0:256], T["D_sb"][:, 0:256], AF.Exp),
                            waits=(t_D3, self.guard.get("Eq")), inc=True, tag=f"Eq{c}")
            self.guard["Dsb"] = t_Eq
            t_Eqm = self.dve(lambda e: e.tensor_tensor(T["dump"][:, 512:768], T["Eq_sb"][:, 0:256],
                                                       T["c_maskAT"][:, :], op=ALU.mult),
                             waits=(t_Eq, self.guard.get("dumpB")), inc=True, tag=f"Eqm{c}")
            t_am = self.dve(lambda e, par=par: e.tensor_tensor(
                T["am_sb"][:, par * 256:par * 256 + 256], T["dump"][:, 512:768],
                T["pF"][:, 0:256], op=ALU.mult),
                waits=(t_Eqm, t_ar, self.guard.get(f"am{par}")), inc=True, tag=f"am{c}")
            self.bank_use("pF", t_am)
            self.guard["Eq"] = t_am
            self.guard["dumpB"] = t_am
            self._t_am = t_am
        self.guard["p1_8"] = t_nb
        self.guard["pan1"] = t_exp
        self.guard["gcum"] = (tg3, tgA, tgB) if not full else (tg3, tgA, tgB, t_aq)
        self.guard["gce"] = (t_a, t_kpe)
        self.guard["gClbig"] = (tg9, t_kpe)
        self._ck[c] = dict(t_exp=t_exp, t_lvl0=self._t_lvl0, t_vev=t_vev, t_k2=t_k2,
                           t_lam=self._t_lam,
                           t_am=(self._t_am if full else None),
                           t_lq=(self._t_lq if full else None),
                           t_lnbe=tok_lnbe, tg1=tg1)

    def chunk_back(self, phase, hg, ti, c, full):
        T = self.t
        PRE = TILE + 3
        c64 = c * CT
        par = c % 2
        kj = [[T["pre"][:, (4 + j * 4 + hl) * PRE + 3 + c64:(4 + j * 4 + hl) * PRE + 3 + c64 + CT]
               for j in range(2)] for hl in range(HPG)]
        qh = [T["pre"][:, hl * PRE + 3 + c64:hl * PRE + 3 + c64 + CT] for hl in range(HPG)]
        tk = [[self.tok_post[4 + j * 4 + hl] for j in range(2)] for hl in range(HPG)]
        tq = [self.tok_post[hl] if full else None for hl in range(HPG)]
        heads = [hg * HPG + hl for hl in range(HPG)]
        ping = self.s_ping[heads[0]]
        S0 = [T["s_bf"][:, (h * 2 + ping) * 128:(h * 2 + ping) * 128 + 128] for h in heads]
        Snx = [T["s_bf"][:, (h * 2 + 1 - ping) * 128:(h * 2 + 1 - ping) * 128 + 128] for h in heads]
        tok_s_in = tuple(self.tok_s.get(h) for h in heads)
        p2 = T["pan2"]
        n0N = "nN_a" if par == 0 else "nN_b"
        n0T = "nT_a" if par == 0 else "nT_b"
        t_exp = self._ck[c]["t_exp"]
        t_lvl0 = self._ck[c]["t_lvl0"]
        t_vev = self._ck[c]["t_vev"]
        t_k2 = self._ck[c]["t_k2"]
        t_lam = self._ck[c]["t_lam"]

        w_x = self.bank_wait("pD")
        t_x = None
        for hl in range(HPG):
            for j in range(2):
                t_x = self.pe(lambda e, hl=hl, j=j: e.matmul(
                    T["pD"][j * 64:(j + 1) * 64, hl * 128:(hl + 1) * 128], kj[hl][j], S0[hl],
                    start=True, stop=True),
                    waits=(tk[hl][j], tok_s_in[hl]) + w_x,
                    inc=(hl == HPG - 1 and j == 1), tag=f"x{c}_{hl}_{j}")
        self.bank_clear("pD")
        t_u0 = None
        for hl in range(HPG):
            t_u0 = self.dve(lambda e, hl=hl, par=par: e.scalar_tensor_tensor(
                T["ucp"][:, hl * 128:(hl + 1) * 128], T["pD"][:, hl * 128:(hl + 1) * 128],
                T["panx"][:, par * 8 + 4 + hl:par * 8 + 5 + hl],
                T["vtm_sb"][:, par * 512 + hl * 128:par * 512 + (hl + 1) * 128],
                op0=ALU.mult, op1=ALU.subtract),
                waits=(t_x, t_exp, t_vev, self.guard.get("ucp0")), inc=(hl == HPG - 1), tag=f"u0{c}_{hl}")
        self.bank_use("pD", t_u0)

        nLN = {0: "nLN_a", 1: "nLN_b"}
        nLT = {0: "nLT_a", 1: "nLT_b"}
        def lvlbuf(k):
            if k == 0:
                return n0N, n0T
            return nLN[(k + 1) % 2], nLT[(k + 1) % 2]
        t_lvl = {0: t_lvl0}
        t_ucur = t_u0
        t_ap_hist = {}
        for k in range(7):
            kpar = k % 2
            knpar = (k + 1) % 2
            ucur0 = kpar * 512
            unext0 = knpar * 512
            bN, bT = lvlbuf(k)
            w_u = self.bank_wait("pC")
            t_mm = None
            for hl in range(HPG):
                t_mm = self.pe(lambda e, hl=hl, bT=bT, ucur0=ucur0: e.matmul(
                    T["pC"][:, hl * 128:(hl + 1) * 128],
                    T[bT][:, hl * 128:(hl + 1) * 128],
                    T["ucp"][:, ucur0 + hl * 128:ucur0 + (hl + 1) * 128],
                    start=True, stop=True),
                    waits=(t_ucur,) + t_lvl[k] + w_u, inc=(hl == HPG - 1), tag=f"ap{c}_{hl}_{k}")
            self.bank_clear("pC")
            t_ap_hist[k] = t_mm
            t_ucur = self.dve(lambda e, ucur0=ucur0, unext0=unext0: e.tensor_tensor(
                T["ucp"][:, unext0:unext0 + 512], T["ucp"][:, ucur0:ucur0 + 512],
                T["pC"][:, 0:512], op=ALU.add),
                waits=(t_mm, self.guard.get(f"ucp{knpar}")), inc=True, tag=f"uacc{c}_{k}")
            self.bank_use("pC", t_ucur)
            self.guard[f"ucp{kpar}"] = t_ucur
            if k < 6:
                dN, dT = lvlbuf(k + 1)
                w_d = self.bank_wait("pD")
                w_b = self.bank_wait("pB")
                dst_guard = (t_mm, self.guard.get("nL") if k > 0 else None)
                t_s1 = t_s2 = None
                if k < 5:
                    for hl in range(HPG):
                        t_s1 = self.pe(lambda e, hl=hl, bN=bN, bT=bT: e.matmul(
                            T["pD"][:, hl * 128:(hl + 1) * 128],
                            T[bT][:, hl * 128:(hl + 1) * 128],
                            T[bN][:, hl * 128:(hl + 1) * 128], start=True, stop=True),
                            waits=t_lvl[k] + w_d, inc=(hl == HPG - 1), tag=f"sqN{c}_{hl}_{k}")
                    for hl in range(HPG):
                        t_s2 = self.pe(lambda e, hl=hl, bN=bN, bT=bT: e.matmul(
                            T["pB"][:, hl * 128:(hl + 1) * 128],
                            T[bN][:, hl * 128:(hl + 1) * 128],
                            T[bT][:, hl * 128:(hl + 1) * 128], start=True, stop=True),
                            waits=t_lvl[k] + w_b, inc=(hl == HPG - 1), tag=f"sqT{c}_{hl}_{k}")
                    self.bank_clear("pD")
                    self.bank_clear("pB")
                    t_evN = self.dve(lambda e, dN=dN: cp(e, T[dN][:, 0:512], T["pD"][:, 0:512]),
                                     waits=(t_s1,) + dst_guard, inc=True, tag=f"sqeN{c}_{k}")
                    self.bank_use("pD", t_evN)
                    t_evT = self.act(lambda e, dT=dT: e.copy(T[dT][:, 0:512], T["pB"][:, 0:512]),
                                     waits=(t_s2,) + dst_guard, inc=True, tag=f"sqeT{c}_{k}")
                    self.bank_use("pB", t_evT)
                    t_lvl[k + 1] = (t_evN, t_evT)
                else:
                    for hl in range(HPG):
                        t_s1 = self.pe(lambda e, hl=hl, bN=bN, bT=bT: e.matmul(
                            T["pD"][:, hl * 128:(hl + 1) * 128],
                            T[bN][:, hl * 128:(hl + 1) * 128],
                            T[bT][:, hl * 128:(hl + 1) * 128], start=True, stop=True),
                            waits=t_lvl[k] + w_d, inc=(hl == HPG - 1), tag=f"sqT6{c}_{hl}")
                    self.bank_clear("pD")
                    t_ev6 = self.dve(lambda e, dT=dT: cp(e, T[dT][:, 0:512], T["pD"][:, 0:512]),
                                     waits=(t_s1,) + dst_guard, inc=True, tag=f"sqe6{c}")
                    self.bank_use("pD", t_ev6)
                    t_lvl[6] = (t_ev6,)
                self.guard["nL"] = t_mm
        t_uf = t_ucur

        if full:
            t_am = self._ck[c]["t_am"]
            t_lq = self._ck[c]["t_lq"]
            w_d2 = self.bank_wait("pD")
            t_o1 = None
            for hl in range(HPG):
                t_o1 = self.pe(lambda e, hl=hl: e.matmul(
                    T["pD"][0:64, hl * 128:(hl + 1) * 128], qh[hl], S0[hl], start=True, stop=True),
                    waits=(tq[hl], tok_s_in[hl]) + w_d2, inc=(hl == HPG - 1), tag=f"o1{c}_{hl}")
            self.bank_clear("pD")
            w_c2 = self.bank_wait("pC")
            t_o2 = None
            for hl in range(HPG):
                t_o2 = self.pe(lambda e, hl=hl, par=par: e.matmul(
                    T["pC"][0:64, hl * 128:(hl + 1) * 128],
                    T["am_sb"][:, par * 256 + hl * 64:par * 256 + (hl + 1) * 64],
                    T["ucp"][:, 512 + hl * 128:512 + (hl + 1) * 128], start=True, stop=True),
                    waits=(t_am, t_uf) + w_c2, inc=(hl == HPG - 1), tag=f"o2{c}_{hl}")
            self.bank_clear("pC")
            t_ot1 = None
            for hl in range(HPG):
                t_ot1 = self.dve(lambda e, hl=hl, par=par: e.tensor_scalar(
                    T["ot1w"][0:64, hl * 128:(hl + 1) * 128], T["pD"][0:64, hl * 128:(hl + 1) * 128],
                    p2[0:64, par * 16 + 12 + hl:par * 16 + 13 + hl], None, op0=ALU.mult),
                    waits=(t_o1, t_lq, self.guard.get("ot1w")), inc=(hl == HPG - 1), tag=f"ot1{c}_{hl}")
            self.bank_use("pD", t_ot1)
            orow = (c % 2) * 64
            ocol = ((c // 2) % 2) * 512
            t_o = self.dve(lambda e, orow=orow, ocol=ocol: e.tensor_tensor(
                T["otile"][orow:orow + 64, ocol:ocol + 512],
                T["ot1w"][0:64, 0:512], T["pC"][0:64, 0:512], op=ALU.add),
                waits=(t_ot1, t_o2, self.guard.get(f"otile{(c // 2) % 2}")), inc=True, tag=f"o{c}")
            self.bank_use("pC", t_o)
            self.guard["ot1w"] = t_o
            self.tok_otile[c] = t_o
            self.guard[f"lamq{par}"] = t_o

        w_s = self.bank_wait("pB")
        t_sn = None
        for hl in range(HPG):
            t_sn = self.pe(lambda e, hl=hl, par=par: e.matmul(
                T["pB"][:, hl * 128:(hl + 1) * 128],
                T["ktm_sb"][:, par * 512 + hl * 128:par * 512 + (hl + 1) * 128],
                T["ucp"][:, 512 + hl * 128:512 + (hl + 1) * 128], start=True, stop=True),
                waits=(t_k2, t_uf) + w_s, inc=(hl == HPG - 1), tag=f"sn{c}_{hl}")
        self.bank_clear("pB")
        t_s = None
        for hl in range(HPG):
            t_s = self.dve(lambda e, hl=hl, par=par: e.scalar_tensor_tensor(
                Snx[hl], S0[hl], T["Lam_big"][:, par * 4 + hl:par * 4 + hl + 1],
                T["pB"][:, hl * 128:(hl + 1) * 128], op0=ALU.mult, op1=ALU.add),
                waits=(t_sn, t_lam), inc=True, tag=f"s{c}_{hl}")
            self.tok_s[heads[hl]] = t_s
            self.s_ping[heads[hl]] = 1 - self.s_ping[heads[hl]]
        self.bank_use("pB", t_s)
        self.guard[f"n0N{par}"] = t_lvl[1]
        self.guard[f"n0T{par}"] = (t_ap_hist[0],) + t_lvl[1]
        self.guard["ucp0"] = t_uf
        self.guard["ucp1"] = (t_sn, t_o2) if full else t_sn
        self.guard[f"vtm{par}"] = t_u0
        self.guard[f"ktm{par}"] = t_sn
        self.guard[f"panx{par}"] = (t_u0, t_k2)
        self.guard[f"Lambig{par}"] = t_s
        if full:
            self.guard[f"am{par}"] = t_o2

    def chunks_tile(self, phase, hg, ti, full):
        T = self.t
        self.tok_otile = {}
        self._ck = {}
        NC = getattr(self, "nchunk_dbg", None) or NCHUNK
        self.chunk_front(phase, hg, ti, 0, full)
        for c in range(NC):
            if c + 1 < NC:
                self.chunk_front(phase, hg, ti, c + 1, full)
            self.chunk_back(phase, hg, ti, c, full)
            if full and c % 2 == 1:
                m = c // 2
                rows = ti * TILE + m * 128
                w = (self.tok_otile.get(c - 1), self.tok_otile.get(c))
                tsp = self.sdma(T["o_sp"][rows:rows + 128, hg * 512:(hg + 1) * 512],
                                T["otile"][:, (m % 2) * 512:(m % 2) * 512 + 512],
                                waits=w, tag=f"osp{ti}_{m}")
                self.guard[f"otile{m % 2}"] = tsp
        hmax = hg * HPG + HPG - 1
        self.guard["postall"] = (self.tok_s[hmax], self.tok_otile.get(NC - 1))

    def attn_phase(self, phase):
        T = self.t
        hg = 0 if phase.endswith("a") else 1
        full = phase[0] == "F"
        self._spill_gates = self.kb.dma_barrier("dact")
        if full and getattr(self, "_pref_hg", None) == hg:
            wtok, ctok = self._pref_wtok, self._pref_ctok
        elif (not full) and getattr(self, "_kv_hg", None) == hg:
            wtok, ctok = self._kv_wtok, self._kv_ctok
        else:
            wtok, ctok = self.load_attn_weights(hg, want_qg=full)
            ctok = self.pool(lambda e: e.memset(T["stats"][0:1, 0:1], 0.0),
                             waits=(ctok,), inc=True, tag=f"cvfence{phase}")
        self.tok_wattn = wtok
        self.tok_wconv = ctok
        if not full and getattr(self, "_pref_hg", None) != hg:
            pw, pc = self.load_attn_weights(hg, want_qg=True, only_qg=True)
            pc = self.pool(lambda e: e.memset(T["stats"][0:1, 0:1], 0.0),
                           waits=(pc,), inc=True, tag=f"pcfence{phase}")
            self._pref_wtok = (wtok, pw)
            self._pref_ctok = (ctok, pc)
            self._pref_hg = hg
        if phase[0] == "R":
            for hl in range(HPG):
                h = hg * HPG + hl
                self.s_ping[h] = 0
                self.tok_s[h] = self.dve(lambda e, h=h: e.memset(
                    T["s_bf"][:, h * 256:h * 256 + 128], 0.0),
                    waits=(self.tok_s.get(h),), inc=True, tag=f"sinit{phase}_{h}")
        else:
            for hl in range(HPG):
                h = hg * HPG + hl
                self.s_ping[h] = 0
                if self.tok_srep.get(h) is None:
                    self.tok_srep[h] = self.dve(lambda e, h=h: e.memset(
                        T["srep_bf"][:, (h % 4) * 128:(h % 4) * 128 + 128], 0.0), inc=True,
                        tag=f"srep0{phase}_{h}")
                self.tok_s[h] = self.dve(lambda e, h=h: e.tensor_scalar(
                    T["s_bf"][:, h * 256:h * 256 + 128],
                    T["srep_bf"][:, (h % 4) * 128:(h % 4) * 128 + 128],
                    T["c_flag"][:, 0:1], None, op0=ALU.mult),
                    waits=(self.tok_s.get(h), self.tok_srep.get(h), self.tok_consts), inc=True,
                    tag=f"sinitF{phase}_{h}")
        if full:
            self.halo_proj(hg, wtok)
        for ti in range(self.ntile):
            self.ln_tile(phase, ti)
            self.proj_tile(phase, hg, ti, full)
            if phase == "Fa" and ti == self.ntile - 1 and "Rb" in self.do_phases:
                kb = self.kb
                snap = tuple((f"pe{i}", kb.cnt[f"pe{i}"]) for i in range(kb.NSEM_PER_Q)
                             if kb.cnt[f"pe{i}"] > 0)
                fence = self.dve(lambda e: e.memset(T["stats"][0:1, 11:12], 0.0),
                                 waits=(snap,), inc=True, tag="whfence")
                self.guard["wsball"] = fence
                nw, nc2 = self.load_attn_weights(1, want_qg=False)
                pw, pc2 = self.load_attn_weights(1, want_qg=True, only_qg=True)
                cf = self.pool(lambda e: e.memset(T["stats"][0:1, 0:1], 0.0),
                               waits=(nc2, pc2), inc=True, tag="hoistfence")
                self._kv_wtok, self._kv_ctok, self._kv_hg = nw, cf, 1
                self._pref_wtok = (nw, pw)
                self._pref_ctok = cf
                self._pref_hg = 1
            self.chunks_tile(phase, hg, ti, full)
        if phase[0] == "R":
            for hl in range(HPG):
                h = hg * HPG + hl
                ping = self.s_ping[h]
                self.tok_srep[h] = self.dve(lambda e, h=h, ping=ping: cp(e, 
                    T["srep_bf"][:, (h % 4) * 128:(h % 4) * 128 + 128],
                    T["s_bf"][:, (h * 2 + ping) * 128:(h * 2 + ping) * 128 + 128]),
                    waits=(self.tok_s[h], self.tok_s.get(h - 4) if h >= 4 else None),
                    inc=True, tag=f"srep{phase}_{h}")

    def ffn(self):
        T = self.t
        self._spill_gates = self.kb.dma_barrier("dact")
        tprev = None
        for piece in range(16):
            tl = self.ldma(T["c_rows"][0:1, 128:256], T["ds_row"][:, piece * 128:(piece + 1) * 128],
                           waits=(tprev,), tag=f"dsl{piece}")
            tb = self.pe(lambda e: e.matmul(T["pf0"][:, 0:128], T["c_ones1f"][:, :],
                                            T["c_rows"][0:1, 128:256], start=True, stop=True),
                         waits=(tl,) + self.bank_wait("pf0"), inc=True, tag=f"dsb{piece}")
            self.bank_clear("pf0")
            tprev = self.dve(lambda e, piece=piece: cp(e, 
                T["ds_bc"][:, piece * 128:(piece + 1) * 128], T["pf0"][:, 0:128]),
                waits=(tb,), inc=True, tag=f"dsc{piece}")
            self.bank_use("pf0", tprev)
        tok_dsbc = tprev
        t_gs = self.ldma(T["gs_sb"][:, :], T["gs"][:, :], tag="gs")
        t_us = self.ldma(T["us_sb"][:, :], T["us"][:, :], tag="us")

        for sp in range(2):
            self.ffn_subpass(sp, tok_dsbc, t_gs, t_us)
        fin = self.kb.dma_barrier("dact") + self.kb.dma_barrier("dsp") + self.kb.dma_barrier("dpool")
        self.kb.op("act", lambda e: e.nop(), waits=fin, tag="final_wait")

    def ffn_subpass(self, sp, tok_dsbc, t_gs, t_us):
        T = self.t
        base = sp * 1024
        tx0 = None
        tx1 = None
        for kt in range(KT):
            tx0 = self.ldma(T["xnT_sb"][:, kt * TILE:(kt + 1) * TILE],
                            T["xnTo_sp"][kt * 128:(kt + 1) * 128, base:base + 512],
                            waits=tuple(self.guard.get("xnT") or ()) + (self._spill_gates or ()),
                            tag=f"fxnT0_{sp}_{kt}")
            tx1 = self.ldma(T["big2"][:, kt * TILE:(kt + 1) * TILE],
                            T["xnTo_sp"][kt * 128:(kt + 1) * 128, base + 512:base + 1024],
                            waits=(self.guard.get("big2"), *(self._spill_gates or (None,))), tag=f"fxnT1_{sp}_{kt}")
        txnT = (tx0, tx1)
        t_h = {}
        for it in range(NIT):
            par = it % 2
            tg = self.ldma(T["fstage"][:, 0:2048], T["gw_b"][it * 128:(it + 1) * 128, :],
                           waits=(self.guard.get("fstage"), self.guard.get("dstage0"),
                                  self.guard.get("dstage1")), tag=f"fg{sp}_{it}")
            tcg = self.pool(lambda e, par=par: cp(e, 
                T["fbf"][:, par * 4096:par * 4096 + 2048], T["fstage"][:, 0:2048]),
                waits=(tg, self.guard.get(f"fbf{par}"), self.guard.get("dbf0"),
                       self.guard.get("dbf1")), inc=True, tag=f"fcg{sp}_{it}")
            tu = self.ldma(T["fstage"][:, 0:2048], T["uw_b"][it * 128:(it + 1) * 128, :],
                           waits=(tcg,), tag=f"fu{sp}_{it}")
            tcu = self.pool(lambda e, par=par: cp(e, 
                T["fbf"][:, par * 4096 + 2048:par * 4096 + 4096], T["fstage"][:, 0:2048]),
                waits=(tu, self.guard.get(f"fbf{par}")), inc=True, tag=f"fcu{sp}_{it}")
            self.guard["fstage"] = tcu
            for half in range(2):
                xsrc = T["xnT_sb"] if half == 0 else T["big2"]
                bg = f"pf{half * 2}"
                bu = f"pf{half * 2 + 1}"
                tmg = None
                tmu = None
                for kt in range(KT):
                    tmg = self.pe(lambda e, kt=kt, par=par, xsrc=xsrc, bg=bg: e.matmul(
                        T[bg][:, 0:512],
                        T["fbf"][:, par * 4096 + kt * 128:par * 4096 + (kt + 1) * 128],
                        xsrc[:, kt * TILE:(kt + 1) * TILE],
                        start=(kt == 0), stop=(kt == KT - 1)),
                        waits=(tcg, txnT[half]) + self.bank_wait(bg),
                        inc=(kt == KT - 1), tag=f"fgm{sp}_{it}_{half}_{kt}")
                self.bank_clear(bg)
                for kt in range(KT):
                    tmu = self.pe(lambda e, kt=kt, par=par, xsrc=xsrc, bu=bu: e.matmul(
                        T[bu][:, 0:512],
                        T["fbf"][:, par * 4096 + 2048 + kt * 128:par * 4096 + 2048 + (kt + 1) * 128],
                        xsrc[:, kt * TILE:(kt + 1) * TILE],
                        start=(kt == 0), stop=(kt == KT - 1)),
                        waits=(tcu, txnT[half]) + self.bank_wait(bu),
                        inc=(kt == KT - 1), tag=f"fum{sp}_{it}_{half}_{kt}")
                self.bank_clear(bu)
                hpar = half
                tsg = self.act(lambda e, bg=bg, it=it, hpar=hpar: e.activation(
                    T["hscr"][:, hpar * 512:(hpar + 1) * 512], T[bg][:, 0:512], AF.Silu,
                    scale=T["gs_sb"][:, it:it + 1]),
                    waits=(tmg, t_gs, self.guard.get(f"hscr{hpar}")), inc=True, tag=f"fsilu{sp}_{it}_{half}")
                self.bank_use(bg, tsg)
                th = self.dve(lambda e, bu=bu, it=it, half=half, hpar=hpar: e.scalar_tensor_tensor(
                    T["h_sb"][:, it * 1024 + half * 512:it * 1024 + (half + 1) * 512],
                    T[bu][:, 0:512], T["us_sb"][:, it:it + 1],
                    T["hscr"][:, hpar * 512:(hpar + 1) * 512],
                    op0=ALU.mult, op1=ALU.mult),
                    waits=(tmu, tsg, t_us, self.guard.get("h_sb")), inc=True, tag=f"fh{sp}_{it}_{half}")
                self.guard[f"hscr{hpar}"] = th
                t_h[(it, half)] = th
            self.guard[f"fbf{par}"] = t_h[(it, 1)]
        t_ong = {}
        for g in range(8):
            rows = base + g * 128
            t_og = self.ldma(T["ogrp"][:, :], T["o_sp"][rows:rows + 128, :],
                             waits=(self.guard.get("ogrp"), *(self._spill_gates or (None,))), tag=f"fog{sp}_{g}")
            t_gg = self.ldma(T["ggrp"][:, :], T["gate_sp"][rows:rows + 128, :],
                             waits=(self.guard.get("ggrp"), *(self._spill_gates or (None,))), tag=f"fgg{sp}_{g}")
            t_sq = self.act(lambda e: e.activation(T["dump"][:, 0:1024], T["ogrp"][:, :], AF.Square),
                            waits=(t_og, self.guard.get("fdump")), inc=True, tag=f"fsq{sp}_{g}")
            t_red = self.dve(lambda e: e.tensor_reduce(
                T["rmsp"][:, 0:8], T["dump"][:, 0:1024].rearrange("p (h f) -> p h f", h=8),
                axis=mybir.AxisListType.X, op=ALU.add),
                waits=(t_sq, self.guard.get("rmsp")), inc=True, tag=f"fred{sp}_{g}")
            self.guard["fdump"] = t_red
            t_ms = self.dve(lambda e: e.tensor_scalar(T["rmsp"][:, 8:16], T["rmsp"][:, 0:8],
                                                      1.0 / 128.0, None, op0=ALU.mult),
                            waits=(t_red,), inc=True, tag=f"fms{sp}_{g}")
            t_ln = self.act(lambda e: e.activation(T["rmsp"][:, 16:24], T["rmsp"][:, 8:16], AF.Ln,
                                                   bias=T["c_epsc"][:, 0:1]),
                            waits=(t_ms,), inc=True, tag=f"fln{sp}_{g}")
            t_rs = self.act(lambda e: e.activation(T["rmsp"][:, 8:16], T["rmsp"][:, 16:24], AF.Exp,
                                                   scale=-0.5),
                            waits=(t_ln,), inc=True, tag=f"frs{sp}_{g}")
            self.guard["rmsp"] = t_rs
            tong_last = None
            for hl8 in range(8):
                sl = slice(hl8 * 128, (hl8 + 1) * 128)
                t_a1 = self.dve(lambda e, sl=sl, hl8=hl8: e.scalar_tensor_tensor(
                    T["ong"][:, sl], T["ogrp"][:, sl], T["rmsp"][:, 8 + hl8:9 + hl8],
                    T["c_normw_bc"][:, :], op0=ALU.mult, op1=ALU.mult),
                    waits=(t_rs, self.guard.get("ong"), self.tok_prologue[2]), inc=False, tag=f"fo1{sp}_{g}_{hl8}")
                tong_last = self.dve(lambda e, sl=sl: e.tensor_tensor(
                    T["ong"][:, sl], T["ong"][:, sl], T["ggrp"][:, sl], op=ALU.mult),
                    waits=(t_gg,), inc=(hl8 == 7), tag=f"fo2{sp}_{g}_{hl8}")
            self.guard["ogrp"] = tong_last
            self.guard["ggrp"] = tong_last
            tev = None
            for hl8 in range(8):
                tt = self.pe(lambda e, hl8=hl8: e.transpose(
                    T["pf4"][:, 0:128], T["ong"][:, hl8 * 128:(hl8 + 1) * 128], T["c_id128f"][:, :]),
                    waits=(tong_last,) + self.bank_wait("pf4"), inc=True, tag=f"fot{sp}_{g}_{hl8}")
                self.bank_clear("pf4")
                tev = self.dve(lambda e, g=g, hl8=hl8: cp(e, 
                    T["big2"][:, g * 1024 + hl8 * 128:g * 1024 + (hl8 + 1) * 128], T["pf4"][:, 0:128]),
                    waits=(tt, self.guard.get("big2f3"), t_h[(NIT - 1, 1)]), inc=True, tag=f"foe{sp}_{g}_{hl8}")
                self.bank_use("pf4", tev)
            self.guard["ong"] = tev
            t_ong[g] = tev
        for hidN in range(4):
            t_wo = {}
            for odt in range(8):
                row = (odt * 4 + hidN) * 128
                tl = self.ldma(T["wostage"][:, :], T["Wo_b"][row:row + 128, :],
                               waits=(self.guard.get("wostage"),), tag=f"fwol{sp}_{hidN}_{odt}")
                tc = self.pool(lambda e, odt=odt: cp(e, 
                    T["wo_sb"][:, odt * 512:(odt + 1) * 512], T["wostage"][:, :]),
                    waits=(tl, self.guard.get(f"wo{odt}")), inc=True, tag=f"fwoc{sp}_{hidN}_{odt}")
                self.guard["wostage"] = tc
                t_wo[odt] = tc
            t_mm = {ts: None for ts in range(8)}
            started = {ts: False for ts in range(8)}
            for it in range(NIT):
                dpar = it % 2
                row = (hidN * NIT + it) * 128
                dstg = T["fstage"][:, dpar * 1024:dpar * 1024 + 512]
                dbfs = T["fbf"][:, dpar * 512:(dpar + 1) * 512]
                tdl = self.ldma(dstg, T["dw_b"][row:row + 128, :],
                                waits=(self.guard.get(f"dstage{dpar}"),), tag=f"fdl{sp}_{hidN}_{it}")
                tdc = self.dve(lambda e, hidN=hidN, dstg=dstg, dbfs=dbfs: e.tensor_tensor(
                    dbfs, dstg, T["ds_bc"][:, hidN * 512:(hidN + 1) * 512], op=ALU.mult),
                    waits=(tdl, tok_dsbc, self.guard.get(f"dbf{dpar}")), inc=True, tag=f"fdc{sp}_{hidN}_{it}")
                self.guard[f"dstage{dpar}"] = tdc
                last_ts = None
                for ts in range(8):
                    bk = f"pf{ts}"
                    wts = (tdc, t_h[(it, ts // 4)])
                    if not started[ts]:
                        wts = wts + self.bank_wait(bk)
                    last_ts = self.pe(lambda e, it=it, ts=ts, bk=bk, dbfs=dbfs, st=(not started[ts]): e.matmul(
                        T[bk][:, 0:512],
                        T["h_sb"][:, it * 1024 + ts * 128:it * 1024 + (ts + 1) * 128],
                        dbfs, start=st, stop=False),
                        waits=wts, inc=(ts == 7), tag=f"fdm{sp}_{hidN}_{it}_{ts}")
                    if not started[ts]:
                        self.bank_clear(bk)
                        started[ts] = True
                self.guard[f"dbf{dpar}"] = last_ts
                self._last_h_reader = last_ts
            t_wom = {}
            for odt in range(8):
                for ts in range(8):
                    bk = f"pf{ts}"
                    t_wom[ts] = self.pe(lambda e, odt=odt, ts=ts, bk=bk: e.matmul(
                        T[bk][:, 0:512],
                        T["big2"][:, ts * 1024 + odt * 128:ts * 1024 + (odt + 1) * 128],
                        T["wo_sb"][:, odt * 512:(odt + 1) * 512],
                        start=False, stop=(odt == 7)),
                        waits=(t_wo[odt], t_ong[ts]), inc=(odt == 7), tag=f"fwm{sp}_{hidN}_{odt}_{ts}")
            for ts in range(8):
                rows = base + ts * 128
                opar = ts % 2
                txl = self.ldma(T["x_sb"][:, opar * 512:(opar + 1) * 512],
                                T["x_own"][rows:rows + 128, hidN * 512:(hidN + 1) * 512],
                                waits=(self.guard.get(f"fx{opar}"),), tag=f"fxl{sp}_{hidN}_{ts}")
                tev = self.dve(lambda e, ts=ts, opar=opar: e.tensor_tensor(
                    T["ostage"][:, opar * 512:(opar + 1) * 512], T[f"pf{ts}"][:, 0:512],
                    T["x_sb"][:, opar * 512:(opar + 1) * 512], op=ALU.add),
                    waits=(t_wom[ts], txl, self.guard.get(f"ost{opar}")), inc=True, tag=f"fev{sp}_{hidN}_{ts}")
                self.bank_use(f"pf{ts}", tev)
                started[ts] = False
                tout = self.sdma(T["out"][rows:rows + 128, hidN * 512:(hidN + 1) * 512],
                                 T["ostage"][:, opar * 512:(opar + 1) * 512],
                                 waits=(tev,), tag=f"fout{sp}_{hidN}_{ts}")
                self.guard[f"ost{opar}"] = tout
                self.guard[f"fx{opar}"] = tev
        self.guard["big2f3"] = t_wom[7] if t_wom else None
        self.guard["big2"] = self.guard.get("big2f3")
        self.guard["xnT"] = (t_h[(NIT - 1, 0)],)
        self.guard["h_sb"] = self._last_h_reader

    def build_all(self):
        from contextlib import ExitStack
        nc = self.nc
        self.dram_inputs()
        self.s_ping = {h: 0 for h in range(NH)}
        self.tok_s = {}
        self.tok_srep = {}
        self.tok_conv_prev = {}
        self.tok_otile = {}
        self._spill_gates = ()
        with ExitStack() as outer:
            self.alloc_outer(outer)
            with ExitStack() as actx:
                self.alloc_attn(actx)
                self.prologue()
                self.prologue2()
                if "S" in self.debug_names:
                    self.dve(lambda e: e.memset(self.t["s_bf"][:, :], 0.0), inc=True, tag="dbgS0")
                for phase in self.do_phases:
                    self.attn_phase(phase)
                if "S" in self.debug_names:
                    ts = self.sdma(self.t["dbg_S"][:, :], self.t["s_bf"][:, :],
                                   waits=tuple(self.tok_s.get(h) for h in range(NH)) +
                                         tuple(self.tok_srep.get(h) for h in range(NH)),
                                   tag="dbgS")
                if not self.do_ffn:
                    fin = self.kb.dma_barrier("dact") + self.kb.dma_barrier("dsp")
                    self.kb.op("act", lambda e: e.nop(), waits=fin, tag="final_wait_dbg")
                self.kb.flush("attn")
            if self.do_ffn:
                with ExitStack() as fctx:
                    self.alloc_ffn(fctx)
                    self.ffn()
                    self.kb.flush("ffn")
        return nc


def build_kernel(debug_names=(), ntile=NTILE, do_ffn=True, do_phases=("Ra", "Fa", "Rb", "Fb")):
    nc = bass.Bass("TRN2", target_bir_lowering=False)
    p = Prog(nc, debug_names=debug_names, ntile=ntile, do_ffn=do_ffn, do_phases=do_phases)
    p.build_all()
    return nc, p


def kernel(**inputs):
    nc, _ = build_kernel()
    in_maps = [stage_inputs_for_core(inputs, c) for c in range(8)]
    res = run_bass_kernel_spmd(nc, in_maps, core_ids=list(range(8)))
    out = np.empty((BATCH, LQ, HID), np.float32)
    for c in range(8):
        b, hh = c // 2, c % 2
        out[b, hh * HALF:(hh + 1) * HALF] = np.asarray(res.results[c]["out"])
    return out


# version 12 (final)
# speedup vs baseline: 1.0616x; 1.0287x over previous
import math
import numpy as np
import ml_dtypes
import concourse.bass as bass
import concourse.mybir as mybir
from concourse.bass_utils import run_bass_kernel_spmd

F32 = mybir.dt.float32
BF16 = mybir.dt.bfloat16
AF = mybir.ActivationFunctionType
ALU = mybir.AluOpType

HID = 2048
NH = 8
DH = 128
NJ = 2
LQ = 4096
BATCH = 4
HALF = 2048
TILE = 512
NTILE = HALF // TILE
CT = 64
NCHUNK = TILE // CT
HPG = 4
INTER = 5632
NIT = INTER // 128
KT = HID // 128
LN128 = math.log(128.0)

ENG2BLK = {"sp": "sync", "pool": "gpsimd", "pe": "tensor", "dve": "vector", "act": "scalar"}



def cp(e, dst, src):
    f = getattr(e, "tensor_copy", None)
    if f is not None:
        return f(dst, src)
    return e.copy(dst, src)

class KB:

    NSEM_PER_Q = 8
    GROUP_QUEUES = ("dsp", "dact", "dpool", "pe")
    QINC = {"dsp": 16, "dact": 16, "dpool": 16, "pe": 1}

    def __init__(self, nc):
        self.nc = nc
        self.ops = {e: [] for e in ENG2BLK}
        self.cnt = {}
        self.known = {e: {} for e in ENG2BLK}
        self.sem_names = ["dve", "act", "pool"]
        for q in self.GROUP_QUEUES:
            self.sem_names += [f"{q}{i}" for i in range(self.NSEM_PER_Q)]
        for s in self.sem_names:
            self.cnt[s] = 0
        self.handles = {}
        self.prunable = {"dve", "act"}
        self.qcur = {q: 0 for q in self.GROUP_QUEUES}
        self.qclosed = {}
        self.last_tok = {}
        self.qrelease = {}
        self.pending_cert = {e: [] for e in ENG2BLK}
        self.dma_sem_set = {sn for q in self.GROUP_QUEUES for sn in
                            [f"{q}{i}" for i in range(self.NSEM_PER_Q)]}

    def dma_sem(self, q):
        for step in range(self.NSEM_PER_Q + 1):
            sn = f"{q}{self.qcur[q]}"
            if not self.qclosed.get(sn):
                return sn, ()
            cert = self.qrelease.get(sn)
            if cert is not None:
                self.qclosed[sn] = False
                del self.qrelease[sn]
                return sn, (cert,)
            self.qcur[q] = (self.qcur[q] + 1) % self.NSEM_PER_Q
        raise AssertionError(f"no reusable dma sem on queue {q}")

    def dma_barrier(self, q):
        toks = []
        for i in range(self.NSEM_PER_Q):
            sn = f"{q}{i}"
            if self.cnt[sn] > 0:
                toks.append((sn, self.cnt[sn]))
                self.qclosed[sn] = True
        return tuple(toks)

    @staticmethod
    def _flatten(waits):
        out = []
        def rec(t):
            if t is None:
                return
            if isinstance(t, tuple):
                if len(t) == 2 and isinstance(t[0], str):
                    out.append(t)
                else:
                    for x in t:
                        rec(x)
        for t in waits:
            rec(t)
        return out

    def op(self, eng, fn, waits=(), inc=None, inc_n=1, tag=""):
        w2 = []
        for t in self._flatten(waits):
            s, v = t
            if s in self.dma_sem_set:
                v = self.cnt[s]
                if v == 0:
                    continue
                if not self.qclosed.get(s):
                    self.qclosed[s] = True
                if s not in self.qrelease:
                    self.pending_cert[eng].append((s, v))
            if s in self.prunable or s == eng:
                if self.known[eng].get(s, 0) >= v:
                    continue
                self.known[eng][s] = max(self.known[eng].get(s, 0), v)
            else:
                if self.known[eng].get(s, 0) >= v:
                    continue
                self.known[eng][s] = max(self.known[eng].get(s, 0), v)
            w2.append((s, v))
        is_dma = inc in self.dma_sem_set if inc is not None else False
        extra = ()
        if eng in ("dve", "act", "pool") and not is_dma:
            lt = self.last_tok.get(eng)
            if lt is not None:
                s2, v2 = lt
                if self.known[eng].get(s2, 0) < v2:
                    self.known[eng][s2] = v2
                    w2.append((s2, v2))
            if inc is None:
                inc = {"dve": "dve", "act": "act", "pool": "pool"}[eng]
        elif eng == "pe":
            inc, extra = self.dma_sem("pe")
            inc_n = 1
            for t in extra:
                s2, v2 = t
                if self.known[eng].get(s2, 0) < v2:
                    self.known[eng][s2] = v2
                    w2.append((s2, v2))
        tok = None
        incs = []
        if inc is not None:
            self.cnt[inc] += inc_n
            incs = [(inc, inc_n)]
            tok = (inc, self.cnt[inc])
        if eng in ("dve", "act", "pool") and not is_dma:
            self.last_tok[eng] = tok
        if tok is not None and self.pending_cert[eng]:
            for (s, v) in self.pending_cert[eng]:
                if self.cnt[s] == v:
                    self.qrelease[s] = tok
            self.pending_cert[eng] = []
        self.ops[eng].append((w2, fn, incs, tag))
        return tok

    def barrier_token(self, eng):
        raise NotImplementedError

    def flush(self, name):
        nc = self.nc
        handles = self.handles
        with nc.Block(name) as blk:
            for ename, blkname in ENG2BLK.items():
                deco = getattr(blk, blkname)
                ops = self.ops[ename]

                def make_body(ops):
                    def body(eng):
                        for (waits, fn, incs, tag) in ops:
                            for (s, v) in waits:
                                eng.wait_ge(handles[s], v)
                            inst = fn(eng)
                            for (s, k) in incs:
                                assert inst is not None, f"op {tag} has inc but returned no instruction"
                                inst.then_inc(handles[s], k)
                    return body

                deco(make_body(ops))
            self.ops = {e: [] for e in ENG2BLK}
        comp = ["dve", "act", "pool"] + [f"pe{i}" for i in range(self.NSEM_PER_Q)]
        for e in ENG2BLK:
            for sname in comp:
                self.known[e][sname] = self.cnt[sname]



def blockify_lhsT(w, mtile=128):
    K, M = w.shape
    nK, nM = K // 128, M // mtile
    return np.ascontiguousarray(
        w.reshape(nK, 128, nM, mtile).transpose(2, 1, 0, 3).reshape(nM, 128, nK * mtile)
    )


def make_masks():
    tt = np.arange(CT)
    strict = (tt[None, :] < tt[:, None]).astype(np.float32)
    incl = (tt[None, :] <= tt[:, None]).astype(np.float32)
    maskM = np.block([[strict, strict], [incl, strict]])
    maskM_neg = np.tile(-maskM, (1, 4)).astype(ml_dtypes.bfloat16)
    maskAT = np.tile(-np.concatenate([incl.T, incl.T], axis=0), (1, 4)).astype(ml_dtypes.bfloat16)
    trilT = np.triu(np.ones((CT, CT), np.float32))
    return maskM_neg, maskAT, trilT


def stage_inputs_for_core(inputs, core):
    b, hh = core // 2, core % 2
    x = np.asarray(inputs["L0_attn_x"])
    d = {}
    d["x_own"] = np.ascontiguousarray(x[b, hh * HALF:(hh + 1) * HALF])
    d["x_replay"] = np.ascontiguousarray(x[b, 0:HALF])
    d["x_halo"] = (np.ascontiguousarray(x[b, HALF - 3:HALF]) if hh == 1
                   else np.zeros((3, HID), np.float32))
    d["flag"] = np.full((128, 1), float(hh), np.float32)

    d["Wq_b"] = blockify_lhsT(np.asarray(inputs["L0_attn_Wq"])).reshape(8 * 128, KT * 128)
    d["Wk_b"] = blockify_lhsT(np.asarray(inputs["L0_attn_Wk"])).reshape(16 * 128, KT * 128)
    d["Wv_b"] = blockify_lhsT(np.asarray(inputs["L0_attn_Wv"])).reshape(16 * 128, KT * 128)
    wg = np.asarray(inputs["L0_attn_Wg"])
    d["Wg_r"] = np.ascontiguousarray(np.concatenate([
        wg[:, hg * 512:(hg + 1) * 512].reshape(KT, 128, 512).transpose(1, 0, 2).reshape(128, KT * 512)
        for hg in range(2)], axis=0))
    wba = np.concatenate([np.asarray(inputs["L0_attn_Wb"]),
                          np.asarray(inputs["L0_attn_Wa"])], axis=1)
    d["Wba_b"] = blockify_lhsT(wba, mtile=24).reshape(128, KT * 24)
    d["Wo_b"] = np.ascontiguousarray(
        np.asarray(inputs["L0_attn_Wo"]).reshape(8, 128, 4, 512).transpose(0, 2, 1, 3)).reshape(8 * 4 * 128, 512)
    d["convq"] = np.ascontiguousarray(np.asarray(inputs["L0_attn_conv_q_w"])[:, 0, :])
    d["convk"] = np.ascontiguousarray(np.asarray(inputs["L0_attn_conv_k_w"])[:, 0, :])
    d["convv"] = np.ascontiguousarray(np.asarray(inputs["L0_attn_conv_v_w"])[:, 0, :])
    d["A_log"] = np.asarray(inputs["L0_attn_A_log"]).reshape(1, 8)
    d["dt_bias"] = np.asarray(inputs["L0_attn_dt_bias"]).reshape(1, 8)
    d["norm_w"] = np.asarray(inputs["L0_attn_norm_w"]).reshape(1, 128)

    gw = np.asarray(inputs["L0_ffn_gate_w"])
    uw = np.asarray(inputs["L0_ffn_up_w"])
    dw = np.asarray(inputs["L0_ffn_down_w"])
    d["gw_b"] = blockify_lhsT(np.ascontiguousarray(gw.T)).reshape(NIT * 128, KT * 128)
    d["uw_b"] = blockify_lhsT(np.ascontiguousarray(uw.T)).reshape(NIT * 128, KT * 128)
    dwT = np.ascontiguousarray(dw.T)
    d["dw_b"] = np.ascontiguousarray(dwT.reshape(NIT, 128, 4, 512).transpose(2, 0, 1, 3)).reshape(4 * NIT * 128, 512)
    d["gs"] = np.ascontiguousarray(np.asarray(inputs["L0_ffn_gate_s"]).reshape(NIT, 128).T)
    d["us"] = np.ascontiguousarray(np.asarray(inputs["L0_ffn_up_s"]).reshape(NIT, 128).T)
    d["ds_row"] = np.asarray(inputs["L0_ffn_down_s"]).reshape(1, HID)

    maskM_neg, maskAT, trilT = make_masks()
    d["maskM"] = maskM_neg
    d["maskAT"] = maskAT
    d["trilT"] = trilT
    d["id128b"] = np.eye(128, dtype=ml_dtypes.bfloat16)
    d["id128f"] = np.eye(128, dtype=np.float32)
    d["id64f"] = np.eye(64, dtype=np.float32)
    d["id4f"] = np.eye(4, dtype=np.float32)
    d["id3b"] = np.eye(3, dtype=ml_dtypes.bfloat16)
    d["ones1f"] = np.ones((1, 128), np.float32)
    d["epsc"] = np.concatenate([
        np.full((128, 1), 1e-5, np.float32),
        np.full((128, 1), 1e-18, np.float32),
        np.full((128, 1), 1.0, np.float32),
        np.full((128, 1), 0.0, np.float32),
    ], axis=1)
    return d



class Prog:
    def __init__(self, nc, debug_names=(), ntile=NTILE, do_ffn=True, do_phases=("Ra", "Fa", "Rb", "Fb")):
        self.nc = nc
        self.kb = KB(nc)
        self.debug_names = set(debug_names)
        self.debug_outs = {}
        self.ntile = ntile
        self.do_ffn = do_ffn
        self.do_phases = do_phases
        self.ctx = []
        self.guard = {}
        self.guardB = {}
        self.t = {}

    def dram_inputs(self):
        nc = self.nc
        T = self.t
        def di(name, shape, dt=F32):
            T[name] = nc.dram_tensor(name, list(shape), dt, kind="ExternalInput")
        di("x_own", (HALF, HID)); di("x_replay", (HALF, HID)); di("x_halo", (3, HID))
        di("flag", (128, 1))
        di("Wq_b", (8 * 128, KT * 128)); di("Wk_b", (16 * 128, KT * 128)); di("Wv_b", (16 * 128, KT * 128))
        di("Wg_r", (2 * 128, KT * 512)); di("Wba_b", (128, KT * 24))
        di("Wo_b", (8 * 4 * 128, 512))
        di("convq", (8 * 128, 4)); di("convk", (16 * 128, 4)); di("convv", (16 * 128, 4))
        di("A_log", (1, 8)); di("dt_bias", (1, 8)); di("norm_w", (1, 128))
        di("gw_b", (NIT * 128, KT * 128)); di("uw_b", (NIT * 128, KT * 128))
        di("dw_b", (4 * NIT * 128, 512))
        di("gs", (128, NIT)); di("us", (128, NIT)); di("ds_row", (1, HID))
        di("maskM", (128, 512), BF16); di("maskAT", (128, 256), BF16)
        di("trilT", (64, 64)); di("id128b", (128, 128), BF16); di("id128f", (128, 128))
        di("id64f", (64, 64)); di("id4f", (4, 4)); di("id3b", (3, 3), BF16)
        di("ones1f", (1, 128)); di("epsc", (128, 4))
        T["out"] = nc.dram_tensor("out", [HALF, HID], F32, kind="ExternalOutput")
        def dk(name):
            return "ExternalOutput" if name in self.debug_names else "Internal"
        T["xnTr_sp"] = nc.dram_tensor("xnTr_sp", [KT * 128, HALF], BF16, kind=dk("xnTr_sp"))
        T["xnTo_sp"] = nc.dram_tensor("xnTo_sp", [KT * 128, HALF], BF16, kind=dk("xnTo_sp"))
        T["o_sp"] = nc.dram_tensor("o_sp", [HALF, 1024], BF16, kind=dk("o_sp"))
        T["gate_sp"] = nc.dram_tensor("gate_sp", [HALF, 1024], BF16, kind=dk("gate_sp"))
        if "S" in self.debug_names:
            T["dbg_S"] = nc.dram_tensor("dbg_S", [128, NH * 2 * 128], BF16, kind="ExternalOutput")

    def dbg(self, name, shape, dt=F32):
        if name in self.debug_names:
            h = self.nc.dram_tensor("dbg_" + name, list(shape), dt, kind="ExternalOutput")
            self.debug_outs[name] = h
            return h
        return None

    def alloc_outer(self, ctx):
        nc = self.nc
        T = self.t
        def sb(name, shape, dt):
            T[name] = ctx.enter_context(nc.sbuf_tensor(name, list(shape), dt))
        sb("c_id128b", (128, 128), BF16); sb("c_id128f", (128, 128), F32)
        sb("c_id64f", (64, 64), F32); sb("c_id4f", (4, 4), F32); sb("c_id3b", (3, 3), BF16)
        sb("c_ones1f", (1, 128), F32); sb("c_maskM", (128, 512), BF16); sb("c_maskAT", (128, 256), BF16)
        sb("c_trilT", (64, 64), F32); sb("c_epsc", (128, 4), F32)
        sb("c_dtb64", (64, 64), F32)
        sb("c_negA64", (64, 64), F32)
        sb("c_normw_bc", (128, 128), F32)
        sb("c_negexpA", (128, 8), F32); sb("c_dtb", (128, 8), F32)
        sb("c_flag", (128, 1), F32)
        sb("c_rows", (1, 768), F32)
        sb("x_sb", (128, HID), F32)
        sb("xh_sb", (3, HID), F32)
        sb("xnh_sb", (3, HID), BF16)
        sb("xnTh_sb", (128, KT * 3), BF16)
        sb("xnT_sb", (128, KT * TILE), BF16)
        sb("stats", (128, 12), F32)
        sb("dump", (128, HID), BF16)
        sb("trow", (1, 256), F32)
        sb("trow2", (1, 128), F32)
        sb("panx", (128, 16), F32)
        for sname in self.kb.sem_names:
            self.kb.handles[sname] = ctx.enter_context(nc.semaphore("sem_" + sname))

    def alloc_attn(self, ctx):
        nc, T = self.nc, self.t
        def sb(name, shape, dt):
            T[name] = ctx.enter_context(nc.sbuf_tensor(name, list(shape), dt))
        def ps(name, shape, dt):
            T[name] = ctx.enter_context(nc.psum_tensor(name, list(shape), dt))
        ps("pT0", (128, 1024), BF16); ps("pT1", (128, 1024), BF16)
        ps("pA", (128, 512), F32); ps("pB", (128, 512), F32)
        ps("pC", (128, 512), F32); ps("pD", (128, 512), F32)
        ps("pE", (128, 512), F32); ps("pF", (128, 512), F32)
        sb("xn_sb", (128, HID), BF16)
        sb("bg_tm", (64, NCHUNK * 24), F32)
        sb("bgtmp", (24, TILE), BF16)
        sb("lnb_tm", (64, NCHUNK * 16), F32)
        sb("gsp_tm", (64, NCHUNK * 8), F32)
        sb("bgt2", (64, NCHUNK * 16), F32)
        sb("bgt3", (64, NCHUNK * 16), F32)
        sb("gcum_all", (64, 64), F32)
        sb("Lam_all", (128, 64), F32)
        sb("gCl_all", (128, 64), F32)
        sb("gce_all", (128, 64), F32)
        sb("pan1", (128, 24), F32)
        sb("pan2", (64, 32), F32)
        sb("s_bf", (128, NH * 2 * 128), BF16)
        sb("srep_bf", (128, 4 * 128), BF16)
        sb("D_sb", (128, 512), F32)
        sb("Eq_sb", (128, 256), BF16)
        sb("nN_a", (128, 512), BF16)
        sb("nN_b", (128, 512), BF16)
        sb("nT_a", (128, 512), BF16)
        sb("nT_b", (128, 512), BF16)
        sb("nLN_a", (128, 512), BF16)
        sb("nLN_b", (128, 512), BF16)
        sb("nLT_a", (128, 512), BF16)
        sb("nLT_b", (128, 512), BF16)
        sb("ucp", (128, 1024), BF16)
        sb("ot1w", (64, 512), BF16)
        sb("ktm_sb", (128, 1024), BF16)
        sb("vtm_sb", (128, 1024), BF16)
        sb("am_sb", (128, 512), BF16)
        sb("kss", (128, 8), F32)
        sb("qss", (64, 8), F32)
        sb("otile", (128, 1024), BF16)
        sb("wq_sb", (128, 4 * KT * 128), BF16)
        sb("wk_sb", (128, 8 * KT * 128), BF16)
        sb("wv_sb", (128, 8 * KT * 128), BF16)
        sb("wg_sb", (128, KT * 512), BF16)
        sb("wba_sb", (128, KT * 24), BF16)
        sb("wstage", (128, 1024), F32)
        sb("convw", (128, 20 * 4), F32)
        sb("pre", (128, 20 * (TILE + 3)), BF16)
        sb("halosave", (128, 20 * 3), BF16)
        sb("cacc", (128, 2 * TILE), BF16)
        sb("gstage", (128, TILE), BF16)

    def alloc_ffn(self, ctx):
        nc, T = self.nc, self.t
        def sb(name, shape, dt):
            T[name] = ctx.enter_context(nc.sbuf_tensor(name, list(shape), dt))
        def ps(name, shape, dt):
            T[name] = ctx.enter_context(nc.psum_tensor(name, list(shape), dt))
        for i in range(8):
            ps(f"pf{i}", (128, 512), F32)
        sb("h_sb", (128, NIT * 1024), BF16)
        sb("big2", (128, 8192), BF16)
        sb("fstage", (128, 2048), F32)
        sb("fbf", (128, 2 * 4096), BF16)
        sb("wostage", (128, 512), F32)
        sb("wo_sb", (128, 8 * 512), BF16)
        sb("ds_bc", (128, HID), BF16)
        sb("gs_sb", (128, NIT), F32)
        sb("us_sb", (128, NIT), F32)
        sb("ogrp", (128, 1024), BF16)
        sb("ggrp", (128, 1024), BF16)
        sb("ong", (128, 1024), F32)
        sb("rmsp", (128, 24), F32)
        sb("hscr", (128, 2 * 512), F32)
        sb("ostage", (128, 2 * 512), F32)
    def pe(self, fn, waits=(), inc=False, tag=""):
        self.kb.op("pe", fn, waits, "pe" if inc else None, tag=tag)
        kb = self.kb
        return tuple((f"pe{i}", kb.cnt[f"pe{i}"]) for i in range(kb.NSEM_PER_Q)
                     if kb.cnt[f"pe{i}"] > 0)

    def dve(self, fn, waits=(), inc=False, tag=""):
        return self.kb.op("dve", fn, waits, "dve" if inc else None, tag=tag)

    def act(self, fn, waits=(), inc=False, tag=""):
        return self.kb.op("act", fn, waits, "act" if inc else None, tag=tag)

    def pool(self, fn, waits=(), inc=False, tag=""):
        return self.kb.op("pool", fn, waits, "pool" if inc else None, tag=tag)

    def ldma(self, dst, src, waits=(), tag=""):
        sn, extra = self.kb.dma_sem("dsp")
        return self.kb.op("sp", lambda e: e.dma_start(dst, src), tuple(waits) + extra, sn, 16, tag=tag)

    def sdma(self, dst, src, waits=(), tag=""):
        sn, extra = self.kb.dma_sem("dact")
        return self.kb.op("act", lambda e: e.dma_start(dst, src), tuple(waits) + extra, sn, 16, tag=tag)

    def pdma(self, dst, src, waits=(), tag=""):
        sn, extra = self.kb.dma_sem("dpool")
        return self.kb.op("pool", lambda e: e.dma_start(dst, src), tuple(waits) + extra, sn, 16, tag=tag)

    def prologue(self):
        T = self.t
        loads = [
            ("c_id128b", T["id128b"]), ("c_id128f", T["id128f"]), ("c_id64f", T["id64f"]),
            ("c_id4f", T["id4f"]), ("c_id3b", T["id3b"]), ("c_ones1f", T["ones1f"]),
            ("c_maskM", T["maskM"]), ("c_maskAT", T["maskAT"]), ("c_trilT", T["trilT"]),
            ("c_epsc", T["epsc"]), ("c_flag", T["flag"]),
        ]
        toks = []
        for name, src in loads:
            toks.append(self.ldma(T[name][:, :], src[:, :], tag="ld_" + name))
        t_alog = self.ldma(T["c_rows"][0:1, 0:8], T["A_log"][:, :], tag="ld_alog")
        t_dtb = self.ldma(T["c_rows"][0:1, 8:16], T["dt_bias"][:, :], tag="ld_dtb")
        t_nw = self.ldma(T["c_rows"][0:1, 16:144], T["norm_w"][:, :], tag="ld_nw")
        t_xh = self.ldma(T["xh_sb"][:, :], T["x_halo"][:, :], tag="ld_xh")
        self.tok_consts = self.dve(lambda e: e.memset(T["stats"][0:1, 11:12], 0.0),
                                   waits=(toks[-1],), inc=True, tag="constfence")
        def bc(dst_psum, row_ap, n):
            return lambda e: e.matmul(dst_psum, T["c_ones1f"][:, :], row_ap, start=True, stop=True)
        tm = self.pe(bc(T["pA"][:, 0:8], T["c_rows"][0:1, 0:8], 8),
                     waits=(t_alog, toks[5]), inc=True, tag="bc_alog")
        tm2 = self.pe(bc(T["pA"][:, 8:16], T["c_rows"][0:1, 8:16], 8),
                      waits=(t_dtb,), inc=True, tag="bc_dtb")
        tm3 = self.pe(bc(T["pA"][:, 16:144], T["c_rows"][0:1, 16:144], 128),
                      waits=(t_nw,), inc=True, tag="bc_nw")
        t1 = self.act(lambda e: e.activation(T["panx"][:, 0:8], T["pA"][:, 0:8], AF.Exp),
                      waits=(tm,), inc=True, tag="expA")
        t2 = self.dve(lambda e: e.tensor_scalar(T["c_negexpA"][:, :], T["panx"][:, 0:8], -1.0, None, op0=ALU.mult),
                      waits=(t1,), inc=True, tag="negA")
        t3 = self.dve(lambda e: cp(e, T["c_dtb"][:, :], T["pA"][:, 8:16]),
                      waits=(tm2,), inc=True, tag="cp_dtb")
        t4 = self.dve(lambda e: cp(e, T["c_normw_bc"][:, :], T["pA"][:, 16:144]),
                      waits=(tm3,), inc=True, tag="cp_nw")
        self.guard["pA"] = t4
        self.tok_prologue = (t2, t3, t4, self.tok_consts)
        tstat = self.act(lambda e: e.activation(T["xnh_sb"][0:3, :], T["xh_sb"][0:3, :], AF.Copy,
                                                accum_out=T["stats"][0:3, 1:2]),
                         waits=(t_xh,), inc=True, tag="halo_sum")
        tsq = self.act(lambda e: e.activation(T["xnh_sb"][0:3, :], T["xh_sb"][0:3, :], AF.Square,
                                              accum_out=T["stats"][0:3, 3:4]),
                       waits=(t_xh,), inc=True, tag="halo_sq")
        tmn = self.dve(lambda e: e.tensor_scalar(T["stats"][0:3, 4:5], T["stats"][0:3, 1:2], 1.0 / HID, None, op0=ALU.mult),
                       waits=(tstat,), inc=True, tag="halo_m")
        tms = self.dve(lambda e: e.tensor_scalar(T["stats"][0:3, 5:6], T["stats"][0:3, 3:4], 1.0 / HID, None, op0=ALU.mult),
                       waits=(tsq,), inc=True, tag="halo_ms")
        tm2v = self.dve(lambda e: e.tensor_tensor(T["stats"][0:3, 6:7], T["stats"][0:3, 4:5], T["stats"][0:3, 4:5], op=ALU.mult),
                        waits=(tmn,), inc=True, tag="halo_m2")
        tvar = self.dve(lambda e: e.tensor_tensor(T["stats"][0:3, 7:8], T["stats"][0:3, 5:6], T["stats"][0:3, 6:7], op=ALU.subtract),
                        waits=(tms, tm2v), inc=True, tag="halo_var")
        tln = self.act(lambda e: e.activation(T["stats"][0:3, 8:9], T["stats"][0:3, 7:8], AF.Ln, bias=T["c_epsc"][0:3, 0:1]),
                       waits=(tvar, self.tok_consts), inc=True, tag="halo_ln")
        trs = self.act(lambda e: e.activation(T["stats"][0:3, 9:10], T["stats"][0:3, 8:9], AF.Exp, scale=-0.5),
                       waits=(tln,), inc=True, tag="halo_rs")
        tmrs = self.dve(lambda e: e.tensor_tensor(T["stats"][0:3, 10:11], T["stats"][0:3, 4:5], T["stats"][0:3, 9:10], op=ALU.mult),
                        waits=(trs,), inc=True, tag="halo_mrs")
        txn = self.dve(lambda e: e.tensor_scalar(T["xnh_sb"][0:3, :], T["xh_sb"][0:3, :],
                                                 T["stats"][0:3, 9:10], T["stats"][0:3, 10:11],
                                                 op0=ALU.mult, op1=ALU.subtract),
                       waits=(tmrs,), inc=True, tag="halo_xn")
        gT = None
        for kt in range(KT):
            tt = self.pe(lambda e, kt=kt: e.transpose(T["pT0"][0:128, kt * 4:kt * 4 + 3],
                                                      T["xnh_sb"][0:3, kt * 128:(kt + 1) * 128],
                                                      T["c_id3b"][:, :]),
                         waits=(txn, self.guard.get("pT0"), self.tok_consts), inc=(kt == KT - 1), tag=f"halo_tp{kt}")
            if kt == KT - 1:
                gT = tt
        tcp = None
        for kt in range(KT):
            tcp = self.dve(lambda e, kt=kt: cp(e, T["xnTh_sb"][:, kt * 3:(kt + 1) * 3],
                                                          T["pT0"][:, kt * 4:kt * 4 + 3]),
                           waits=(gT,), inc=(kt == KT - 1), tag=f"halo_evac{kt}")
        self.guard["pT0"] = tcp
        self.tok_xnTh = tcp

    def prologue2(self):
        T = self.t
        t2, t3, _, _ = self.tok_prologue
        last = None
        for c in range(NCHUNK):
            last = self.dve(lambda e, c=c: cp(e, T["c_dtb64"][0:64, c * 8:(c + 1) * 8], T["c_dtb"][0:64, :]),
                            waits=(t3,), inc=(c == NCHUNK - 1), tag=f"dtb64_{c}")
        self.tok_dtb64 = last
        last = None
        for c in range(NCHUNK):
            last = self.dve(lambda e, c=c: cp(e, T["c_negA64"][0:64, c * 8:(c + 1) * 8], T["c_negexpA"][0:64, :]),
                            waits=(t2,), inc=(c == NCHUNK - 1), tag=f"negA64_{c}")
        self.tok_negA64 = last

    def load_attn_weights(self, hg, want_qg, only_qg=False):
        T = self.t
        jobs = []
        if not only_qg:
            for i in range(HPG):
                for j in range(NJ):
                    mt = j * NH + hg * HPG + i
                    loc = j * HPG + i
                    jobs.append((T["Wk_b"][mt * 128:(mt + 1) * 128, :], "wk_sb", loc * KT * 128, KT * 128))
                    jobs.append((T["Wv_b"][mt * 128:(mt + 1) * 128, :], "wv_sb", loc * KT * 128, KT * 128))
        if want_qg:
            for i in range(HPG):
                mt = hg * HPG + i
                jobs.append((T["Wq_b"][mt * 128:(mt + 1) * 128, :], "wq_sb", i * KT * 128, KT * 128))
            for c in range(4):
                jobs.append((T["Wg_r"][hg * 128:(hg + 1) * 128, c * 2048:(c + 1) * 2048], "wg_sb", c * 2048, 2048))
        if not only_qg:
            jobs.append((T["Wba_b"][:, :], "wba_sb", 0, KT * 24))
        last = None
        wsb = self.guard.get("wsball")
        pieces = []
        for n, (src, dstn, c0, w) in enumerate(jobs):
            for half in range(0, w, 512):
                pieces.append((src, dstn, c0, half, min(512, w - half), n))
        base = getattr(self, "_wst_base", 0)
        tls = {}
        for i in range(len(pieces) + 1):
            if i < len(pieces):
                src, dstn, c0, half, hw, n = pieces[i]
                flip = (base + i) % 2
                stg = T["wstage"][:, flip * 512:flip * 512 + hw]
                tls[i] = self.pdma(stg, src[:, half:half + hw],
                                   waits=(self.guard.get(f"wstage{flip}"), wsb),
                                   tag=f"wld{hg}_{only_qg}_{n}_{half}")
            j = i - 1
            if j >= 0:
                src, dstn, c0, half, hw, n = pieces[j]
                flip = (base + j) % 2
                stg = T["wstage"][:, flip * 512:flip * 512 + hw]
                tc = self.pool(lambda e, stg=stg, dstn=dstn, c0=c0, half=half, hw=hw: cp(e, 
                    T[dstn][:, c0 + half:c0 + half + hw], stg), waits=(tls[j], wsb), inc=True,
                    tag=f"wcast{hg}_{only_qg}_{n}_{half}")
                self.guard[f"wstage{flip}"] = tc
                last = tc
        self._wst_base = (base + len(pieces)) % 2
        convjobs = []
        for i in range(HPG):
            if not only_qg:
                for j in range(NJ):
                    g = j * NH + hg * HPG + i
                    loc = j * HPG + i
                    convjobs.append((T["convk"][g * 128:(g + 1) * 128, :], 4 + loc))
                    convjobs.append((T["convv"][g * 128:(g + 1) * 128, :], 12 + loc))
            if want_qg:
                gq = hg * HPG + i
                convjobs.append((T["convq"][gq * 128:(gq + 1) * 128, :], 0 + i))
        tconv = None
        for src, slot in convjobs:
            tconv = self.pdma(T["convw"][:, slot * 4:(slot + 1) * 4], src, waits=(wsb,),
                              tag=f"cw{hg}_{slot}")
        return last, tconv

    def bank_wait(self, bank):
        return tuple(self.guardB.get(bank, ()))

    def bank_use(self, bank, tok):
        self.guardB.setdefault(bank, []).append(tok)

    def bank_clear(self, bank):
        self.guardB[bank] = []

    def ln_tile(self, phase, ti):
        T = self.t
        xsrc = T["x_replay"] if phase[0] == "R" else T["x_own"]
        spill = T["xnTr_sp"] if phase[0] == "R" else T["xnTo_sp"]
        if phase in ("Rb", "Fb"):
            tok = None
            for kt in range(KT):
                tok = self.ldma(T["xnT_sb"][:, kt * TILE:(kt + 1) * TILE],
                                spill[kt * 128:(kt + 1) * 128, ti * TILE:(ti + 1) * TILE],
                                waits=tuple(self.guard.get("xnT") or ()) + (self._spill_gates or ()),
                                tag=f"xnTld{ti}_{kt}")
            self.tok_xnT = (tok,)
            return (tok,)
        st = T["stats"]
        lastev = None
        prev = getattr(self, "_ln_prev", {})
        prev_spill = getattr(self, "tok_xnT_spill", None)
        for g in range(4):
            par = 0
            rows = ti * TILE + g * 128
            xg = T["x_sb"][:, 0:HID]
            tx = self.ldma(xg, xsrc[rows:rows + 128, :],
                           waits=(self.guard.get(f"x{par}"),), tag=f"xld{ti}_{g}")
            xng0 = T["xn_sb"][:, 0:HID]
            t1 = self.act(lambda e, xg=xg, xng0=xng0: e.activation(xng0, xg, AF.Copy, accum_out=st[:, 1:2]),
                          waits=(tx, prev.get("t3"), self.guard.get(f"xn{par}")),
                          inc=True, tag=f"lnsum{ti}_{g}")
            t2 = self.act(lambda e, xg=xg, xng0=xng0: e.activation(xng0, xg, AF.Square, accum_out=st[:, 3:4]),
                          waits=(tx, prev.get("t4")), inc=True, tag=f"lnsq{ti}_{g}")
            t3 = self.dve(lambda e: e.tensor_scalar(st[:, 4:5], st[:, 1:2], 1.0 / HID, None, op0=ALU.mult),
                          waits=(t1,), inc=True, tag=f"lnm{ti}_{g}")
            t4 = self.dve(lambda e: e.tensor_scalar(st[:, 5:6], st[:, 3:4], 1.0 / HID, None, op0=ALU.mult),
                          waits=(t2,), inc=True, tag=f"lnms{ti}_{g}")
            t5 = self.dve(lambda e: e.tensor_tensor(st[:, 6:7], st[:, 4:5], st[:, 4:5], op=ALU.mult),
                          waits=(t3,), inc=True, tag=f"lnm2{ti}_{g}")
            t6 = self.dve(lambda e: e.tensor_tensor(st[:, 7:8], st[:, 5:6], st[:, 6:7], op=ALU.subtract),
                          waits=(t4, t5), inc=True, tag=f"lnvar{ti}_{g}")
            t7 = self.act(lambda e: e.activation(st[:, 8:9], st[:, 7:8], AF.Ln, bias=T["c_epsc"][:, 0:1]),
                          waits=(t6, prev.get("t10")), inc=True, tag=f"lnln{ti}_{g}")
            t8 = self.act(lambda e: e.activation(st[:, 9:10], st[:, 8:9], AF.Exp, scale=-0.5),
                          waits=(t7,), inc=True, tag=f"lnrs{ti}_{g}")
            t9 = self.dve(lambda e: e.tensor_tensor(st[:, 10:11], st[:, 4:5], st[:, 9:10], op=ALU.mult),
                          waits=(t8,), inc=True, tag=f"lnmrs{ti}_{g}")
            xng = T["xn_sb"][:, 0:HID]
            t10 = self.dve(lambda e, xg=xg, xng=xng: e.tensor_scalar(
                xng, xg, st[:, 9:10], st[:, 10:11], op0=ALU.mult, op1=ALU.subtract),
                waits=(t9, self.guard.get(f"xn{par}")), inc=True, tag=f"lnxn{ti}_{g}")
            prev = {"t3": t3, "t4": t4, "t10": t10}
            self.guard[f"x{par}"] = t10
            lastev = {}
            for kg in range(4):
                ptname = "pT0" if kg % 2 == 0 else "pT1"
                pt = T[ptname]
                ttp = None
                for i in range(4):
                    kt = kg * 4 + i
                    ttp = self.pe(lambda e, kt=kt, xng=xng, pt=pt, i=i: e.transpose(
                        pt[:, i * 128:(i + 1) * 128], xng[:, kt * 128:(kt + 1) * 128], T["c_id128b"][:, :]),
                        waits=(t10, self.tok_consts) + self.bank_wait(ptname), inc=(i == 3), tag=f"xtp{ti}_{g}_{kt}")
                self.bank_clear(ptname)
                eng = self.dve if kg % 2 == 0 else self.act
                ev = None
                for i in range(4):
                    kt = kg * 4 + i
                    ev = eng(lambda e, kt=kt, i=i, g=g, pt=pt: cp(e, 
                        T["xnT_sb"][:, kt * TILE + g * 128:kt * TILE + (g + 1) * 128],
                        pt[:, i * 128:(i + 1) * 128]),
                        waits=(ttp, prev_spill) + tuple(self.guard.get("xnT") or ()),
                        inc=(i == 3), tag=f"xev{ti}_{g}_{kt}")
                self.bank_use(ptname, ev)
                lastev[kg % 2] = ev
            self.guard[f"xn{par}"] = lastev[1]
        self._ln_prev = prev
        evtoks = (lastev[0], lastev[1])
        tok = None
        for kt in range(KT):
            tok = self.sdma(spill[kt * 128:(kt + 1) * 128, ti * TILE:(ti + 1) * TILE],
                            T["xnT_sb"][:, kt * TILE:(kt + 1) * TILE],
                            waits=evtoks, tag=f"xnTsp{ti}_{kt}")
        self.tok_xnT = evtoks
        self.tok_xnT_spill = tok
        return evtoks

    def halo_proj(self, hg, wtok):
        T = self.t
        jobs = [("wq_sb", i, i) for i in range(HPG)] + \
               [("wk_sb", l, 4 + l) for l in range(8)] + \
               [("wv_sb", l, 12 + l) for l in range(8)]
        self.tok_halo_pre = {}
        for n, (wname, strip, cht) in enumerate(jobs):
            bank = "pA" if n % 2 == 0 else "pB"
            pb = T[bank]
            tm = None
            for kt in range(KT):
                tm = self.pe(lambda e, wname=wname, strip=strip, kt=kt, pb=pb: e.matmul(
                    pb[:, 0:3],
                    T[wname][:, strip * KT * 128 + kt * 128:strip * KT * 128 + (kt + 1) * 128],
                    T["xnTh_sb"][:, kt * 3:(kt + 1) * 3],
                    start=(kt == 0), stop=(kt == KT - 1)),
                    waits=(wtok, self.tok_xnTh) + self.bank_wait(bank),
                    inc=(kt == KT - 1), tag=f"hproj{cht}_{kt}")
            self.bank_clear(bank)
            tv = self.dve(lambda e, cht=cht, pb=pb: cp(e, 
                T["pre"][:, cht * (TILE + 3):cht * (TILE + 3) + 3], pb[:, 0:3]),
                waits=(tm,), inc=True, tag=f"hpre{cht}")
            self.bank_use(bank, tv)
            self.tok_halo_pre[cht] = tv

    def proj_tile(self, phase, hg, ti, full):
        T = self.t
        txnT = self.tok_xnT
        PRE = TILE + 3
        jobs = [("wba_sb", 0, None)]
        for l in range(8):
            jobs.append(("wk_sb", l, 4 + l))
        for l in range(8):
            jobs.append(("wv_sb", l, 12 + l))
        if full:
            for i in range(HPG):
                jobs.append(("wq_sb", i, 0 + i))
        tok_evac = {}
        for n, (wname, strip, cht) in enumerate(jobs):
            bank = "pA" if n % 2 == 0 else "pB"
            pb = T[bank]
            ncols = 24 if cht is None else 128
            width = TILE
            tm = None
            nout = 24 if cht is None else 128
            for kt in range(KT):
                tm = self.pe(lambda e, wname=wname, strip=strip, kt=kt, pb=pb, ncols=ncols, nout=nout: e.matmul(
                    pb[0:nout, 0:width],
                    T[wname][:, strip * KT * ncols + kt * ncols:strip * KT * ncols + (kt + 1) * ncols],
                    T["xnT_sb"][:, kt * TILE:(kt + 1) * TILE],
                    start=(kt == 0), stop=(kt == KT - 1)),
                    waits=tuple(txnT) + (self.tok_wattn,) + self.bank_wait(bank),
                    inc=(kt == KT - 1), tag=f"proj{ti}_{n}_{kt}")
            self.bank_clear(bank)
            if cht is None:
                tcp = self.dve(lambda e, pb=pb: cp(e, T["bgtmp"][0:24, :], pb[0:24, 0:TILE]),
                               waits=(tm, self.guard.get("bgtmp")), inc=True, tag=f"bgev{ti}")
                self.bank_use(bank, tcp)
                self.emit_bg_panels(ti, tcp)
                self.panels_tile(ti)
            else:
                if ti == 0:
                    if phase[0] == "R":
                        troll = self.dve(lambda e, cht=cht: e.memset(
                            T["pre"][:, cht * PRE:cht * PRE + 3], 0.0),
                            waits=(self.tok_conv_prev.get(cht),), inc=True, tag=f"roll0{cht}")
                    else:
                        troll = self.tok_halo_pre[cht]
                else:
                    troll = self.dve(lambda e, cht=cht: cp(e, 
                        T["pre"][:, cht * PRE:cht * PRE + 3],
                        T["halosave"][:, cht * 3:(cht + 1) * 3]),
                        waits=(self.tok_conv_prev.get(cht), self.guard.get(f"hs{cht}")),
                        inc=True, tag=f"roll{ti}_{cht}")
                eng = self.dve if n % 2 == 0 else self.act
                tev = eng(lambda e, cht=cht, pb=pb: cp(e, 
                    T["pre"][:, cht * PRE + 3:cht * PRE + PRE], pb[:, 0:TILE]),
                    waits=(tm, troll, self.guard.get("postall")), inc=True, tag=f"prev{ti}_{cht}")
                self.bank_use(bank, tev)
                tok_evac[cht] = (tev, troll)
        tok_post = {}
        conv_last = {}
        for cht in sorted(tok_evac):
            tev, troll = tok_evac[cht]
            eng = self.dve
            a_idx = (0, TILE)
            accA = T["cacc"][:, a_idx[0]:a_idx[0] + TILE]
            accB = T["cacc"][:, a_idx[1]:a_idx[1] + TILE]
            w = T["convw"]
            p0 = cht * PRE
            tc = eng(lambda e, p0=p0, cht=cht, accA=accA: e.tensor_scalar(
                accA, T["pre"][:, p0 + 0:p0 + 0 + TILE], w[:, cht * 4 + 0:cht * 4 + 1], None, op0=ALU.mult),
                waits=(tev, self.tok_wconv, self.guard.get(f"cacc{a_idx[0]}")), inc=True, tag=f"cv{ti}_{cht}_0")
            cur, oth = accB, accA
            for d in (1, 2, 3):
                tc = eng(lambda e, p0=p0, cht=cht, d=d, cur=cur, oth=oth: e.scalar_tensor_tensor(
                    cur, T["pre"][:, p0 + d:p0 + d + TILE], w[:, cht * 4 + d:cht * 4 + d + 1], oth,
                    op0=ALU.mult, op1=ALU.add),
                    waits=(tc, self.guard.get(f"cacc{a_idx[1] if d % 2 == 1 else a_idx[0]}"),),
                    inc=True, tag=f"cv{ti}_{cht}_{d}")
                cur, oth = oth, cur
            ths = self.dve(lambda e, cht=cht, p0=p0: cp(e, 
                T["halosave"][:, cht * 3:(cht + 1) * 3],
                T["pre"][:, p0 + TILE:p0 + TILE + 3]),
                waits=(tev, troll), inc=True, tag=f"hsave{ti}_{cht}")
            self.guard[f"hs{cht}"] = ths
            tpost = self.act(lambda e, cht=cht, p0=p0, oth=oth: e.activation(
                T["pre"][:, p0 + 3:p0 + PRE], oth, AF.Silu),
                waits=(tc, ths), inc=True, tag=f"silu{ti}_{cht}")
            self.guard[f"cacc{a_idx[0]}"] = tpost
            self.guard[f"cacc{a_idx[1]}"] = tpost
            conv_last[cht] = tc
            tok_post[cht] = tpost
        self.tok_conv_prev = conv_last
        self.tok_post = tok_post
        if full:
            for g in range(4):
                bank = "pA" if g % 2 == 0 else "pB"
                pb = T[bank]
                tm = None
                for kt in range(KT):
                    tm = self.pe(lambda e, g=g, kt=kt, pb=pb: e.matmul(
                        pb[:, 0:TILE],
                        T["xnT_sb"][:, kt * TILE + g * 128:kt * TILE + (g + 1) * 128],
                        T["wg_sb"][:, kt * 512:(kt + 1) * 512],
                        start=(kt == 0), stop=(kt == KT - 1)),
                        waits=tuple(txnT) + (self.tok_wattn,) + self.bank_wait(bank),
                        inc=(kt == KT - 1), tag=f"gate{ti}_{g}_{kt}")
                self.bank_clear(bank)
                tsl = self.act(lambda e, pb=pb: e.activation(
                    T["gstage"][:, 0:TILE], pb[:, 0:TILE], AF.Silu),
                    waits=(tm, self.guard.get("gst0")), inc=True, tag=f"gsilu{ti}_{g}")
                self.bank_use(bank, tsl)
                rows = ti * TILE + g * 128
                tdma = self.pdma(T["gate_sp"][rows:rows + 128, hg * 512:(hg + 1) * 512],
                                 T["gstage"][:, 0:TILE],
                                 waits=(tsl,), tag=f"gsp{ti}_{g}")
                self.guard["gst0"] = tdma
        self.guard["xnT"] = (tm,) if tm is not None else self.guard.get("xnT")
        return tok_post

    def emit_bg_panels(self, ti, tcp):
        T = self.t
        last = None
        for c in range(NCHUNK):
            tt = self.pe(lambda e, c=c: e.transpose(
                T["pT0"][0:64, 512:536], T["bgtmp"][0:24, c * 64:(c + 1) * 64], T["c_id128b"][0:24, 0:24]),
                waits=(tcp,) + self.bank_wait("pT0"), inc=True, tag=f"bgtp{ti}_{c}")
            self.bank_clear("pT0")
            tev = self.dve(lambda e, c=c: cp(e, 
                T["bg_tm"][0:64, c * 24:(c + 1) * 24], T["pT0"][0:64, 512:536]),
                waits=(tt, self.guard.get("bg_tm")), inc=True, tag=f"bgtm{ti}_{c}")
            self.bank_use("pT0", tev)
            last = tev
        self.guard["bgtmp"] = last
        bg3 = T["bg_tm"][0:64, :].rearrange("p (c w) -> p c w", c=NCHUNK)
        t1 = self.act(lambda e: e.activation(
            T["bgt2"][0:64, :].rearrange("p (c w) -> p c w", c=NCHUNK), bg3[:, :, 0:16], AF.Tanh, scale=0.5),
            waits=(last, self.guard.get("bgt2")), inc=True, tag=f"btanh{ti}")
        t2 = self.dve(lambda e: e.tensor_scalar(
            T["bgt3"][0:64, :], T["bgt2"][0:64, :], -1.0, None, op0=ALU.max),
            waits=(t1, self.guard.get("bgt3")), inc=True, tag=f"bmax{ti}")
        t3 = self.act(lambda e: e.activation(
            T["lnb_tm"][0:64, :], T["bgt3"][0:64, :], AF.Ln, bias=T["c_epsc"][0:64, 2:3]),
            waits=(t2, self.guard.get("lnb_tm")), inc=True, tag=f"blnb{ti}")
        self.tok_lnb = t3
        g1 = self.dve(lambda e: e.tensor_tensor(
            T["gsp_tm"][0:64, :].rearrange("p (c w) -> p c w", c=NCHUNK), bg3[:, :, 16:24],
            T["c_dtb64"][0:64, :].rearrange("p (c w) -> p c w", c=NCHUNK), op=ALU.add),
            waits=(last, self.tok_dtb64, self.guard.get("gsp_tm")), inc=True, tag=f"gy{ti}")
        g2a = self.dve(lambda e: e.tensor_scalar(
            T["bgt3"][0:64, 64:128], T["gsp_tm"][0:64, :], -1.0, None, op0=ALU.mult),
            waits=(g1, t3), inc=True, tag=f"gneg{ti}")
        g2 = self.dve(lambda e: e.tensor_tensor(
            T["bgt2"][0:64, 0:64], T["gsp_tm"][0:64, :], T["bgt3"][0:64, 64:128], op=ALU.max),
            waits=(g2a,), inc=True, tag=f"gabs{ti}")
        g3 = self.act(lambda e: e.activation(T["bgt2"][0:64, 64:128], T["bgt2"][0:64, 0:64], AF.Exp, scale=-1.0),
                      waits=(g2,), inc=True, tag=f"gexp{ti}")
        g4 = self.act(lambda e: e.activation(T["bgt3"][0:64, 0:64], T["bgt2"][0:64, 64:128], AF.Ln,
                                             bias=T["c_epsc"][0:64, 2:3]),
                      waits=(g3, t2), inc=True, tag=f"gln{ti}")
        g5 = self.dve(lambda e: e.tensor_scalar(T["bgt3"][0:64, 64:128], T["gsp_tm"][0:64, :], 0.0, None,
                                                op0=ALU.max),
                      waits=(g1, t2), inc=True, tag=f"gmax{ti}")
        g6 = self.dve(lambda e: e.tensor_tensor(T["bgt2"][0:64, 0:64], T["bgt3"][0:64, 0:64],
                                                T["bgt3"][0:64, 64:128], op=ALU.add),
                      waits=(g4, g5), inc=True, tag=f"gsp{ti}")
        g7 = self.dve(lambda e: e.tensor_tensor(T["gsp_tm"][0:64, :], T["bgt2"][0:64, 0:64],
                                                T["c_negA64"][0:64, :], op=ALU.mult),
                      waits=(g6, self.tok_negA64), inc=True, tag=f"gfin{ti}")
        self.tok_gsp = g7
        self.guard["bg_tm"] = g7
        self.guard["bgt2"] = g7
        self.guard["bgt3"] = g6

    def panels_tile(self, ti):
        T = self.t
        w_a = self.bank_wait("pA")
        m1 = self.pe(lambda e: e.matmul(T["pA"][0:64, 0:64], T["c_trilT"][:, :],
                                        T["gsp_tm"][0:64, 0:64], start=True, stop=True),
                     waits=(self.tok_gsp, self.tok_consts) + w_a, inc=True, tag=f"pgc{ti}")
        m2 = self.pe(lambda e: e.matmul(T["pA"][0:1, 64:128], T["c_trilT"][:, 63:64],
                                        T["gsp_tm"][0:64, 0:64], start=True, stop=True),
                     waits=(self.tok_gsp,), inc=True, tag=f"pgt{ti}")
        self.bank_clear("pA")
        e1 = self.dve(lambda e: cp(e, T["gcum_all"][0:64, :], T["pA"][0:64, 0:64]),
                      waits=(m2,), inc=True, tag=f"pgce{ti}")
        e2 = self.dve(lambda e: cp(e, T["trow2"][0:1, 0:64], T["pA"][0:1, 64:128]),
                      waits=(m2, self.guard.get("trow2")), inc=True, tag=f"pgte{ti}")
        m3 = self.pe(lambda e: e.matmul(T["pA"][:, 128:192], T["c_ones1f"][:, :],
                                        T["trow2"][0:1, 0:64], start=True, stop=True),
                     waits=(e1, e2), inc=True, tag=f"pgb{ti}")
        e3b = self.dve(lambda e: cp(e, T["gCl_all"][:, :], T["pA"][:, 128:192]),
                       waits=(m3,), inc=True, tag=f"pgcl{ti}")
        e3 = self.act(lambda e: e.activation(T["Lam_all"][:, :], T["gCl_all"][:, :], AF.Exp),
                      waits=(e3b, self.guard.get("LamA")), inc=True, tag=f"plam{ti}")
        self.bank_use("pA", e1)
        self.bank_use("pA", e2)
        self.bank_use("pA", e3b)
        self.dve(lambda e: cp(e, T["gce_all"][0:64, :], T["gcum_all"][0:64, :]),
                 waits=(e1,), inc=True, tag=f"pgcea{ti}")
        self.dve(lambda e: cp(e, T["gce_all"][64:128, :], T["gcum_all"][0:64, :]),
                 waits=(e1,), inc=True, tag=f"pgceb{ti}")
        self.guard["trow2"] = m3
        self.guard["gsp_tm"] = (m1, m2)
        self.tok_lamA = e3

    def chunk_front(self, phase, hg, ti, c, full):
        T = self.t
        PRE = TILE + 3
        c64 = c * CT
        par = c % 2
        kj = [[T["pre"][:, (4 + j * 4 + hl) * PRE + 3 + c64:(4 + j * 4 + hl) * PRE + 3 + c64 + CT]
               for j in range(2)] for hl in range(HPG)]
        qh = [T["pre"][:, hl * PRE + 3 + c64:hl * PRE + 3 + c64 + CT] for hl in range(HPG)]
        tk = [[self.tok_post[4 + j * 4 + hl] for j in range(2)] for hl in range(HPG)]
        tv = [[self.tok_post[12 + j * 4 + hl] for j in range(2)] for hl in range(HPG)]
        tq = [self.tok_post[hl] if full else None for hl in range(HPG)]
        vj = [[T["pre"][:, (12 + j * 4 + hl) * PRE + 3 + c64:(12 + j * 4 + hl) * PRE + 3 + c64 + CT]
               for j in range(2)] for hl in range(HPG)]
        p1 = T["pan1"]
        p2 = T["pan2"]
        n0N = "nN_a" if par == 0 else "nN_b"
        n0T = "nT_a" if par == 0 else "nT_b"

        pb = c * 8 + hg * 4
        tok_lnbe = None
        for hl in range(HPG):
            c0 = c * 16 + hg * 4 + hl
            c1 = c * 16 + 8 + hg * 4 + hl
            self.dve(lambda e, hl=hl, c0=c0: cp(e, p1[0:64, 16 + hl:17 + hl], T["lnb_tm"][0:64, c0:c0 + 1]),
                     waits=(self.tok_lnb, self.guard.get("pan1")), inc=False, tag=f"lnbe0{ti}_{c}_{hl}")
            tok_lnbe = self.dve(lambda e, hl=hl, c1=c1: cp(e, p1[64:128, 16 + hl:17 + hl],
                                                           T["lnb_tm"][0:64, c1:c1 + 1]),
                                waits=(self.tok_lnb,), inc=(hl == HPG - 1), tag=f"lnbe1{ti}_{c}_{hl}")

        w_t1 = self.bank_wait("pT1")
        t_tp = None
        for hl in range(HPG):
            for j in range(2):
                t_tp = self.pe(lambda e, hl=hl, j=j: e.transpose(
                    T["pT1"][j * 64:(j + 1) * 64, hl * 128:(hl + 1) * 128], kj[hl][j], T["c_id128b"][:, :]),
                    waits=(tk[hl][j],) + w_t1, inc=False, tag=f"tk{c}_{hl}_{j}")
        if full:
            for hl in range(HPG):
                t_tp = self.pe(lambda e, hl=hl: e.transpose(
                    T["pT1"][0:64, 512 + hl * 128:512 + (hl + 1) * 128], qh[hl], T["c_id128b"][:, :]),
                    waits=(tq[hl],), inc=False, tag=f"tq{c}_{hl}")
        self.bank_clear("pT1")
        w_t0 = self.bank_wait("pT0")
        t_tv = None
        for hl in range(HPG):
            for j in range(2):
                t_tv = self.pe(lambda e, hl=hl, j=j: e.transpose(
                    T["pT0"][j * 64:(j + 1) * 64, hl * 128:(hl + 1) * 128], vj[hl][j], T["c_id128b"][:, :]),
                    waits=(tv[hl][j],) + w_t0, inc=(hl == HPG - 1 and j == 1), tag=f"tv{c}_{hl}_{j}")
        self.bank_clear("pT0")
        t_tp = self.pe(lambda e: e.nop(), waits=(), inc=True, tag=f"tpfin{c}")

        ksq = T["dump"][:, 1024:1536]
        qsq = T["dump"][0:64, 1536:2048]
        t_ksq = self.act(lambda e, ksq=ksq: e.activation(ksq, T["pT1"][:, 0:512], AF.Square),
                         waits=(t_tp, self.guard.get("ksq")), inc=True, tag=f"ksq{c}")
        t_kss = self.dve(lambda e, ksq=ksq: e.tensor_reduce(T["kss"][:, 0:4],
                                                            ksq.rearrange("p (h f) -> p h f", h=4),
                                                            axis=mybir.AxisListType.X, op=ALU.add),
                         waits=(t_ksq, self.guard.get("kss")), inc=True, tag=f"kss{c}")
        self.guard["ksq"] = t_kss
        if full:
            t_qsq = self.act(lambda e, qsq=qsq: e.activation(qsq, T["pT1"][0:64, 512:1024], AF.Square),
                             waits=(t_tp, self.guard.get("qsq")), inc=True, tag=f"qsq{c}")
            t_qss = self.dve(lambda e, qsq=qsq: e.tensor_reduce(T["qss"][0:64, 0:4],
                                                                qsq.rearrange("p (h f) -> p h f", h=4),
                                                                axis=mybir.AxisListType.X, op=ALU.add),
                             waits=(t_qsq, self.guard.get("qss")), inc=True, tag=f"qss{c}")
            self.guard["qsq"] = t_qss
            self.bank_use("pT1", t_qsq)
        t_vev = self.dve(lambda e, par=par: cp(e, T["vtm_sb"][:, par * 512:par * 512 + 512],
                                               T["pT0"][:, 0:512]),
                         waits=(t_tv, self.guard.get(f"vtm{par}")), inc=True, tag=f"vev{c}")
        self.bank_use("pT0", t_vev)
        self.bank_use("pT1", t_ksq)
        t_lnk = self.act(lambda e: e.activation(p1[:, 8:12], T["kss"][:, 0:4], AF.Ln,
                                                bias=T["c_epsc"][:, 1:2]),
                         waits=(t_kss, self.guard.get("p1_8")), inc=True, tag=f"lnk{c}")
        self.guard["kss"] = t_lnk
        t_lnrn = self.dve(lambda e: e.tensor_scalar(p1[:, 20:24], p1[:, 8:12], -0.5, None, op0=ALU.mult),
                          waits=(t_lnk,), inc=True, tag=f"lnrn{c}")
        t_a = self.dve(lambda e, pb=pb: e.tensor_tensor(p1[:, 4:8], T["gce_all"][:, pb:pb + 4],
                                                        p1[:, 20:24], op=ALU.add),
                       waits=(t_lnrn, self.guard.get("pan1")), inc=True, tag=f"a{c}")
        t_nb1 = self.dve(lambda e: e.tensor_tensor(p1[:, 8:12], p1[:, 20:24], p1[:, 16:20], op=ALU.add),
                         waits=(t_lnrn, tok_lnbe), inc=True, tag=f"nb1{c}")
        t_nb = self.dve(lambda e, pb=pb: e.tensor_tensor(p1[:, 12:16], p1[:, 8:12],
                                                         T["gce_all"][:, pb:pb + 4], op=ALU.subtract),
                        waits=(t_nb1, self.guard.get("p1_12")), inc=True, tag=f"nb{c}")
        t_kpe = self.dve(lambda e, pb=pb: e.tensor_tensor(p1[:, 0:4], p1[:, 12:16],
                                                          T["gCl_all"][:, pb:pb + 4], op=ALU.add),
                         waits=(t_nb, self.guard.get("pan1")), inc=True, tag=f"kpe{c}")
        t_exp = self.act(lambda e, par=par: e.activation(T["panx"][:, par * 8:par * 8 + 8], p1[:, 0:8], AF.Exp),
                         waits=(t_kpe, t_a, self.guard.get(f"panx{par}")), inc=True, tag=f"exp{c}")
        self._t_exp = t_exp
        if full:
            t_lnq = self.act(lambda e: e.activation(p2[0:64, 4:8], T["qss"][0:64, 0:4], AF.Ln,
                                                    bias=T["c_epsc"][0:64, 1:2]),
                             waits=(t_qss, self.guard.get("p2_8")), inc=True, tag=f"lnq{c}")
            t_aq1 = self.dve(lambda e: e.tensor_scalar(p2[0:64, 8:12], p2[0:64, 4:8], -0.5, -0.5 * LN128,
                                                       op0=ALU.mult, op1=ALU.add),
                             waits=(t_lnq,), inc=True, tag=f"aq1{c}")
            t_aq = self.dve(lambda e, pb=pb: e.tensor_tensor(p2[0:64, 0:4], p2[0:64, 8:12],
                                                             T["gcum_all"][0:64, pb:pb + 4], op=ALU.add),
                            waits=(t_aq1, self.guard.get("p2_04")), inc=True, tag=f"aq{c}")
            t_lq = self.act(lambda e, par=par: e.activation(p2[0:64, par * 16 + 12:par * 16 + 16],
                                                            p2[0:64, 0:4], AF.Exp),
                            waits=(t_aq, self.guard.get(f"lamq{par}")), inc=True, tag=f"lamq{c}")
            self._t_lq = t_lq
            self.guard["p2_8"] = t_aq
            self.guard["p2_04"] = t_lq
        t_k2 = None
        for hl in range(HPG):
            t_k2 = self.dve(lambda e, hl=hl, par=par: e.tensor_scalar(
                T["ktm_sb"][:, par * 512 + hl * 128:par * 512 + (hl + 1) * 128],
                T["pT1"][:, hl * 128:(hl + 1) * 128],
                T["panx"][:, par * 8 + hl:par * 8 + hl + 1], -1.0, op0=ALU.mult, op1=ALU.mult),
                waits=(t_exp, t_tp, self.guard.get(f"ktm{par}")), inc=(hl == HPG - 1), tag=f"k2{c}_{hl}")
        self.bank_use("pT1", t_k2)
        self._t_k2 = t_k2

        w_e = self.bank_wait("pE")
        t_g = None
        for hl in range(HPG):
            for a in range(2):
                for b in range(2):
                    t_g = self.pe(lambda e, hl=hl, a=a, b=b: e.matmul(
                        T["pE"][a * 64:(a + 1) * 64, hl * 128 + b * 64:hl * 128 + (b + 1) * 64],
                        kj[hl][a], kj[hl][b], start=True, stop=True),
                        waits=(tk[hl][0], tk[hl][1]) + w_e,
                        inc=(hl == HPG - 1 and a == 1 and b == 1), tag=f"g{c}_{hl}_{a}{b}")
        self.bank_clear("pE")
        w_f = self.bank_wait("pF")
        t_bt = None
        for hl in range(HPG):
            t_bt = self.pe(lambda e, hl=hl: e.transpose(
                T["pA"][0:1, hl * 128:(hl + 1) * 128], p1[:, 12 + hl:13 + hl], T["c_id128f"][:, :]),
                waits=(t_nb,) + self.bank_wait("pA"), inc=(hl == HPG - 1), tag=f"bt{c}_{hl}")
        self.bank_clear("pA")
        t_br = self.dve(lambda e: cp(e, T["c_rows"][0:1, 256:768], T["pA"][0:1, 0:512]),
                        waits=(t_bt, self.guard.get("negbT")), inc=True, tag=f"br{c}")
        self.bank_use("pA", t_br)
        t_bc = None
        for hl in range(HPG):
            t_bc = self.pe(lambda e, hl=hl: e.matmul(
                T["pF"][:, hl * 128:(hl + 1) * 128], T["c_ones1f"][:, :],
                T["c_rows"][0:1, 256 + hl * 128:256 + (hl + 1) * 128], start=True, stop=True),
                waits=(t_br,) + w_f, inc=(hl == HPG - 1), tag=f"bc{c}_{hl}")
        self.bank_clear("pF")
        self.guard["negbT"] = t_bc
        self.guard["p1_12"] = t_bt

        t_D = None
        for hl in range(HPG):
            t_D = self.dve(lambda e, hl=hl: e.tensor_scalar(
                T["D_sb"][:, hl * 128:(hl + 1) * 128], T["pF"][:, hl * 128:(hl + 1) * 128],
                p1[:, 4 + hl:5 + hl], 60.0, op0=ALU.add, op1=ALU.min),
                waits=(t_bc, t_a, self.guard.get("Dsb")), inc=(hl == HPG - 1), tag=f"D{c}_{hl}")
        self.bank_use("pF", t_D)
        t_E = self.act(lambda e: e.activation(T["dump"][:, 1024:1536], T["D_sb"][:, 0:512], AF.Exp),
                       waits=(t_D, t_kss), inc=True, tag=f"E{c}")
        self.guard["Dsb"] = t_E
        t_Em = self.dve(lambda e: e.tensor_tensor(T["dump"][:, 0:512], T["dump"][:, 1024:1536],
                                                  T["c_maskM"][:, :], op=ALU.mult),
                        waits=(t_E, self.guard.get("dumpA")), inc=True, tag=f"Em{c}")
        t_N0 = self.dve(lambda e, n0N=n0N: e.tensor_tensor(T[n0N][:, 0:512], T["dump"][:, 0:512],
                                                           T["pE"][:, 0:512], op=ALU.mult),
                        waits=(t_Em, t_g, self.guard.get(f"n0N{par}")), inc=True, tag=f"N0{c}")
        self.bank_use("pE", t_N0)
        self.guard["ksq"] = t_Em
        self.guard["dumpA"] = t_N0
        t_nt0 = None
        for hl in range(HPG):
            t_nt0 = self.pe(lambda e, hl=hl, n0N=n0N: e.transpose(
                T["pT0"][:, 512 + hl * 128:512 + (hl + 1) * 128],
                T[n0N][:, hl * 128:(hl + 1) * 128], T["c_id128b"][:, :]),
                waits=(t_N0,) + self.bank_wait("pT0"), inc=(hl == HPG - 1), tag=f"nt0{c}_{hl}")
        self.bank_clear("pT0")
        t_nt0e = self.act(lambda e, n0T=n0T: e.copy(T[n0T][:, 0:512], T["pT0"][:, 512:1024]),
                          waits=(t_nt0, self.guard.get(f"n0T{par}")), inc=True, tag=f"nt0e{c}")
        self.bank_use("pT0", t_nt0e)
        self._t_lvl0 = (t_nt0e, t_N0)

        if full:
            w_f2 = self.bank_wait("pF")
            t_ar = None
            for hl in range(HPG):
                for j in range(2):
                    t_ar = self.pe(lambda e, hl=hl, j=j: e.matmul(
                        T["pF"][j * 64:(j + 1) * 64, hl * 64:(hl + 1) * 64], kj[hl][j], qh[hl],
                        start=True, stop=True),
                        waits=(tk[hl][j], tq[hl]) + w_f2,
                        inc=(hl == HPG - 1 and j == 1), tag=f"ar{c}_{hl}_{j}")
            t_at = None
            for hl in range(HPG):
                t_at = self.pe(lambda e, hl=hl: e.transpose(
                    T["pA"][0:1, hl * 64:(hl + 1) * 64], p2[0:64, hl:hl + 1], T["c_id64f"][:, :]),
                    waits=(t_aq,) + self.bank_wait("pA"), inc=(hl == HPG - 1), tag=f"at{c}_{hl}")
            self.bank_clear("pA")
            t_arr = self.dve(lambda e: cp(e, T["trow"][0:1, 0:256], T["pA"][0:1, 0:256]),
                             waits=(t_at, self.guard.get("trow4")), inc=True, tag=f"arr{c}")
            self.bank_use("pA", t_arr)
            t_ab = None
            for hl in range(HPG):
                t_ab = self.pe(lambda e, hl=hl: e.matmul(
                    T["pF"][:, 256 + hl * 64:256 + (hl + 1) * 64], T["c_ones1f"][:, :],
                    T["trow"][0:1, hl * 64:(hl + 1) * 64], start=True, stop=True),
                    waits=(t_arr,), inc=(hl == HPG - 1), tag=f"ab{c}_{hl}")
            self.bank_clear("pF")
            self.guard["trow4"] = t_ab
            self.guard["p2_04"] = (t_lq, t_at)
            t_D3 = None
            for hl in range(HPG):
                t_D3 = self.dve(lambda e, hl=hl: e.tensor_scalar(
                    T["D_sb"][:, hl * 64:(hl + 1) * 64], T["pF"][:, 256 + hl * 64:256 + (hl + 1) * 64],
                    p1[:, 12 + hl:13 + hl], 60.0, op0=ALU.add, op1=ALU.min),
                    waits=(t_ab, t_nb, t_E), inc=(hl == HPG - 1), tag=f"D3{c}_{hl}")
            self.bank_use("pF", t_D3)
            t_Eq = self.act(lambda e: e.activation(T["Eq_sb"][:, 0:256], T["D_sb"][:, 0:256], AF.Exp),
                            waits=(t_D3, self.guard.get("Eq")), inc=True, tag=f"Eq{c}")
            self.guard["Dsb"] = t_Eq
            t_Eqm = self.dve(lambda e: e.tensor_tensor(T["dump"][:, 512:768], T["Eq_sb"][:, 0:256],
                                                       T["c_maskAT"][:, :], op=ALU.mult),
                             waits=(t_Eq, self.guard.get("dumpB")), inc=True, tag=f"Eqm{c}")
            t_am = self.dve(lambda e, par=par: e.tensor_tensor(
                T["am_sb"][:, par * 256:par * 256 + 256], T["dump"][:, 512:768],
                T["pF"][:, 0:256], op=ALU.mult),
                waits=(t_Eqm, t_ar, self.guard.get(f"am{par}")), inc=True, tag=f"am{c}")
            self.bank_use("pF", t_am)
            self.guard["Eq"] = t_am
            self.guard["dumpB"] = t_am
            self._t_am = t_am
        self.guard["p1_8"] = t_nb
        self.guard["pan1"] = t_exp
        self._ck[c] = dict(t_exp=t_exp, t_lvl0=self._t_lvl0, t_vev=t_vev, t_k2=t_k2,
                           t_lam=self.tok_lamA,
                           t_am=(self._t_am if full else None),
                           t_lq=(self._t_lq if full else None),
                           t_lnbe=tok_lnbe)

    def chunk_back(self, phase, hg, ti, c, full):
        T = self.t
        PRE = TILE + 3
        c64 = c * CT
        par = c % 2
        kj = [[T["pre"][:, (4 + j * 4 + hl) * PRE + 3 + c64:(4 + j * 4 + hl) * PRE + 3 + c64 + CT]
               for j in range(2)] for hl in range(HPG)]
        qh = [T["pre"][:, hl * PRE + 3 + c64:hl * PRE + 3 + c64 + CT] for hl in range(HPG)]
        tk = [[self.tok_post[4 + j * 4 + hl] for j in range(2)] for hl in range(HPG)]
        tq = [self.tok_post[hl] if full else None for hl in range(HPG)]
        heads = [hg * HPG + hl for hl in range(HPG)]
        ping = self.s_ping[heads[0]]
        S0 = [T["s_bf"][:, (h * 2 + ping) * 128:(h * 2 + ping) * 128 + 128] for h in heads]
        Snx = [T["s_bf"][:, (h * 2 + 1 - ping) * 128:(h * 2 + 1 - ping) * 128 + 128] for h in heads]
        tok_s_in = tuple(self.tok_s.get(h) for h in heads)
        p2 = T["pan2"]
        n0N = "nN_a" if par == 0 else "nN_b"
        n0T = "nT_a" if par == 0 else "nT_b"
        t_exp = self._ck[c]["t_exp"]
        t_lvl0 = self._ck[c]["t_lvl0"]
        t_vev = self._ck[c]["t_vev"]
        t_k2 = self._ck[c]["t_k2"]
        t_lam = self._ck[c]["t_lam"]

        w_x = self.bank_wait("pD")
        t_x = None
        for hl in range(HPG):
            for j in range(2):
                t_x = self.pe(lambda e, hl=hl, j=j: e.matmul(
                    T["pD"][j * 64:(j + 1) * 64, hl * 128:(hl + 1) * 128], kj[hl][j], S0[hl],
                    start=True, stop=True),
                    waits=(tk[hl][j], tok_s_in[hl]) + w_x,
                    inc=(hl == HPG - 1 and j == 1), tag=f"x{c}_{hl}_{j}")
        self.bank_clear("pD")
        t_u0 = None
        for hl in range(HPG):
            t_u0 = self.dve(lambda e, hl=hl, par=par: e.scalar_tensor_tensor(
                T["ucp"][:, hl * 128:(hl + 1) * 128], T["pD"][:, hl * 128:(hl + 1) * 128],
                T["panx"][:, par * 8 + 4 + hl:par * 8 + 5 + hl],
                T["vtm_sb"][:, par * 512 + hl * 128:par * 512 + (hl + 1) * 128],
                op0=ALU.mult, op1=ALU.subtract),
                waits=(t_x, t_exp, t_vev, self.guard.get("ucp0")), inc=(hl == HPG - 1), tag=f"u0{c}_{hl}")
        self.bank_use("pD", t_u0)

        nLN = {0: "nLN_a", 1: "nLN_b"}
        nLT = {0: "nLT_a", 1: "nLT_b"}
        def lvlbuf(k):
            if k == 0:
                return n0N, n0T
            return nLN[(k + 1) % 2], nLT[(k + 1) % 2]
        t_lvl = {0: t_lvl0}
        t_ucur = t_u0
        t_ap_hist = {}
        for k in range(7):
            kpar = k % 2
            knpar = (k + 1) % 2
            ucur0 = kpar * 512
            unext0 = knpar * 512
            bN, bT = lvlbuf(k)
            w_u = self.bank_wait("pC")
            t_mm = None
            for hl in range(HPG):
                t_mm = self.pe(lambda e, hl=hl, bT=bT, ucur0=ucur0: e.matmul(
                    T["pC"][:, hl * 128:(hl + 1) * 128],
                    T[bT][:, hl * 128:(hl + 1) * 128],
                    T["ucp"][:, ucur0 + hl * 128:ucur0 + (hl + 1) * 128],
                    start=True, stop=True),
                    waits=(t_ucur,) + t_lvl[k] + w_u, inc=(hl == HPG - 1), tag=f"ap{c}_{hl}_{k}")
            self.bank_clear("pC")
            t_ap_hist[k] = t_mm
            t_ucur = self.dve(lambda e, ucur0=ucur0, unext0=unext0: e.tensor_tensor(
                T["ucp"][:, unext0:unext0 + 512], T["ucp"][:, ucur0:ucur0 + 512],
                T["pC"][:, 0:512], op=ALU.add),
                waits=(t_mm, self.guard.get(f"ucp{knpar}")), inc=True, tag=f"uacc{c}_{k}")
            self.bank_use("pC", t_ucur)
            self.guard[f"ucp{kpar}"] = t_ucur
            if k < 6:
                dN, dT = lvlbuf(k + 1)
                w_d = self.bank_wait("pD")
                w_b = self.bank_wait("pB")
                dst_guard = (t_mm, self.guard.get("nL") if k > 0 else None)
                t_s1 = t_s2 = None
                if k < 5:
                    for hl in range(HPG):
                        t_s1 = self.pe(lambda e, hl=hl, bN=bN, bT=bT: e.matmul(
                            T["pD"][:, hl * 128:(hl + 1) * 128],
                            T[bT][:, hl * 128:(hl + 1) * 128],
                            T[bN][:, hl * 128:(hl + 1) * 128], start=True, stop=True),
                            waits=t_lvl[k] + w_d, inc=(hl == HPG - 1), tag=f"sqN{c}_{hl}_{k}")
                    for hl in range(HPG):
                        t_s2 = self.pe(lambda e, hl=hl, bN=bN, bT=bT: e.matmul(
                            T["pB"][:, hl * 128:(hl + 1) * 128],
                            T[bN][:, hl * 128:(hl + 1) * 128],
                            T[bT][:, hl * 128:(hl + 1) * 128], start=True, stop=True),
                            waits=t_lvl[k] + w_b, inc=(hl == HPG - 1), tag=f"sqT{c}_{hl}_{k}")
                    self.bank_clear("pD")
                    self.bank_clear("pB")
                    t_evN = self.dve(lambda e, dN=dN: cp(e, T[dN][:, 0:512], T["pD"][:, 0:512]),
                                     waits=(t_s1,) + dst_guard, inc=True, tag=f"sqeN{c}_{k}")
                    self.bank_use("pD", t_evN)
                    t_evT = self.act(lambda e, dT=dT: e.copy(T[dT][:, 0:512], T["pB"][:, 0:512]),
                                     waits=(t_s2,) + dst_guard, inc=True, tag=f"sqeT{c}_{k}")
                    self.bank_use("pB", t_evT)
                    t_lvl[k + 1] = (t_evN, t_evT)
                else:
                    for hl in range(HPG):
                        t_s1 = self.pe(lambda e, hl=hl, bN=bN, bT=bT: e.matmul(
                            T["pD"][:, hl * 128:(hl + 1) * 128],
                            T[bN][:, hl * 128:(hl + 1) * 128],
                            T[bT][:, hl * 128:(hl + 1) * 128], start=True, stop=True),
                            waits=t_lvl[k] + w_d, inc=(hl == HPG - 1), tag=f"sqT6{c}_{hl}")
                    self.bank_clear("pD")
                    t_ev6 = self.dve(lambda e, dT=dT: cp(e, T[dT][:, 0:512], T["pD"][:, 0:512]),
                                     waits=(t_s1,) + dst_guard, inc=True, tag=f"sqe6{c}")
                    self.bank_use("pD", t_ev6)
                    t_lvl[6] = (t_ev6,)
                self.guard["nL"] = t_mm
        t_uf = t_ucur

        if full:
            t_am = self._ck[c]["t_am"]
            t_lq = self._ck[c]["t_lq"]
            w_d2 = self.bank_wait("pD")
            t_o1 = None
            for hl in range(HPG):
                t_o1 = self.pe(lambda e, hl=hl: e.matmul(
                    T["pD"][0:64, hl * 128:(hl + 1) * 128], qh[hl], S0[hl], start=True, stop=True),
                    waits=(tq[hl], tok_s_in[hl]) + w_d2, inc=(hl == HPG - 1), tag=f"o1{c}_{hl}")
            self.bank_clear("pD")
            w_c2 = self.bank_wait("pC")
            t_o2 = None
            for hl in range(HPG):
                t_o2 = self.pe(lambda e, hl=hl, par=par: e.matmul(
                    T["pC"][0:64, hl * 128:(hl + 1) * 128],
                    T["am_sb"][:, par * 256 + hl * 64:par * 256 + (hl + 1) * 64],
                    T["ucp"][:, 512 + hl * 128:512 + (hl + 1) * 128], start=True, stop=True),
                    waits=(t_am, t_uf) + w_c2, inc=(hl == HPG - 1), tag=f"o2{c}_{hl}")
            self.bank_clear("pC")
            t_ot1 = None
            for hl in range(HPG):
                t_ot1 = self.dve(lambda e, hl=hl, par=par: e.tensor_scalar(
                    T["ot1w"][0:64, hl * 128:(hl + 1) * 128], T["pD"][0:64, hl * 128:(hl + 1) * 128],
                    p2[0:64, par * 16 + 12 + hl:par * 16 + 13 + hl], None, op0=ALU.mult),
                    waits=(t_o1, t_lq, self.guard.get("ot1w")), inc=(hl == HPG - 1), tag=f"ot1{c}_{hl}")
            self.bank_use("pD", t_ot1)
            orow = (c % 2) * 64
            ocol = ((c // 2) % 2) * 512
            t_o = self.dve(lambda e, orow=orow, ocol=ocol: e.tensor_tensor(
                T["otile"][orow:orow + 64, ocol:ocol + 512],
                T["ot1w"][0:64, 0:512], T["pC"][0:64, 0:512], op=ALU.add),
                waits=(t_ot1, t_o2, self.guard.get(f"otile{(c // 2) % 2}")), inc=True, tag=f"o{c}")
            self.bank_use("pC", t_o)
            self.guard["ot1w"] = t_o
            self.tok_otile[c] = t_o
            self.guard[f"lamq{par}"] = t_o

        w_s = self.bank_wait("pB")
        t_sn = None
        for hl in range(HPG):
            t_sn = self.pe(lambda e, hl=hl, par=par: e.matmul(
                T["pB"][:, hl * 128:(hl + 1) * 128],
                T["ktm_sb"][:, par * 512 + hl * 128:par * 512 + (hl + 1) * 128],
                T["ucp"][:, 512 + hl * 128:512 + (hl + 1) * 128], start=True, stop=True),
                waits=(t_k2, t_uf) + w_s, inc=(hl == HPG - 1), tag=f"sn{c}_{hl}")
        self.bank_clear("pB")
        t_s = None
        for hl in range(HPG):
            t_s = self.dve(lambda e, hl=hl, pb=c * 8 + hg * 4: e.scalar_tensor_tensor(
                Snx[hl], S0[hl], T["Lam_all"][:, pb + hl:pb + hl + 1],
                T["pB"][:, hl * 128:(hl + 1) * 128], op0=ALU.mult, op1=ALU.add),
                waits=(t_sn, t_lam), inc=True, tag=f"s{c}_{hl}")
            self.tok_s[heads[hl]] = t_s
            self.s_ping[heads[hl]] = 1 - self.s_ping[heads[hl]]
        self.bank_use("pB", t_s)
        self.guard[f"n0N{par}"] = t_lvl[1]
        self.guard[f"n0T{par}"] = (t_ap_hist[0],) + t_lvl[1]
        self.guard["ucp0"] = t_uf
        self.guard["ucp1"] = (t_sn, t_o2) if full else t_sn
        self.guard[f"vtm{par}"] = t_u0
        self.guard[f"ktm{par}"] = t_sn
        self.guard[f"panx{par}"] = (t_u0, t_k2)
        if full:
            self.guard[f"am{par}"] = t_o2

    def chunks_tile(self, phase, hg, ti, full):
        T = self.t
        self.tok_otile = {}
        self._ck = {}
        NC = getattr(self, "nchunk_dbg", None) or NCHUNK
        self.chunk_front(phase, hg, ti, 0, full)
        for c in range(NC):
            if c + 1 < NC:
                self.chunk_front(phase, hg, ti, c + 1, full)
            self.chunk_back(phase, hg, ti, c, full)
            if full and c % 2 == 1:
                m = c // 2
                rows = ti * TILE + m * 128
                w = (self.tok_otile.get(c - 1), self.tok_otile.get(c))
                tsp = self.sdma(T["o_sp"][rows:rows + 128, hg * 512:(hg + 1) * 512],
                                T["otile"][:, (m % 2) * 512:(m % 2) * 512 + 512],
                                waits=w, tag=f"osp{ti}_{m}")
                self.guard[f"otile{m % 2}"] = tsp
        hmax = hg * HPG + HPG - 1
        self.guard["postall"] = (self.tok_s[hmax], self.tok_otile.get(NC - 1))
        self.guard["LamA"] = self.tok_s[hmax]
        self.guard["lnb_tm"] = self._ck[NC - 1]["t_lnbe"]

    def attn_phase(self, phase):
        T = self.t
        hg = 0 if phase.endswith("a") else 1
        full = phase[0] == "F"
        self._spill_gates = self.kb.dma_barrier("dact")
        if full and getattr(self, "_pref_hg", None) == hg:
            wtok, ctok = self._pref_wtok, self._pref_ctok
        elif (not full) and getattr(self, "_kv_hg", None) == hg:
            wtok, ctok = self._kv_wtok, self._kv_ctok
        else:
            wtok, ctok = self.load_attn_weights(hg, want_qg=full)
            ctok = self.pool(lambda e: e.memset(T["stats"][0:1, 0:1], 0.0),
                             waits=(ctok,), inc=True, tag=f"cvfence{phase}")
        self.tok_wattn = wtok
        self.tok_wconv = ctok
        if not full and getattr(self, "_pref_hg", None) != hg:
            pw, pc = self.load_attn_weights(hg, want_qg=True, only_qg=True)
            pc = self.pool(lambda e: e.memset(T["stats"][0:1, 0:1], 0.0),
                           waits=(pc,), inc=True, tag=f"pcfence{phase}")
            self._pref_wtok = (wtok, pw)
            self._pref_ctok = (ctok, pc)
            self._pref_hg = hg
        if phase[0] == "R":
            for hl in range(HPG):
                h = hg * HPG + hl
                self.s_ping[h] = 0
                self.tok_s[h] = self.dve(lambda e, h=h: e.memset(
                    T["s_bf"][:, h * 256:h * 256 + 128], 0.0),
                    waits=(self.tok_s.get(h),), inc=True, tag=f"sinit{phase}_{h}")
        else:
            for hl in range(HPG):
                h = hg * HPG + hl
                self.s_ping[h] = 0
                if self.tok_srep.get(h) is None:
                    self.tok_srep[h] = self.dve(lambda e, h=h: e.memset(
                        T["srep_bf"][:, (h % 4) * 128:(h % 4) * 128 + 128], 0.0), inc=True,
                        tag=f"srep0{phase}_{h}")
                self.tok_s[h] = self.dve(lambda e, h=h: e.tensor_scalar(
                    T["s_bf"][:, h * 256:h * 256 + 128],
                    T["srep_bf"][:, (h % 4) * 128:(h % 4) * 128 + 128],
                    T["c_flag"][:, 0:1], None, op0=ALU.mult),
                    waits=(self.tok_s.get(h), self.tok_srep.get(h), self.tok_consts), inc=True,
                    tag=f"sinitF{phase}_{h}")
        if full:
            self.halo_proj(hg, wtok)
        self.ln_tile(phase, 0)
        for ti in range(self.ntile):
            self.proj_tile(phase, hg, ti, full)
            if ti + 1 < self.ntile:
                self.ln_tile(phase, ti + 1)
            if phase == "Fa" and ti == self.ntile - 1 and "Rb" in self.do_phases:
                kb = self.kb
                snap = tuple((f"pe{i}", kb.cnt[f"pe{i}"]) for i in range(kb.NSEM_PER_Q)
                             if kb.cnt[f"pe{i}"] > 0)
                fence = self.dve(lambda e: e.memset(T["stats"][0:1, 11:12], 0.0),
                                 waits=(snap,), inc=True, tag="whfence")
                self.guard["wsball"] = fence
                nw, nc2 = self.load_attn_weights(1, want_qg=False)
                pw, pc2 = self.load_attn_weights(1, want_qg=True, only_qg=True)
                cf = self.pool(lambda e: e.memset(T["stats"][0:1, 0:1], 0.0),
                               waits=(nc2, pc2), inc=True, tag="hoistfence")
                self._kv_wtok, self._kv_ctok, self._kv_hg = nw, cf, 1
                self._pref_wtok = (nw, pw)
                self._pref_ctok = cf
                self._pref_hg = 1
            self.chunks_tile(phase, hg, ti, full)
        if phase[0] == "R":
            for hl in range(HPG):
                h = hg * HPG + hl
                ping = self.s_ping[h]
                self.tok_srep[h] = self.dve(lambda e, h=h, ping=ping: cp(e, 
                    T["srep_bf"][:, (h % 4) * 128:(h % 4) * 128 + 128],
                    T["s_bf"][:, (h * 2 + ping) * 128:(h * 2 + ping) * 128 + 128]),
                    waits=(self.tok_s[h], self.tok_s.get(h - 4) if h >= 4 else None),
                    inc=True, tag=f"srep{phase}_{h}")

    def ffn(self):
        T = self.t
        self._spill_gates = self.kb.dma_barrier("dact")
        tprev = None
        for piece in range(16):
            tl = self.ldma(T["c_rows"][0:1, 128:256], T["ds_row"][:, piece * 128:(piece + 1) * 128],
                           waits=(tprev,), tag=f"dsl{piece}")
            tb = self.pe(lambda e: e.matmul(T["pf0"][:, 0:128], T["c_ones1f"][:, :],
                                            T["c_rows"][0:1, 128:256], start=True, stop=True),
                         waits=(tl,) + self.bank_wait("pf0"), inc=True, tag=f"dsb{piece}")
            self.bank_clear("pf0")
            tprev = self.dve(lambda e, piece=piece: cp(e, 
                T["ds_bc"][:, piece * 128:(piece + 1) * 128], T["pf0"][:, 0:128]),
                waits=(tb,), inc=True, tag=f"dsc{piece}")
            self.bank_use("pf0", tprev)
        tok_dsbc = tprev
        t_gs = self.ldma(T["gs_sb"][:, :], T["gs"][:, :], tag="gs")
        t_us = self.ldma(T["us_sb"][:, :], T["us"][:, :], tag="us")

        for sp in range(2):
            self.ffn_subpass(sp, tok_dsbc, t_gs, t_us)
        fin = self.kb.dma_barrier("dact") + self.kb.dma_barrier("dsp") + self.kb.dma_barrier("dpool")
        self.kb.op("act", lambda e: e.nop(), waits=fin, tag="final_wait")

    def ffn_subpass(self, sp, tok_dsbc, t_gs, t_us):
        T = self.t
        base = sp * 1024
        tx0 = None
        tx1 = None
        for kt in range(KT):
            tx0 = self.ldma(T["xnT_sb"][:, kt * TILE:(kt + 1) * TILE],
                            T["xnTo_sp"][kt * 128:(kt + 1) * 128, base:base + 512],
                            waits=tuple(self.guard.get("xnT") or ()) + (self._spill_gates or ()),
                            tag=f"fxnT0_{sp}_{kt}")
            tx1 = self.ldma(T["big2"][:, kt * TILE:(kt + 1) * TILE],
                            T["xnTo_sp"][kt * 128:(kt + 1) * 128, base + 512:base + 1024],
                            waits=(self.guard.get("big2"), *(self._spill_gates or (None,))), tag=f"fxnT1_{sp}_{kt}")
        txnT = (tx0, tx1)
        t_h = {}
        for it in range(NIT):
            par = it % 2
            tg = self.ldma(T["fstage"][:, 0:2048], T["gw_b"][it * 128:(it + 1) * 128, :],
                           waits=(self.guard.get("fstage"), self.guard.get("dstage0"),
                                  self.guard.get("dstage1")), tag=f"fg{sp}_{it}")
            tcg = self.pool(lambda e, par=par: cp(e, 
                T["fbf"][:, par * 4096:par * 4096 + 2048], T["fstage"][:, 0:2048]),
                waits=(tg, self.guard.get(f"fbf{par}"), self.guard.get("dbf0"),
                       self.guard.get("dbf1")), inc=True, tag=f"fcg{sp}_{it}")
            tu = self.ldma(T["fstage"][:, 0:2048], T["uw_b"][it * 128:(it + 1) * 128, :],
                           waits=(tcg,), tag=f"fu{sp}_{it}")
            tcu = self.pool(lambda e, par=par: cp(e, 
                T["fbf"][:, par * 4096 + 2048:par * 4096 + 4096], T["fstage"][:, 0:2048]),
                waits=(tu, self.guard.get(f"fbf{par}")), inc=True, tag=f"fcu{sp}_{it}")
            self.guard["fstage"] = tcu
            for half in range(2):
                xsrc = T["xnT_sb"] if half == 0 else T["big2"]
                bg = f"pf{half * 2}"
                bu = f"pf{half * 2 + 1}"
                tmg = None
                tmu = None
                for kt in range(KT):
                    tmg = self.pe(lambda e, kt=kt, par=par, xsrc=xsrc, bg=bg: e.matmul(
                        T[bg][:, 0:512],
                        T["fbf"][:, par * 4096 + kt * 128:par * 4096 + (kt + 1) * 128],
                        xsrc[:, kt * TILE:(kt + 1) * TILE],
                        start=(kt == 0), stop=(kt == KT - 1)),
                        waits=(tcg, txnT[half]) + self.bank_wait(bg),
                        inc=(kt == KT - 1), tag=f"fgm{sp}_{it}_{half}_{kt}")
                self.bank_clear(bg)
                for kt in range(KT):
                    tmu = self.pe(lambda e, kt=kt, par=par, xsrc=xsrc, bu=bu: e.matmul(
                        T[bu][:, 0:512],
                        T["fbf"][:, par * 4096 + 2048 + kt * 128:par * 4096 + 2048 + (kt + 1) * 128],
                        xsrc[:, kt * TILE:(kt + 1) * TILE],
                        start=(kt == 0), stop=(kt == KT - 1)),
                        waits=(tcu, txnT[half]) + self.bank_wait(bu),
                        inc=(kt == KT - 1), tag=f"fum{sp}_{it}_{half}_{kt}")
                self.bank_clear(bu)
                hpar = half
                tsg = self.act(lambda e, bg=bg, it=it, hpar=hpar: e.activation(
                    T["hscr"][:, hpar * 512:(hpar + 1) * 512], T[bg][:, 0:512], AF.Silu,
                    scale=T["gs_sb"][:, it:it + 1]),
                    waits=(tmg, t_gs, self.guard.get(f"hscr{hpar}")), inc=True, tag=f"fsilu{sp}_{it}_{half}")
                self.bank_use(bg, tsg)
                th = self.dve(lambda e, bu=bu, it=it, half=half, hpar=hpar: e.scalar_tensor_tensor(
                    T["h_sb"][:, it * 1024 + half * 512:it * 1024 + (half + 1) * 512],
                    T[bu][:, 0:512], T["us_sb"][:, it:it + 1],
                    T["hscr"][:, hpar * 512:(hpar + 1) * 512],
                    op0=ALU.mult, op1=ALU.mult),
                    waits=(tmu, tsg, t_us, self.guard.get("h_sb")), inc=True, tag=f"fh{sp}_{it}_{half}")
                self.guard[f"hscr{hpar}"] = th
                t_h[(it, half)] = th
            self.guard[f"fbf{par}"] = t_h[(it, 1)]
        t_ong = {}
        for g in range(8):
            rows = base + g * 128
            t_og = self.ldma(T["ogrp"][:, :], T["o_sp"][rows:rows + 128, :],
                             waits=(self.guard.get("ogrp"), *(self._spill_gates or (None,))), tag=f"fog{sp}_{g}")
            t_gg = self.ldma(T["ggrp"][:, :], T["gate_sp"][rows:rows + 128, :],
                             waits=(self.guard.get("ggrp"), *(self._spill_gates or (None,))), tag=f"fgg{sp}_{g}")
            t_sq = self.act(lambda e: e.activation(T["dump"][:, 0:1024], T["ogrp"][:, :], AF.Square),
                            waits=(t_og, self.guard.get("fdump")), inc=True, tag=f"fsq{sp}_{g}")
            t_red = self.dve(lambda e: e.tensor_reduce(
                T["rmsp"][:, 0:8], T["dump"][:, 0:1024].rearrange("p (h f) -> p h f", h=8),
                axis=mybir.AxisListType.X, op=ALU.add),
                waits=(t_sq, self.guard.get("rmsp")), inc=True, tag=f"fred{sp}_{g}")
            self.guard["fdump"] = t_red
            t_ms = self.dve(lambda e: e.tensor_scalar(T["rmsp"][:, 8:16], T["rmsp"][:, 0:8],
                                                      1.0 / 128.0, None, op0=ALU.mult),
                            waits=(t_red,), inc=True, tag=f"fms{sp}_{g}")
            t_ln = self.act(lambda e: e.activation(T["rmsp"][:, 16:24], T["rmsp"][:, 8:16], AF.Ln,
                                                   bias=T["c_epsc"][:, 0:1]),
                            waits=(t_ms,), inc=True, tag=f"fln{sp}_{g}")
            t_rs = self.act(lambda e: e.activation(T["rmsp"][:, 8:16], T["rmsp"][:, 16:24], AF.Exp,
                                                   scale=-0.5),
                            waits=(t_ln,), inc=True, tag=f"frs{sp}_{g}")
            self.guard["rmsp"] = t_rs
            tong_last = None
            for hl8 in range(8):
                sl = slice(hl8 * 128, (hl8 + 1) * 128)
                t_a1 = self.dve(lambda e, sl=sl, hl8=hl8: e.scalar_tensor_tensor(
                    T["ong"][:, sl], T["ogrp"][:, sl], T["rmsp"][:, 8 + hl8:9 + hl8],
                    T["c_normw_bc"][:, :], op0=ALU.mult, op1=ALU.mult),
                    waits=(t_rs, self.guard.get("ong"), self.tok_prologue[2]), inc=False, tag=f"fo1{sp}_{g}_{hl8}")
                tong_last = self.dve(lambda e, sl=sl: e.tensor_tensor(
                    T["ong"][:, sl], T["ong"][:, sl], T["ggrp"][:, sl], op=ALU.mult),
                    waits=(t_gg,), inc=(hl8 == 7), tag=f"fo2{sp}_{g}_{hl8}")
            self.guard["ogrp"] = tong_last
            self.guard["ggrp"] = tong_last
            tev = None
            for hl8 in range(8):
                tt = self.pe(lambda e, hl8=hl8: e.transpose(
                    T["pf4"][:, 0:128], T["ong"][:, hl8 * 128:(hl8 + 1) * 128], T["c_id128f"][:, :]),
                    waits=(tong_last,) + self.bank_wait("pf4"), inc=True, tag=f"fot{sp}_{g}_{hl8}")
                self.bank_clear("pf4")
                tev = self.dve(lambda e, g=g, hl8=hl8: cp(e, 
                    T["big2"][:, g * 1024 + hl8 * 128:g * 1024 + (hl8 + 1) * 128], T["pf4"][:, 0:128]),
                    waits=(tt, self.guard.get("big2f3"), t_h[(NIT - 1, 1)]), inc=True, tag=f"foe{sp}_{g}_{hl8}")
                self.bank_use("pf4", tev)
            self.guard["ong"] = tev
            t_ong[g] = tev
        for hidN in range(4):
            t_wo = {}
            for odt in range(8):
                row = (odt * 4 + hidN) * 128
                tl = self.ldma(T["wostage"][:, :], T["Wo_b"][row:row + 128, :],
                               waits=(self.guard.get("wostage"),), tag=f"fwol{sp}_{hidN}_{odt}")
                tc = self.pool(lambda e, odt=odt: cp(e, 
                    T["wo_sb"][:, odt * 512:(odt + 1) * 512], T["wostage"][:, :]),
                    waits=(tl, self.guard.get(f"wo{odt}")), inc=True, tag=f"fwoc{sp}_{hidN}_{odt}")
                self.guard["wostage"] = tc
                t_wo[odt] = tc
            t_mm = {ts: None for ts in range(8)}
            started = {ts: False for ts in range(8)}
            for it in range(NIT):
                dpar = it % 2
                row = (hidN * NIT + it) * 128
                dstg = T["fstage"][:, dpar * 1024:dpar * 1024 + 512]
                dbfs = T["fbf"][:, dpar * 512:(dpar + 1) * 512]
                tdl = self.ldma(dstg, T["dw_b"][row:row + 128, :],
                                waits=(self.guard.get(f"dstage{dpar}"),), tag=f"fdl{sp}_{hidN}_{it}")
                tdc = self.dve(lambda e, hidN=hidN, dstg=dstg, dbfs=dbfs: e.tensor_tensor(
                    dbfs, dstg, T["ds_bc"][:, hidN * 512:(hidN + 1) * 512], op=ALU.mult),
                    waits=(tdl, tok_dsbc, self.guard.get(f"dbf{dpar}")), inc=True, tag=f"fdc{sp}_{hidN}_{it}")
                self.guard[f"dstage{dpar}"] = tdc
                last_ts = None
                for ts in range(8):
                    bk = f"pf{ts}"
                    wts = (tdc, t_h[(it, ts // 4)])
                    if not started[ts]:
                        wts = wts + self.bank_wait(bk)
                    last_ts = self.pe(lambda e, it=it, ts=ts, bk=bk, dbfs=dbfs, st=(not started[ts]): e.matmul(
                        T[bk][:, 0:512],
                        T["h_sb"][:, it * 1024 + ts * 128:it * 1024 + (ts + 1) * 128],
                        dbfs, start=st, stop=False),
                        waits=wts, inc=(ts == 7), tag=f"fdm{sp}_{hidN}_{it}_{ts}")
                    if not started[ts]:
                        self.bank_clear(bk)
                        started[ts] = True
                self.guard[f"dbf{dpar}"] = last_ts
                self._last_h_reader = last_ts
            t_wom = {}
            for odt in range(8):
                for ts in range(8):
                    bk = f"pf{ts}"
                    t_wom[ts] = self.pe(lambda e, odt=odt, ts=ts, bk=bk: e.matmul(
                        T[bk][:, 0:512],
                        T["big2"][:, ts * 1024 + odt * 128:ts * 1024 + (odt + 1) * 128],
                        T["wo_sb"][:, odt * 512:(odt + 1) * 512],
                        start=False, stop=(odt == 7)),
                        waits=(t_wo[odt], t_ong[ts]), inc=(odt == 7), tag=f"fwm{sp}_{hidN}_{odt}_{ts}")
            for ts in range(8):
                rows = base + ts * 128
                opar = ts % 2
                txl = self.ldma(T["x_sb"][:, opar * 512:(opar + 1) * 512],
                                T["x_own"][rows:rows + 128, hidN * 512:(hidN + 1) * 512],
                                waits=(self.guard.get(f"fx{opar}"),), tag=f"fxl{sp}_{hidN}_{ts}")
                tev = self.dve(lambda e, ts=ts, opar=opar: e.tensor_tensor(
                    T["ostage"][:, opar * 512:(opar + 1) * 512], T[f"pf{ts}"][:, 0:512],
                    T["x_sb"][:, opar * 512:(opar + 1) * 512], op=ALU.add),
                    waits=(t_wom[ts], txl, self.guard.get(f"ost{opar}")), inc=True, tag=f"fev{sp}_{hidN}_{ts}")
                self.bank_use(f"pf{ts}", tev)
                started[ts] = False
                tout = self.sdma(T["out"][rows:rows + 128, hidN * 512:(hidN + 1) * 512],
                                 T["ostage"][:, opar * 512:(opar + 1) * 512],
                                 waits=(tev,), tag=f"fout{sp}_{hidN}_{ts}")
                self.guard[f"ost{opar}"] = tout
                self.guard[f"fx{opar}"] = tev
        self.guard["big2f3"] = t_wom[7] if t_wom else None
        self.guard["big2"] = self.guard.get("big2f3")
        self.guard["xnT"] = (t_h[(NIT - 1, 0)],)
        self.guard["h_sb"] = self._last_h_reader

    def build_all(self):
        from contextlib import ExitStack
        nc = self.nc
        self.dram_inputs()
        self.s_ping = {h: 0 for h in range(NH)}
        self.tok_s = {}
        self.tok_srep = {}
        self.tok_conv_prev = {}
        self.tok_otile = {}
        self._spill_gates = ()
        with ExitStack() as outer:
            self.alloc_outer(outer)
            with ExitStack() as actx:
                self.alloc_attn(actx)
                self.prologue()
                self.prologue2()
                if "S" in self.debug_names:
                    self.dve(lambda e: e.memset(self.t["s_bf"][:, :], 0.0), inc=True, tag="dbgS0")
                for phase in self.do_phases:
                    self.attn_phase(phase)
                if "S" in self.debug_names:
                    ts = self.sdma(self.t["dbg_S"][:, :], self.t["s_bf"][:, :],
                                   waits=tuple(self.tok_s.get(h) for h in range(NH)) +
                                         tuple(self.tok_srep.get(h) for h in range(NH)),
                                   tag="dbgS")
                if not self.do_ffn:
                    fin = self.kb.dma_barrier("dact") + self.kb.dma_barrier("dsp")
                    self.kb.op("act", lambda e: e.nop(), waits=fin, tag="final_wait_dbg")
                self.kb.flush("attn")
            if self.do_ffn:
                with ExitStack() as fctx:
                    self.alloc_ffn(fctx)
                    self.ffn()
                    self.kb.flush("ffn")
        return nc


def build_kernel(debug_names=(), ntile=NTILE, do_ffn=True, do_phases=("Ra", "Fa", "Rb", "Fb")):
    nc = bass.Bass("TRN2", target_bir_lowering=False)
    p = Prog(nc, debug_names=debug_names, ntile=ntile, do_ffn=do_ffn, do_phases=do_phases)
    p.build_all()
    return nc, p


def kernel(**inputs):
    nc, _ = build_kernel()
    in_maps = [stage_inputs_for_core(inputs, c) for c in range(8)]
    res = run_bass_kernel_spmd(nc, in_maps, core_ids=list(range(8)))
    out = np.empty((BATCH, LQ, HID), np.float32)
    for c in range(8):
        b, hh = c // 2, c % 2
        out[b, hh * HALF:(hh + 1) * HALF] = np.asarray(res.results[c]["out"])
    return out
